# version 19
# baseline (speedup 1.0000x reference)
import contextlib
import numpy as np
import concourse.bass as bass
import concourse.mybir as mybir

F32 = mybir.dt.float32
F32R = mybir.dt.float32r
BF16 = mybir.dt.bfloat16
AF = mybir.ActivationFunctionType
ALU = mybir.AluOpType
AXX = mybir.AxisListType.X

D = 2048
M = 2048
E = 8
T = 1024
NCH = 16
MBLK = 256
NBLK = M // MBLK
EPS = 1e-5
RG = [[0, 1, 2, 3, 4, 5, 6, 7]]


class Em:
    def __init__(self, nc, stack):
        self.nc = nc
        self.stack = stack
        self.q = {k: [] for k in ("t", "v", "s", "g", "y")}
        self.sems = {}
        self.cnt = {}

    def _sem(self, name):
        if name not in self.sems:
            self.sems[name] = self.stack.enter_context(self.nc.semaphore(name))
            self.cnt[name] = 0
        return self.sems[name]

    def emit(self, eng, fn, inc=None, inc_n=1):
        sem_h = None
        if inc is not None:
            sem_h = self._sem(inc)
            self.cnt[inc] += inc_n
        def run(engine, fn=fn, sem_h=sem_h, inc_n=inc_n):
            ins = fn(engine)
            if sem_h is not None:
                ins.then_inc(sem_h, inc_n)
            return ins
        self.q[eng].append(run)

    def emit_cc(self, fn, inc):
        sem_h = self._sem(inc)
        self.cnt[inc] += 1
        def run(engine, fn=fn, sem_h=sem_h):
            return fn(engine).then_inc(sem_h)
        self.q["g"].append(run)

    def wait(self, eng, sem, value=None):
        v = self.cnt[sem] if value is None else value
        h = self._sem(sem)
        self.q[eng].append(lambda engine, h=h, v=v: engine.wait_ge(h, v))

    def val(self, sem):
        self._sem(sem)
        return self.cnt[sem]

    def gemit(self, eng, fn):
        self.emit(eng, fn, inc="chain_" + eng)
        self.wait(eng, "chain_" + eng)

    def dma(self, eng, out, in_, sem):
        self.emit(eng, lambda q, o=out, i=in_: q.dma_start(out=o, in_=i),
                  inc=sem, inc_n=16)


def build(upto="full", dbg=None):
    nc = bass.Bass("TRN2", target_bir_lowering=False)
    stack = contextlib.ExitStack()

    di = {}
    di["x"] = nc.dram_tensor("x", [D, T], F32, kind="ExternalInput").ap()
    for w in ("wkT", "wvT", "wrT", "woT"):
        di[w] = nc.dram_tensor(w, [D, D], F32, kind="ExternalInput").ap()
    di["wi0"] = nc.dram_tensor("wi0", [E, D, M], F32, kind="ExternalInput").ap()
    di["wi1"] = nc.dram_tensor("wi1", [E, D, M], F32, kind="ExternalInput").ap()
    di["wo2"] = nc.dram_tensor("wo2", [E, M, D], F32, kind="ExternalInput").ap()
    di["gk"] = nc.dram_tensor("gk", [D, E], F32, kind="ExternalInput").ap()
    for w in ("td", "tf", "tmk", "tmv", "tmr"):
        di[w] = nc.dram_tensor(w, [128, NCH], F32, kind="ExternalInput").ap()
    di["e8"] = nc.dram_tensor("e8", [E, E * 128], F32, kind="ExternalInput").ap()
    di["ident"] = nc.dram_tensor("ident", [128, 128], F32, kind="ExternalInput").ap()
    di["sel"] = nc.dram_tensor("sel", [128, 8], F32, kind="ExternalInput").ap()
    di["hmask"] = nc.dram_tensor("hmask", [128, 1], F32, kind="ExternalInput").ap()

    dout = nc.dram_tensor("out", [D, T], F32, kind="ExternalOutput").ap()
    ddbg = nc.dram_tensor("dbg", [D, T], F32, kind="ExternalOutput").ap() \
        if dbg is not None else None

    spill = nc.dram_tensor("spill", [D, T], F32, kind="Internal").ap()
    cc_h_in = nc.dram_tensor("cc_h_in", [128, NCH], F32, kind="Internal").ap()
    cc_h_out = nc.dram_tensor("cc_h_out", [8 * 128, NCH], F32, kind="Internal").ap()
    cc_s_in = [nc.dram_tensor(f"cc_s_in{j}", [128, 2], F32, kind="Internal").ap()
               for j in range(NCH)]
    cc_s_out = [nc.dram_tensor(f"cc_s_out{j}", [8 * 128, 2], F32, kind="Internal").ap()
                for j in range(NCH)]

    em = Em(nc, stack)
    sb = lambda name, shape, dtype: stack.enter_context(
        nc.sbuf_tensor(name, shape, dtype)).ap()
    ps = lambda name, shape: stack.enter_context(
        nc.psum_tensor(name, shape, F32)).ap()

    x_res = sb("x_res", [128, NCH * T], F32)
    ar2 = sb("ar2", [128, NCH * T // 2], F32)
    ar3 = sb("ar3", [128, NCH * T // 2], F32)
    ar4 = sb("ar4", [128, NCH * T // 2], F32)
    ar5 = sb("ar5", [128, NCH * T // 4], F32)
    scr = sb("scr", [128, 5 * T], F32)
    tmp = sb("tmpt", [128, T], F32)

    xn = ar2.bitcast(BF16)
    xrT = ar2.bitcast(BF16)
    g_ = ar3.bitcast(BF16)
    y_ = ar3.bitcast(BF16)
    wib = ar4.bitcast(BF16)
    xkT = ar4.bitcast(BF16)
    xvT = x_res.bitcast(BF16)
    wob = ar5.bitcast(BF16)

    vecs = {w: sb("v_" + w, [128, NCH], F32) for w in ("td", "tf", "tmk", "tmv", "tmr")}
    onev = sb("onev", [128, 1], F32)
    onevb = sb("onevb", [128, 1], BF16)
    ones1 = sb("ones1", [1, 128], F32)
    e8t = sb("e8t", [E, E * 128], BF16)
    ident = sb("identt", [128, 128], F32)
    selt = sb("selt", [128, 8], F32)
    hmaskt = sb("hmaskt", [128, 1], F32)
    gkf = sb("gkf", [128, NCH * E], F32)
    gksb = sb("gksb", [128, E], F32)
    invcol = sb("invcol", [128, 8], F32)
    nm2col = sb("nm2col", [128, 8], F32)
    cwT = sb("cwT", [E, T], BF16)
    invb = scr[:, 0:T]
    nm2b = scr[:, T:2 * T]
    cwb = scr[:, 2 * T:3 * T]
    mean = scr[0:1, 3 * T:4 * T]
    ex2 = scr[0:1, 4 * T:5 * T]
    var = ex2
    inv = ex2
    nm2 = mean
    halo_o = sb("halo_o", [128, NCH], F32)
    halo_i = sb("halo_i", [128, NCH], F32)
    z0 = sb("z0", [128, NCH], F32)
    eu = sb("eu", [128, NCH], F32)
    omk = sb("omk", [128, NCH], F32)
    omv = sb("omv", [128, NCH], F32)
    omr = sb("omr", [128, NCH], F32)
    dec = sb("dec", [128, NCH], F32)
    d1024 = sb("d1024", [128, NCH], F32)
    w_ = sb("w_", [128, NCH], F32)
    dsc = sb("dscr", [128, 96], F32)
    sumt = sb("sumt", [128, 8], F32)

    PA = ps("PA", [128, T])
    PB = ps("PB", [128, T])
    PC = ps("PC", [128, T])
    PM = ps("PM", [128, T])
    PST = [PA, PB, PC]
    SL = lambda s: PST[s // 2][:, (s % 2) * 512:(s % 2 + 1) * 512]
    slu = [0] * 6

    nst = {"pm_w": 0, "pm_r": 0}

    def pm_write_group(mm_emits):
        em.wait("t", "evpm", nst["pm_r"])
        for i, f in enumerate(mm_emits):
            em.emit("t", f, inc=("mmpm" if i == len(mm_emits) - 1 else None))
        nst["pm_w"] += 1

    def pm_read_chain(eng, fns):
        em.wait(eng, "mmpm", nst["pm_w"])
        for i, f in enumerate(fns):
            em.emit(eng, f, inc=("evpm" if i == len(fns) - 1 else None))
        nst["pm_r"] += 1
        return nst["pm_r"]

    for name in ("td", "tf", "tmk", "tmv", "tmr"):
        em.dma("y", vecs[name][:, :], di[name][:, :], "sx")
    em.dma("g", e8t[:, :], di["e8"][:, :], "wge")
    em.dma("y", ident[:, :], di["ident"][:, :], "sx")
    em.dma("y", selt[:, :], di["sel"][:, :], "sx")
    em.dma("y", hmaskt[:, :], di["hmask"][:, :], "sx")
    em.dma("y", gkf.rearrange("p (j e) -> p j e", j=NCH),
           di["gk"].rearrange("(j p) e -> p j e", p=128), "wg")
    em.dma("y", x_res.rearrange("p (j t) -> p j t", j=NCH),
           di["x"].rearrange("(j p) t -> p j t", p=128), "sx")
    for e_ in ("v", "s", "t"):
        em.wait(e_, "sx")
    em.wait("t", "wg")
    em.wait("t", "wge")
    em.wait("v", "wg")
    em.emit("v", lambda v: v.memset(onev[:, :], 1.0))
    em.emit("v", lambda v: v.memset(onevb[:, :], 1.0))
    em.emit("v", lambda v: v.memset(ones1[:, :], 1.0), inc="vc")
    em.wait("t", "vc")
    em.emit("s", lambda s: s.activation(w_[:, :], vecs["td"][:, :], AF.Exp), inc="sc")
    em.wait("v", "sc")
    em.emit("v", lambda v: v.tensor_scalar(w_[:, :], w_[:, :], -1.0, None, ALU.mult),
            inc="vc")
    em.wait("s", "vc")
    em.emit("s", lambda s: s.activation(dec[:, :], w_[:, :], AF.Exp))
    em.emit("s", lambda s: s.activation(d1024[:, :], w_[:, :], AF.Exp, scale=1024.0))
    em.emit("s", lambda s: s.activation(eu[:, :], vecs["tf"][:, :], AF.Exp), inc="sc")
    for nm, om in (("tmk", omk), ("tmv", omv), ("tmr", omr)):
        em.emit("v", lambda v, nm=nm, om=om: v.tensor_scalar(
            om[:, :], vecs[nm][:, :], -1.0, 1.0, ALU.mult, ALU.add), inc="vc")

    mms = [lambda t, j=j: t.matmul(PM[0:1, 0:E], onev[:, :],
                                   gkf[:, j * E:(j + 1) * E],
                                   start=(j == 0), stop=(j == NCH - 1))
           for j in range(NCH)]
    pm_write_group(mms)
    pm_read_chain("s", [lambda s: s.activation(tmp[0:1, 0:E], PM[0:1, 0:E],
                                               AF.Copy)])
    em.wait("t", "evpm", nst["pm_r"])
    pm_write_group([lambda t: t.matmul(PM[:, 0:E], ones1[:, :], tmp[0:1, 0:E],
                                       start=True, stop=True)])
    pm_read_chain("s", [lambda s: s.activation(gksb[:, :], PM[:, 0:E], AF.Copy)])

    def ln_stats(src_ap, cols, lnw=None, lnsem="evoa"):
        em.wait("t", "evpm", nst["pm_r"])
        tbm = tmp.bitcast(BF16)
        nsq = 0
        for half in range(2):
            for j in range(NCH):
                sl = slice((nsq % 4) * 512, (nsq % 4) * 512 + 512)
                if nsq >= 4:
                    em.wait("s", "mmsq", em.val("mmsq") - 3)
                if lnw is not None and half == 0:
                    em.wait("s", lnsem, lnw(j))
                em.emit("s", lambda s, j=j, half=half, sl=sl: s.activation(
                    tbm[:, sl], src_ap[:, cols(j)][:, half * 512:(half + 1) * 512],
                    AF.Copy), inc="sq")
                em.wait("t", "sq")
                em.emit("t", lambda t, j=j, half=half, sl=sl: t.matmul(
                    PM[0:1, half * 512:(half + 1) * 512],
                    onevb[:, :], tbm[:, sl],
                    start=(j == 0), stop=(j == NCH - 1)),
                    inc="mmsq")
                if half == 1 and j == NCH - 1:
                    em.emit("t", lambda t: t.matmul(
                        PC[0:1, 256:512], onevb[:, :],
                        tbm[:, 0:256], start=True, stop=True), inc="mmpm")
                nsq += 1
        nst["pm_w"] += 1
        pm_read_chain("s", [
            lambda s: s.activation(mean[0:1, 0:512], PM[0:1, 0:512],
                                   AF.Copy, scale=1.0 / D),
            lambda s: s.activation(mean[0:1, 512:1024], PM[0:1, 512:1024],
                                   AF.Copy, scale=1.0 / D)])
        em.wait("t", "evpm", nst["pm_r"])
        tb = tmp.bitcast(BF16)
        nsq = 0
        for half in range(2):
            for j in range(NCH):
                sl = slice((nsq % 4) * 512, (nsq % 4) * 512 + 512)
                if nsq >= 4:
                    em.wait("s", "mmsq", em.val("mmsq") - 3)
                if lnw is not None and half == 0:
                    em.wait("s", lnsem, lnw(j))
                em.emit("s", lambda s, j=j, half=half, sl=sl: s.activation(
                    tb[:, sl], src_ap[:, cols(j)][:, half * 512:(half + 1) * 512],
                    AF.Square), inc="sq")
                em.wait("t", "sq")
                fin = (half == 1 and j == NCH - 1)
                em.emit("t", lambda t, j=j, half=half, sl=sl: t.matmul(
                    PM[0:1, half * 512:(half + 1) * 512],
                    onevb[:, :], tb[:, sl],
                    start=(j == 0), stop=(j == NCH - 1)), inc="mmsq")
                if fin:
                    em.emit("t", lambda t: t.matmul(
                        PC[0:1, 0:256], onevb[:, :],
                        tb[:, 0:256], start=True, stop=True),
                        inc="mmpm")
                nsq += 1
        nst["pm_w"] += 1
        pm_read_chain("v", [
            lambda v: v.tensor_scalar(ex2[0:1, 0:512], PM[0:1, 0:512],
                                      1.0 / D, None, ALU.mult),
            lambda v: v.tensor_scalar(ex2[0:1, 512:1024], PM[0:1, 512:1024],
                                      1.0 / D, None, ALU.mult)])
        em.gemit("v", lambda v: v.tensor_tensor(tmp[0:1, 0:T], mean, mean, ALU.mult))
        em.gemit("v", lambda v: v.tensor_tensor(var, ex2, tmp[0:1, 0:T], ALU.subtract))
        em.emit("v", lambda v: v.tensor_scalar(var, var, EPS, None, ALU.add), inc="vc")
        em.wait("s", "vc")
        em.gemit("s", lambda s: s.activation(tmp[0:1, 0:T], var, AF.Ln))
        em.emit("s", lambda s: s.activation(inv, tmp[0:1, 0:T], AF.Exp,
                                            scale=-0.5), inc="sc")
        em.wait("v", "sc")
        em.gemit("v", lambda v: v.tensor_tensor(nm2, mean, inv, ALU.mult))
        em.emit("v", lambda v: v.tensor_scalar(nm2, nm2, -1.0, None, ALU.mult),
                inc="vc")
        em.wait("t", "vc")
        for vsrc, vdst in ((inv, invb), (nm2, nm2b)):
            mms = []
            for half in range(2):
                mms.append(lambda t, half=half, vsrc=vsrc: t.matmul(
                    PM[:, half * 512:(half + 1) * 512],
                    ones1[:, :],
                    vsrc[0:1, half * 512:(half + 1) * 512],
                    start=True, stop=True))
            pm_write_group(mms)
            pm_read_chain("s", [
                lambda s, vdst=vdst: s.activation(vdst[:, 0:512], PM[:, 0:512], AF.Copy),
                lambda s, vdst=vdst: s.activation(vdst[:, 512:1024], PM[:, 512:1024],
                                                  AF.Copy)])
        em.wait("v", "evpm", nst["pm_r"])

    def ln_apply(dst_cols, src_ap, cols, save_cols=False):
        for j in range(NCH):
            em.emit("v", lambda v, j=j: v.tensor_tensor(
                tmp[:, :], src_ap[:, cols(j)], invb[:, :], ALU.mult))
            em.emit("v", lambda v, j=j: v.tensor_tensor(
                dst_cols(j), tmp[:, :], nm2b[:, :], ALU.add), inc="lnap")
            if save_cols:
                em.emit("v", lambda v, j=j: v.tensor_tensor(
                    z0[:, j:j + 1], tmp[:, 0:1], nm2b[:, 0:1], ALU.add))
                em.emit("v", lambda v, j=j: v.tensor_tensor(
                    halo_o[:, j:j + 1], tmp[:, T - 1:T], nm2b[:, T - 1:T],
                    ALU.add), inc="lnap")

    def gating(src_ap, cols, mid_emit=None):
        X = tmp
        mms = []
        for tc in range(8):
            for j in range(NCH):
                mms.append(lambda t, tc=tc, j=j: t.matmul(
                    PM[:, tc * 8:(tc + 1) * 8],
                    src_ap[:, cols(j)][:, tc * 128:(tc + 1) * 128],
                    gkf[:, j * E:(j + 1) * E],
                    start=(j == 0), stop=(j == NCH - 1)))
        for tc in range(8):
            mms.append(lambda t, tc=tc: t.matmul(
                PM[:, 512 + tc:513 + tc], inv[0:1, tc * 128:(tc + 1) * 128],
                ones1[0:1, 0:1], start=True, stop=True))
            mms.append(lambda t, tc=tc: t.matmul(
                PM[:, 520 + tc:521 + tc], nm2[0:1, tc * 128:(tc + 1) * 128],
                ones1[0:1, 0:1], start=True, stop=True))
        pm_write_group(mms)
        if mid_emit is not None:
            mid_emit()
        em.wait("s", "mmpm", nst["pm_w"])
        em.gemit("s", lambda s: s.activation(invcol[:, 0:8], PM[:, 512:520],
                                             AF.Copy))
        em.emit("s", lambda s: s.activation(nm2col[:, 0:8], PM[:, 520:528],
                                            AF.Copy), inc="scst")
        r3 = lambda ap: ap.rearrange("p (tc e) -> p tc e", e=8)
        bc = lambda ap: ap.rearrange("p (tc one) -> p tc one", one=1).broadcast_to(
            [128, 8, 8])
        l_all = X[:, 0:64]; t2 = X[:, 64:128]; m1 = X[:, 128:136]
        m2 = X[:, 136:144]; w1 = X[:, 144:152]; w2 = X[:, 152:160]
        mk1 = X[:, 160:224]; l2 = X[:, 224:288]; mk2 = X[:, 288:352]
        u1 = X[:, 352:416]; cw_all = X[:, 416:480]
        gksb3 = gksb.rearrange("p (one e) -> p one e", one=1).broadcast_to(
            [128, 8, 8])
        em.wait("v", "mmpm", nst["pm_w"])
        em.wait("v", "scst")
        em.gemit("v", lambda v: v.tensor_tensor(r3(t2), gksb3, bc(nm2col[:, 0:8]),
                                                ALU.mult))
        em.gemit("v", lambda v: v.tensor_tensor(r3(l_all), r3(PM[:, 0:64]),
                                                bc(invcol[:, 0:8]), ALU.mult))
        em.emit("v", lambda v: v.tensor_tensor(l_all, l_all, t2, ALU.add),
                inc="evpm")
        em.gemit("v", lambda v: v.engine_nop())
        nst["pm_r"] += 1
        em.gemit("v", lambda v: v.tensor_reduce(
            m1.rearrange("p (tc one) -> p tc one", one=1), r3(l_all),
            axis=AXX, op=ALU.max))
        em.gemit("v", lambda v: v.tensor_tensor(r3(mk1), r3(l_all), bc(m1),
                                                ALU.is_equal))
        em.gemit("v", lambda v: v.scalar_tensor_tensor(l2, mk1, -1e30, l_all,
                                                       ALU.mult, ALU.add))
        em.gemit("v", lambda v: v.tensor_reduce(
            m2.rearrange("p (tc one) -> p tc one", one=1), r3(l2),
            axis=AXX, op=ALU.max))
        em.gemit("v", lambda v: v.tensor_tensor(r3(mk2), r3(l2), bc(m2),
                                                ALU.is_equal))
        em.emit("v", lambda v: v.tensor_tensor(w1, m1, m2, ALU.subtract),
                inc="vc")
        em.wait("s", "vc")
        em.emit("s", lambda s: s.activation(w1, w1, AF.Sigmoid), inc="scst")
        em.wait("v", "scst")
        em.gemit("v", lambda v: v.tensor_scalar(w2, w1, -1.0, 1.0, ALU.mult,
                                                ALU.add))
        em.gemit("v", lambda v: v.tensor_tensor(r3(u1), r3(mk1), bc(w1),
                                                ALU.mult))
        em.gemit("v", lambda v: v.tensor_tensor(r3(cw_all), r3(mk2), bc(w2),
                                                ALU.mult))
        em.emit("v", lambda v: v.tensor_tensor(cw_all, cw_all, u1, ALU.add),
                inc="vc")
        em.wait("t", "vc")
        for grp in range(2):
            mms = []
            for q in range(4):
                tc = grp * 4 + q
                mms.append(lambda t, tc=tc, q=q, grp=grp: t.transpose(
                    PM[0:8, grp * 512 + q * 128:grp * 512 + (q + 1) * 128],
                    cw_all[:, tc * 8:(tc + 1) * 8], ident[:, :]))
            pm_write_group(mms)
            pm_read_chain("s", [lambda s, grp=grp: s.activation(
                cwT[:, grp * 512:(grp + 1) * 512],
                PM[0:8, grp * 512:(grp + 1) * 512], AF.Copy)])

    def cw_broadcast(e, scalar_wait):
        mms = []
        for half in range(2):
            mms.append(lambda t, e=e, half=half: t.matmul(
                PM[:, half * 512:(half + 1) * 512],
                e8t[:, e * 128:(e + 1) * 128],
                cwT[:, half * 512:(half + 1) * 512],
                start=True, stop=True))
        pm_write_group(mms)
        if scalar_wait is not None:
            em.wait("s", scalar_wait[0], scalar_wait[1])
        return pm_read_chain("s", [
            lambda s: s.activation(cwb[:, 0:512], PM[:, 0:512], AF.Copy),
            lambda s: s.activation(cwb[:, 512:1024], PM[:, 512:1024], AF.Copy)])

    def ffn_pass(xn_cols, tag):
        wsem, wosem = "wf" + tag, "wq" + tag
        mmh, mmo = "mmh" + tag, "mmo" + tag
        evg, evo = "evg" + tag, "evo" + tag
        wi_cols = lambda b, m_: slice((b * 2 + m_) * (NCH * MBLK),
                                      (b * 2 + m_ + 1) * (NCH * MBLK))
        wo_cols = lambda b: slice(b * (NCH * MBLK), (b + 1) * (NCH * MBLK))

        def load_wi(nuse):
            e, blk = divmod(nuse, NBLK)
            b = nuse % 2
            if nuse == 0 and tag == "b":
                em.wait("g", "mmr", 16)
            if nuse >= 2:
                em.wait("g", mmh, 4 * (nuse - 1))
            for m_, w in ((0, di["wi0"]), (1, di["wi1"])):
                src_ = w[e, :, blk * MBLK:(blk + 1) * MBLK].rearrange(
                    "(j p) m -> p j m", p=128)
                dst = wib[:, wi_cols(b, m_)].rearrange("p (j m) -> p j m", j=NCH)
                em.dma("g", dst, src_, wsem)

        def load_wo(nuse):
            e, blk = divmod(nuse, NBLK)
            b = nuse % 2
            if nuse == 0 and tag == "b":
                em.wait("g", "mmwo", 16)
            if nuse >= 2:
                em.wait("g", mmo, 4 * (nuse - 1))
            src_ = di["wo2"][e, :, blk * MBLK:(blk + 1) * MBLK].rearrange(
                "(j p) m -> p j m", p=128)
            dst = wob[:, wo_cols(b)].rearrange("p (j m) -> p j m", j=NCH)
            em.dma("g", dst, src_, wosem)

        load_wi(0)
        load_wo(0)
        em.wait("t", "lnap")
        sil_gate = em.val("vc")
        n_h = 0
        n_o = 0
        n_g = 0
        for e in range(E):
            cwb_rdy = cw_broadcast(e, ((evg, 32 * e) if e > 0 else None))
            for blk in range(NBLK):
                nuse = e * NBLK + blk
                b = nuse % 2
                if nuse + 1 < E * NBLK:
                    load_wi(nuse + 1)
                em.wait("t", wsem, 16 * 2 * (nuse + 1))
                for mc in range(2):
                    mi = blk * 2 + mc
                    for half in range(2):
                        hs = slice(half * 512, (half + 1) * 512)
                        s0 = (2 * n_h) % 6
                        s1 = (2 * n_h + 1) % 6
                        em.wait("t", f"evsl{s0}", slu[s0])
                        em.wait("t", f"evsl{s1}", slu[s1])
                        slu[s0] += 1; slu[s1] += 1
                        for j in range(NCH):
                            em.emit("t", lambda t, b=b, mc=mc, j=j, hs=hs, s0=s0:
                                    t.matmul(SL(s0),
                                             wib[:, wi_cols(b, 0)][:, j * MBLK + mc * 128:
                                                                   j * MBLK + (mc + 1) * 128],
                                             xn_cols(j)[:, hs],
                                             start=(j == 0), stop=(j == NCH - 1)))
                        for j in range(NCH):
                            em.emit("t", lambda t, b=b, mc=mc, j=j, hs=hs, s1=s1:
                                    t.matmul(SL(s1),
                                             wib[:, wi_cols(b, 1)][:, j * MBLK + mc * 128:
                                                                   j * MBLK + (mc + 1) * 128],
                                             xn_cols(j)[:, hs],
                                             start=(j == 0), stop=(j == NCH - 1)),
                                    inc=(mmh if j == NCH - 1 else None))
                        n_h += 1
                        th = slice((n_h % 2) * 512, (n_h % 2) * 512 + 512)
                        em.wait("s", mmh, n_h)
                        if n_h == 1:
                            em.wait("s", "vc", sil_gate)
                        if n_g >= 2:
                            em.wait("s", evg, n_g - 1)
                        em.emit("s", lambda s, th=th, s0=s0: s.activation(
                            tmp[:, th], SL(s0), AF.Silu), inc=f"evsl{s0}")
                        em.wait("v", f"evsl{s0}", slu[s0])
                        if n_h == 1:
                            em.wait("v", "evpm", cwb_rdy)
                        em.emit("v", lambda v, th=th, s1=s1: v.tensor_tensor(
                            tmp[:, th], tmp[:, th], SL(s1), ALU.mult),
                            inc=f"evsl{s1}")
                        em.emit("v", lambda v, mi=mi, hs=hs, th=th: v.tensor_tensor(
                            g_[:, mi * T:(mi + 1) * T][:, hs], tmp[:, th],
                            cwb[:, hs], ALU.mult), inc=evg)
                        n_g += 1
            last_w = [32 * e + 2]
            def gwait(j, e=e, last_w=last_w):
                tgt = 32 * e + 2 * (j + 1)
                if tgt > last_w[0]:
                    em.wait("t", evg, tgt)
                    last_w[0] = tgt
            for blk in range(NBLK):
                nuse = e * NBLK + blk
                b = nuse % 2
                if nuse + 1 < E * NBLK:
                    load_wo(nuse + 1)
                em.wait("t", wosem, 16 * (nuse + 1))
                for dc in range(2):
                    dii = blk * 2 + dc
                    for half in range(2):
                        hs = slice(half * 512, (half + 1) * 512)
                        s0 = n_o % 6
                        em.wait("t", f"evsl{s0}", slu[s0])
                        slu[s0] += 1
                        for j in range(NCH):
                            gwait(j)
                            em.emit("t", lambda t, b=b, dc=dc, j=j, hs=hs, s0=s0:
                                    t.matmul(SL(s0),
                                             wob[:, wo_cols(b)][:, j * MBLK + dc * 128:
                                                                j * MBLK + (dc + 1) * 128],
                                             g_[:, j * T:(j + 1) * T][:, hs],
                                             start=(j == 0), stop=(j == NCH - 1)),
                                    inc=(mmo if j == NCH - 1 else None))
                        n_o += 1
                        em.wait("v", mmo, n_o)
                        em.emit("v", lambda v, dii=dii, hs=hs, s0=s0:
                                v.scalar_tensor_tensor(
                                    x_res[:, dii * T:(dii + 1) * T][:, hs], SL(s0),
                                    0.5, x_res[:, dii * T:(dii + 1) * T][:, hs],
                                    ALU.mult, ALU.add), inc=f"evsl{s0}")
                        em.emit("v", lambda v: v.engine_nop(), inc=evo)
        em.wait("y", evo, n_o)
        em.wait("g", evo, n_o)
        em.wait("v", evo, n_o)
        if tag != "a":
            em.wait("t", evo, n_o)
            em.wait("s", evo, n_o)

    def attn_pass():
        xk_cols = lambda j: xkT[:, j * T:(j + 1) * T]
        xv_cols = lambda j: xvT[:, j * T:(j + 1) * T]
        xr_cols = lambda j: xrT[:, j * T:(j + 1) * T]
        y_cols = lambda j: y_[:, j * T:(j + 1) * T]
        projk = x_res.bitcast(BF16)
        SE = [scr[:, 0:T], scr[:, 2 * T:3 * T]]
        SB = [scr[:, T:2 * T], scr[:, 3 * T:4 * T]]
        SC4 = scr[:, 4 * T:5 * T]

        em.dma("y", spill.rearrange("(j p) t -> p j t", p=128),
               x_res.rearrange("p (j t) -> p j t", j=NCH), "spl")
        lnw = lambda j: 224 + (j // 2) * 4 + (j % 2) * 2 + 2
        ln_stats(x_res, x_cols, lnw=lnw)
        x3 = x_res.rearrange("p (j t) -> p j t", j=NCH)
        ivl = invb[:, T - 1:T].broadcast_to([128, NCH])
        nml = nm2b[:, T - 1:T].broadcast_to([128, NCH])
        iv0 = invb[:, 0:1].broadcast_to([128, NCH])
        nm0 = nm2b[:, 0:1].broadcast_to([128, NCH])
        em.gemit("v", lambda v: v.tensor_tensor(
            halo_o[:, :], x3[:, :, T - 1], ivl, ALU.mult))
        em.emit("v", lambda v: v.tensor_tensor(
            halo_o[:, :], halo_o[:, :], nml, ALU.add))
        em.gemit("v", lambda v: v.tensor_tensor(
            z0[:, :], x3[:, :, 0], iv0, ALU.mult))
        em.emit("v", lambda v: v.tensor_tensor(
            z0[:, :], z0[:, :], nm0, ALU.add), inc="mixz")
        em.wait("y", "mixz")
        em.dma("y", cc_h_in[:, :], halo_o[:, :], "chi")
        em.wait("g", "chi")
        em.emit_cc(lambda g: g.collective_compute(
            "AllGather", ALU.bypass, replica_groups=RG,
            ins=[cc_h_in.opt()], outs=[cc_h_out.opt()]), "agh")
        em.wait("y", "agh", 1)
        em.dma("y", scr[:, 2 * T:2 * T + 8 * NCH].rearrange(
                   "p (r c) -> p r c", r=8),
               cc_h_out.rearrange("(r p) c -> p r c", p=128), "cho")
        em.wait("v", "spl")
        em.wait("g", "spl")
        zb = tmp.bitcast(BF16)[:, 0:T]
        db = tmp.bitcast(BF16)[:, T:2 * T]
        for j in range(NCH):
            em.emit("v", lambda v, j=j: v.tensor_tensor(
                SC4[:, :], x_res[:, x_cols(j)], invb[:, :], ALU.mult))
            em.emit("v", lambda v, j=j: v.tensor_tensor(
                zb, SC4[:, :], nm2b[:, :], ALU.add))
            em.emit("v", lambda v: v.tensor_tensor(
                db[0:128, 0:T - 1], zb[0:128, 1:T], zb[0:128, 0:T - 1],
                ALU.subtract))
            for tm, dst in (("tmk", xk_cols), ("tmv", xv_cols), ("tmr", xr_cols)):
                em.emit("v", lambda v, j=j, tm=tm, dst=dst: v.scalar_tensor_tensor(
                    dst(j)[:, 1:T], db[0:128, 0:T - 1], vecs[tm][:, j:j + 1],
                    zb[0:128, 0:T - 1], ALU.mult, ALU.add),
                    inc=("mixv" if tm == "tmr" else None))
        em.wait("v", "cho")
        em.gemit("v", lambda v: v.memset(halo_i[:, :], 0.0))
        for r in range(8):
            em.gemit("v", lambda v, r=r: v.scalar_tensor_tensor(
                halo_i[:, :], scr[:, 2 * T + r * NCH:2 * T + (r + 1) * NCH],
                selt[:, r:r + 1], halo_i[:, :], ALU.mult, ALU.add))
        for tm, om, dst in (("tmk", omk, xk_cols), ("tmv", omv, xv_cols),
                            ("tmr", omr, xr_cols)):
            em.emit("v", lambda v, om=om: v.tensor_tensor(
                tmp[:, 128:128 + NCH], halo_i[:, :], om[:, :], ALU.mult))
            em.gemit("v", lambda v, tm=tm: v.tensor_tensor(
                tmp[:, 160:160 + NCH], z0[:, :], vecs[tm][:, :], ALU.mult))
            em.gemit("v", lambda v: v.tensor_tensor(
                tmp[:, 192:192 + NCH], tmp[:, 128:128 + NCH],
                tmp[:, 160:160 + NCH], ALU.add))
            for j in range(NCH):
                em.emit("v", lambda v, j=j, dst=dst: v.tensor_copy(
                    dst(j)[:, 0:1], tmp[:, 192 + j:193 + j]),
                    inc=("mixf" if j == NCH - 1 else None))
        em.wait("v", "mmpm", nst["pm_w"])
        em.wait("v", "evpm", nst["pm_r"])

        pk_cols = lambda b: slice(16384 + b * (NCH * MBLK),
                                  16384 + (b + 1) * (NCH * MBLK))
        pv_cols = lambda b: slice(16384 + (2 + b) * (NCH * MBLK),
                                  16384 + (3 + b) * (NCH * MBLK))
        pr_cols = lambda b: slice(b * (NCH * MBLK), (b + 1) * (NCH * MBLK))

        def load_proj(blk):
            b = blk % 2
            if blk >= 2:
                em.wait("g", "mmr", 2 * (blk - 1))
            if blk == 0:
                em.wait("g", "spl")
            for w, cols, buf, mw in ((di["wkT"], pk_cols, projk, 10 + 2 * b),
                                     (di["wvT"], pv_cols, projk, 14 + 2 * b),
                                     (di["wrT"], pr_cols, wob, None)):
                if blk < 2 and mw is not None:
                    em.wait("g", "mixv", mw)
                src_ = w[:, blk * MBLK:(blk + 1) * MBLK].rearrange(
                    "(j p) m -> p j m", p=128)
                em.dma("g", buf[:, cols(b)].rearrange("p (j m) -> p j m", j=NCH),
                       src_, "wpj")
        load_proj(0)
        load_proj(1)

        def emit_tensor(dj):
            blk, dc = divmod(dj, 2)
            b = blk % 2
            if dj % 2 == 0 and 2 <= dj and blk + 1 < 8:
                load_proj(blk + 1)
            em.wait("t", "wpj", 16 * 3 * (blk + 1))
            for s in range(6):
                em.wait("t", f"evsl{s}", slu[s])
                slu[s] += 1
            for buf, cols_, xc, mmn, P_ in ((projk, pk_cols, xk_cols, "mmk", PA),
                                            (projk, pv_cols, xv_cols, "mmv", PB),
                                            (wob, pr_cols, xr_cols, "mmr", PC)):
                for half in range(2):
                    hs = slice(half * 512, (half + 1) * 512)
                    for j in range(NCH):
                        em.emit("t", lambda t, b=b, dc=dc, j=j, hs=hs, buf=buf,
                                cols_=cols_, xc=xc, P_=P_: t.matmul(
                            P_[:, hs],
                            buf[:, cols_(b)][:, j * MBLK + dc * 128:
                                             j * MBLK + (dc + 1) * 128],
                            xc(j)[:, hs],
                            start=(j == 0), stop=(j == NCH - 1)),
                            inc=(mmn if (j == NCH - 1 and half == 1) else None))

        def emit_pre(dj):
            p = dj % 2
            e_t, b_t = SE[p], SB[p]
            so = 4 * p
            dbc = dec[:, dj:dj + 1].broadcast_to([128, T])
            em.wait("s", "mmk", dj + 1)
            if dj >= 2:
                em.wait("s", "sfree", dj - 1)
            em.emit("s", lambda s, e_t=e_t: s.activation(
                e_t[:, 0:512], PA[:, 0:512], AF.Exp), inc="evsl0")
            em.emit("s", lambda s, e_t=e_t: s.activation(
                e_t[:, 512:1024], PA[:, 512:1024], AF.Exp), inc="evsl1")
            em.wait("s", "mmr", dj + 1)
            em.emit("s", lambda s: s.activation(
                PC[:, 0:512], PC[:, 0:512], AF.Exp, scale=-1.0))
            em.emit("s", lambda s: s.activation(
                PC[:, 512:1024], PC[:, 512:1024], AF.Exp, scale=-1.0))
            em.emit("s", lambda s: s.activation(
                PC[:, 0:512], PC[:, 0:512], AF.Ln, bias=1.0))
            em.emit("s", lambda s: s.activation(
                PC[:, 512:1024], PC[:, 512:1024], AF.Ln, bias=1.0))
            em.emit("s", lambda s, dj=dj: s.activation(
                y_cols(dj)[:, 0:512], PC[:, 0:512], AF.Exp, scale=-1.0),
                inc="evsl4")
            em.emit("s", lambda s, dj=dj: s.activation(
                y_cols(dj)[:, 512:1024], PC[:, 512:1024], AF.Exp, scale=-1.0),
                inc="evsl5")
            em.wait("v", "evsl1", slu[1])
            em.wait("v", "mmv", dj + 1)
            em.emit("v", lambda v, e_t=e_t, b_t=b_t: v.tensor_tensor(
                b_t[:, 0:512], e_t[:, 0:512], PB[:, 0:512], ALU.mult),
                inc="evsl2")
            em.emit("v", lambda v, e_t=e_t, b_t=b_t: v.tensor_tensor(
                b_t[:, 512:1024], e_t[:, 512:1024], PB[:, 512:1024], ALU.mult),
                inc="evsl3")
            if dj >= 2:
                em.wait("v", "cin", 16 * (dj - 1))
            em.gemit("v", lambda v, dbc=dbc, b_t=b_t: v.tensor_tensor_scan(
                PM[:, :], dbc, b_t[:, :], 0.0, ALU.mult, ALU.add))
            em.gemit("v", lambda v, so=so: v.tensor_copy(
                sumt[:, so:so + 1], PM[:, T - 1:T]))
            em.gemit("v", lambda v, dbc=dbc, e_t=e_t: v.tensor_tensor_scan(
                PM[:, :], dbc, e_t[:, :], 0.0, ALU.mult, ALU.add))
            em.emit("v", lambda v, so=so: v.tensor_copy(
                sumt[:, so + 1:so + 2], PM[:, T - 1:T]), inc="sumv")
            em.wait("y", "sumv", dj + 1)
            em.dma("y", cc_s_in[dj][:, :], sumt[:, so:so + 2], "cin")
            em.wait("g", "cin", 16 * (dj + 1))
            em.emit_cc(lambda g, dj=dj: g.collective_compute(
                "AllGather", ALU.bypass, replica_groups=RG,
                ins=[cc_s_in[dj].opt()], outs=[cc_s_out[dj].opt()]), f"ag{dj}")
            em.wait("y", f"ag{dj}", 1)
            em.wait("y", "rsdone", max(0, dj - 1))
            dbase = 48 * (dj % 2)
            em.dma("y", dsc[:, dbase:dbase + 16].rearrange("p (r c) -> p r c", c=2),
                   cc_s_out[dj].rearrange("(r p) c -> p r c", p=128), "bdma")

        def emit_post(dj):
            p = dj % 2
            e_t, b_t = SE[p], SB[p]
            so = 4 * p
            dbc = dec[:, dj:dj + 1].broadcast_to([128, T])
            d1b = d1024[:, dj:dj + 1].broadcast_to([128, 8])
            dbase = 48 * (dj % 2)
            bnum_r = dsc[:, dbase:dbase + 16].rearrange(
                "p (r c) -> p c r", c=2)[:, 0, :]
            bden_r = dsc[:, dbase:dbase + 16].rearrange(
                "p (r c) -> p c r", c=2)[:, 1, :]
            em.wait("v", "bdma", 16 * (dj + 1))
            em.gemit("v", lambda v, d1b=d1b, bnum_r=bnum_r, dbase=dbase:
                     v.tensor_tensor_scan(
                dsc[:, dbase + 24:dbase + 32], d1b, bnum_r, 0.0,
                ALU.mult, ALU.add))
            em.gemit("v", lambda v, dbase=dbase: v.tensor_tensor(
                dsc[:, dbase + 32:dbase + 40], dsc[:, dbase + 24:dbase + 32],
                selt[:, :], ALU.mult))
            em.gemit("v", lambda v, so=so, dbase=dbase: v.reduce_sum(
                sumt[:, so + 2:so + 3], dsc[:, dbase + 32:dbase + 40], axis=AXX))
            em.gemit("v", lambda v, d1b=d1b, bden_r=bden_r, dbase=dbase:
                     v.tensor_tensor_scan(
                dsc[:, dbase + 24:dbase + 32], d1b, bden_r, 0.0,
                ALU.mult, ALU.add))
            em.gemit("v", lambda v, dbase=dbase: v.tensor_tensor(
                dsc[:, dbase + 32:dbase + 40], dsc[:, dbase + 24:dbase + 32],
                selt[:, :], ALU.mult))
            em.gemit("v", lambda v, so=so, dbase=dbase: v.reduce_sum(
                sumt[:, so + 3:so + 4], dsc[:, dbase + 32:dbase + 40], axis=AXX))
            em.emit("v", lambda v: v.engine_nop(), inc="sden")
            em.emit("v", lambda v: v.engine_nop(), inc="rsdone")
            eub = eu[:, dj:dj + 1].broadcast_to([128, T])
            em.emit("v", lambda v, b_t=b_t, eub=eub: v.tensor_tensor(
                tmp[:, :], b_t[:, :], eub, ALU.mult))
            em.emit("v", lambda v, e_t=e_t, eub=eub: v.tensor_tensor(
                SC4[:, :], e_t[:, :], eub, ALU.mult))
            em.gemit("v", lambda v, dbc=dbc, b_t=b_t, so=so: v.tensor_tensor_scan(
                b_t[:, :], dbc, b_t[:, :], sumt[:, so + 2:so + 3],
                ALU.mult, ALU.add))
            em.gemit("v", lambda v, dbc=dbc, e_t=e_t, so=so: v.tensor_tensor_scan(
                e_t[:, :], dbc, e_t[:, :], sumt[:, so + 3:so + 4],
                ALU.mult, ALU.add))
            em.emit("v", lambda v, b_t=b_t: v.tensor_tensor(
                tmp[:, 1:T], tmp[:, 1:T], b_t[:, 0:T - 1], ALU.add))
            em.emit("v", lambda v, e_t=e_t: v.tensor_tensor(
                SC4[:, 1:T], SC4[:, 1:T], e_t[:, 0:T - 1], ALU.add))
            em.gemit("v", lambda v, so=so: v.tensor_tensor(
                tmp[:, 0:1], tmp[:, 0:1], sumt[:, so + 2:so + 3], ALU.add))
            em.gemit("v", lambda v, so=so: v.tensor_tensor(
                SC4[:, 0:1], SC4[:, 0:1], sumt[:, so + 3:so + 4], ALU.add))
            em.emit("v", lambda v: v.engine_nop(), inc="dden")
            em.wait("s", "dden", dj + 1)
            em.emit("s", lambda s: s.activation(SC4[:, :], SC4[:, :], AF.Ln))
            em.emit("s", lambda s, e_t=e_t: s.activation(
                e_t[:, :], SC4[:, :], AF.Exp, scale=-1.0), inc="srec")
            em.wait("v", "srec", dj + 1)
            em.emit("v", lambda v, e_t=e_t: v.tensor_tensor(
                tmp[:, :], tmp[:, :], e_t[:, :], ALU.mult))
            em.wait("v", "evsl5", slu[5])
            em.emit("v", lambda v, dj=dj: v.tensor_tensor(
                y_cols(dj), tmp[:, :], y_cols(dj), ALU.mult), inc="sfree")

        em.wait("t", "mixf")
        for dj in range(NCH):
            emit_tensor(dj)
            emit_pre(dj)
            if dj >= 1:
                emit_post(dj - 1)
        emit_post(NCH - 1)
        nst["pm_r"] += 1
        em.emit("v", lambda v: v.engine_nop(), inc="evpm")

        em.wait("y", "mmr", 16)
        em.dma("y", x_res.rearrange("p (j t) -> p j t", j=NCH),
               spill.rearrange("(j p) t -> p j t", p=128), "rld")

        def load_wo2(blk):
            b = blk % 2
            if blk >= 2:
                em.wait("g", "mmwo", 2 * (blk - 1))
            else:
                em.wait("g", "mmr", 16)
            src_ = di["woT"][:, blk * MBLK:(blk + 1) * MBLK].rearrange(
                "(j p) m -> p j m", p=128)
            em.dma("g", wob[:, pr_cols(b)].rearrange("p (j m) -> p j m", j=NCH),
                   src_, "wwo")
        load_wo2(0)
        n_wo = 0
        for blk in range(8):
            b = blk % 2
            if blk + 1 < 8:
                load_wo2(blk + 1)
            em.wait("t", "wwo", 16 * (blk + 1))
            for cc_ in range(2):
                ci = blk * 2 + cc_
                em.wait("t", "evsl0", slu[0])
                em.wait("t", "evsl1", slu[1])
                slu[0] += 1; slu[1] += 1
                for half in range(2):
                    hs = slice(half * 512, (half + 1) * 512)
                    for j in range(NCH):
                        if n_wo == 0 and half == 0:
                            em.wait("t", "sfree", j + 1)
                        em.emit("t", lambda t, b=b, cc_=cc_, j=j, hs=hs: t.matmul(
                            PA[:, hs],
                            wob[:, pr_cols(b)][:, j * MBLK + cc_ * 128:
                                               j * MBLK + (cc_ + 1) * 128],
                            y_cols(j)[:, hs],
                            start=(j == 0), stop=(j == NCH - 1)),
                            inc=("mmwo" if (j == NCH - 1 and half == 1) else None))
                n_wo += 1
                em.wait("v", "mmwo", n_wo)
                if ci == 0:
                    em.wait("v", "rld")
                em.emit("v", lambda v, ci=ci: v.tensor_tensor(
                    x_res[:, ci * T:(ci + 1) * T][:, 0:512], PA[:, 0:512],
                    x_res[:, ci * T:(ci + 1) * T][:, 0:512], ALU.add),
                    inc="evsl0")
                em.emit("v", lambda v, ci=ci: v.tensor_tensor(
                    x_res[:, ci * T:(ci + 1) * T][:, 512:1024], PA[:, 512:1024],
                    x_res[:, ci * T:(ci + 1) * T][:, 512:1024], ALU.add),
                    inc="evsl1")
                em.emit("v", lambda v: v.engine_nop(), inc="evwo")
        em.wait("v", "evwo", n_wo)
        em.wait("y", "evwo", n_wo)
        em.wait("v", "mmwo", n_wo)

    xn_cols = lambda j: xn[:, j * T:(j + 1) * T]
    x_cols = lambda j: slice(j * T, (j + 1) * T)

    ln_stats(x_res, x_cols)
    if upto in ("ln1",):
        ln_apply(xn_cols, x_res, x_cols)
    if dbg == "xn1":
        em.wait("g", "lnap")
        for j in range(NCH):
            em.dma("g", ddbg[j * 128:(j + 1) * 128, :], xn_cols(j), "dbo")
        em.wait("y", "dbo")
    if upto not in ("ln1",):
        gating(x_res, x_cols,
               mid_emit=lambda: ln_apply(xn_cols, x_res, x_cols))
        if dbg == "cw":
            em.wait("g", "evpm", nst["pm_r"])
            em.dma("g", ddbg[0:8, :], cwT[:, :], "dbo")
            em.wait("y", "dbo")
        ffn_pass(xn_cols, "a")
        if dbg == "ffn1":
            em.dma("y", ddbg.rearrange("(j p) t -> p j t", p=128),
                   x_res.rearrange("p (j t) -> p j t", j=NCH), "dbo")
            em.wait("y", "dbo")
    if upto in ("attn", "full"):
        attn_pass()
        if dbg == "attn":
            em.dma("y", ddbg.rearrange("(j p) t -> p j t", p=128),
                   x_res.rearrange("p (j t) -> p j t", j=NCH), "dbo")
            em.wait("y", "dbo")
    if upto == "full":
        em.wait("v", "mmr", 16)
        ln_stats(x_res, x_cols, lnw=lambda j: j + 1, lnsem="evwo")
        gating(x_res, x_cols,
               mid_emit=lambda: ln_apply(xn_cols, x_res, x_cols))
        ffn_pass(xn_cols, "b")

    em.dma("y", dout.rearrange("(j p) t -> p j t", p=128),
           x_res.rearrange("p (j t) -> p j t", j=NCH), "so")
    em.wait("y", "so")

    with nc.Block() as block:
        block.tensor(lambda t: [f(t) for f in em.q["t"]])
        block.vector(lambda v: [f(v) for f in em.q["v"]])
        block.scalar(lambda s: [f(s) for f in em.q["s"]])
        block.gpsimd(lambda gq: [f(gq) for f in em.q["g"]])
        block.sync(lambda sy: [f(sy) for f in em.q["y"]])
    stack.close()
    return nc


def make_inputs(ins, core):
    x = np.asarray(ins["L0_attn_x"])[0]
    xT = np.ascontiguousarray(x.T)
    cvec = lambda v: np.ascontiguousarray(
        np.asarray(v, np.float32).reshape(NCH, 128).T)
    e8 = np.zeros((E, E * 128), np.float32)
    for e in range(E):
        e8[e, e * 128:(e + 1) * 128] = 1.0
    sel = np.zeros((128, 8), np.float32)
    if core > 0:
        sel[:, core - 1] = 1.0
    hmask = np.full((128, 1), 0.0 if core == 0 else 1.0, np.float32)
    return {
        "x": np.ascontiguousarray(xT[:, core * T:(core + 1) * T]),
        "wkT": np.ascontiguousarray(np.asarray(ins["L0_attn_Wk"]).T),
        "wvT": np.ascontiguousarray(np.asarray(ins["L0_attn_Wv"]).T),
        "wrT": np.ascontiguousarray(np.asarray(ins["L0_attn_Wr"]).T),
        "woT": np.ascontiguousarray(np.asarray(ins["L0_attn_Wo"]).T),
        "wi0": np.ascontiguousarray(np.asarray(ins["L0_ffn_wi_0"])),
        "wi1": np.ascontiguousarray(np.asarray(ins["L0_ffn_wi_1"])),
        "wo2": np.ascontiguousarray(np.asarray(ins["L0_ffn_wo"])),
        "gk": np.ascontiguousarray(np.asarray(ins["L0_ffn_gate_kernel"])),
        "td": cvec(ins["L0_attn_time_decay"]),
        "tf": cvec(ins["L0_attn_time_first"]),
        "tmk": cvec(ins["L0_attn_time_mix_k"]),
        "tmv": cvec(ins["L0_attn_time_mix_v"]),
        "tmr": cvec(ins["L0_attn_time_mix_r"]),
        "e8": e8, "ident": np.eye(128, dtype=np.float32),
        "sel": sel, "hmask": hmask,
    }


from concourse.bass_utils import run_bass_kernel_spmd

_NC_CACHE = {}

def kernel(**inputs):
    if "nc" not in _NC_CACHE:
        _NC_CACHE["nc"] = build(upto="full", dbg=None)
    nc = _NC_CACHE["nc"]
    in_maps = [make_inputs(inputs, c) for c in range(8)]
    res = run_bass_kernel_spmd(nc, in_maps, core_ids=list(range(8)))
    out = np.concatenate([res.results[c]["out"].T for c in range(8)], axis=0)
    return np.ascontiguousarray(out, dtype=np.float32).reshape(1, 8192, 2048)


# version 20
# speedup vs baseline: 1.0055x; 1.0055x over previous
import contextlib
import numpy as np
import concourse.bass as bass
import concourse.mybir as mybir

F32 = mybir.dt.float32
F32R = mybir.dt.float32r
BF16 = mybir.dt.bfloat16
AF = mybir.ActivationFunctionType
ALU = mybir.AluOpType
AXX = mybir.AxisListType.X

D = 2048
M = 2048
E = 8
T = 1024
NCH = 16
MBLK = 256
NBLK = M // MBLK
EPS = 1e-5
RG = [[0, 1, 2, 3, 4, 5, 6, 7]]


class Em:
    def __init__(self, nc, stack):
        self.nc = nc
        self.stack = stack
        self.q = {k: [] for k in ("t", "v", "s", "g", "y")}
        self.sems = {}
        self.cnt = {}

    def _sem(self, name):
        if name not in self.sems:
            self.sems[name] = self.stack.enter_context(self.nc.semaphore(name))
            self.cnt[name] = 0
        return self.sems[name]

    def emit(self, eng, fn, inc=None, inc_n=1):
        sem_h = None
        if inc is not None:
            sem_h = self._sem(inc)
            self.cnt[inc] += inc_n
        def run(engine, fn=fn, sem_h=sem_h, inc_n=inc_n):
            ins = fn(engine)
            if sem_h is not None:
                ins.then_inc(sem_h, inc_n)
            return ins
        self.q[eng].append(run)

    def emit_cc(self, fn, inc):
        sem_h = self._sem(inc)
        self.cnt[inc] += 1
        def run(engine, fn=fn, sem_h=sem_h):
            return fn(engine).then_inc(sem_h)
        self.q["g"].append(run)

    def wait(self, eng, sem, value=None):
        v = self.cnt[sem] if value is None else value
        h = self._sem(sem)
        self.q[eng].append(lambda engine, h=h, v=v: engine.wait_ge(h, v))

    def val(self, sem):
        self._sem(sem)
        return self.cnt[sem]

    def gemit(self, eng, fn):
        self.emit(eng, fn, inc="chain_" + eng)
        self.wait(eng, "chain_" + eng)

    def dma(self, eng, out, in_, sem):
        self.emit(eng, lambda q, o=out, i=in_: q.dma_start(out=o, in_=i),
                  inc=sem, inc_n=16)


def build(upto="full", dbg=None):
    nc = bass.Bass("TRN2", target_bir_lowering=False)
    stack = contextlib.ExitStack()

    di = {}
    di["x"] = nc.dram_tensor("x", [D, T], F32, kind="ExternalInput").ap()
    for w in ("wkT", "wvT", "wrT", "woT"):
        di[w] = nc.dram_tensor(w, [D, D], F32, kind="ExternalInput").ap()
    di["wi0"] = nc.dram_tensor("wi0", [E, D, M], F32, kind="ExternalInput").ap()
    di["wi1"] = nc.dram_tensor("wi1", [E, D, M], F32, kind="ExternalInput").ap()
    di["wo2"] = nc.dram_tensor("wo2", [E, M, D], F32, kind="ExternalInput").ap()
    di["gk"] = nc.dram_tensor("gk", [D, E], F32, kind="ExternalInput").ap()
    for w in ("td", "tf", "tmk", "tmv", "tmr"):
        di[w] = nc.dram_tensor(w, [128, NCH], F32, kind="ExternalInput").ap()
    di["e8"] = nc.dram_tensor("e8", [E, E * 128], F32, kind="ExternalInput").ap()
    di["ident"] = nc.dram_tensor("ident", [128, 128], F32, kind="ExternalInput").ap()
    di["sel"] = nc.dram_tensor("sel", [128, 8], F32, kind="ExternalInput").ap()
    di["hmask"] = nc.dram_tensor("hmask", [128, 1], F32, kind="ExternalInput").ap()

    dout = nc.dram_tensor("out", [D, T], F32, kind="ExternalOutput").ap()
    ddbg = nc.dram_tensor("dbg", [D, T], F32, kind="ExternalOutput").ap() \
        if dbg is not None else None

    spill = nc.dram_tensor("spill", [D, T], F32, kind="Internal").ap()
    cc_h_in = nc.dram_tensor("cc_h_in", [128, NCH], F32, kind="Internal").ap()
    cc_h_out = nc.dram_tensor("cc_h_out", [8 * 128, NCH], F32, kind="Internal").ap()
    cc_s_in = [nc.dram_tensor(f"cc_s_in{j}", [128, 2], F32, kind="Internal").ap()
               for j in range(NCH)]
    cc_s_out = [nc.dram_tensor(f"cc_s_out{j}", [8 * 128, 2], F32, kind="Internal").ap()
                for j in range(NCH)]

    em = Em(nc, stack)
    sb = lambda name, shape, dtype: stack.enter_context(
        nc.sbuf_tensor(name, shape, dtype)).ap()
    ps = lambda name, shape: stack.enter_context(
        nc.psum_tensor(name, shape, F32)).ap()

    x_res = sb("x_res", [128, NCH * T], F32)
    ar2 = sb("ar2", [128, NCH * T // 2], F32)
    ar3 = sb("ar3", [128, NCH * T // 2], F32)
    ar4 = sb("ar4", [128, NCH * T // 2], F32)
    ar5 = sb("ar5", [128, NCH * T // 4], F32)
    scr = sb("scr", [128, 5 * T], F32)
    tmp = sb("tmpt", [128, T], F32)

    xn = ar2.bitcast(BF16)
    xrT = ar2.bitcast(BF16)
    g_ = ar3.bitcast(BF16)
    y_ = ar3.bitcast(BF16)
    wib = ar4.bitcast(BF16)
    xkT = ar4.bitcast(BF16)
    xvT = x_res.bitcast(BF16)
    wob = ar5.bitcast(BF16)

    vecs = {w: sb("v_" + w, [128, NCH], F32) for w in ("td", "tf", "tmk", "tmv", "tmr")}
    onev = sb("onev", [128, 1], F32)
    onevb = sb("onevb", [128, 1], BF16)
    ones1 = sb("ones1", [1, 128], F32)
    e8t = sb("e8t", [E, E * 128], BF16)
    ident = sb("identt", [128, 128], F32)
    selt = sb("selt", [128, 8], F32)
    hmaskt = sb("hmaskt", [128, 1], F32)
    gkf = sb("gkf", [128, NCH * E], F32)
    gksb = sb("gksb", [128, E], F32)
    invcol = sb("invcol", [128, 8], F32)
    nm2col = sb("nm2col", [128, 8], F32)
    cwT = sb("cwT", [E, T], BF16)
    invb = scr[:, 0:T]
    nm2b = scr[:, T:2 * T]
    cwb = scr[:, 2 * T:3 * T]
    mean = scr[0:1, 3 * T:4 * T]
    ex2 = scr[0:1, 4 * T:5 * T]
    var = ex2
    inv = ex2
    nm2 = mean
    halo_o = sb("halo_o", [128, NCH], F32)
    halo_i = sb("halo_i", [128, NCH], F32)
    z0 = sb("z0", [128, NCH], F32)
    eu = sb("eu", [128, NCH], F32)
    omk = sb("omk", [128, NCH], F32)
    omv = sb("omv", [128, NCH], F32)
    omr = sb("omr", [128, NCH], F32)
    dec = sb("dec", [128, NCH], F32)
    d1024 = sb("d1024", [128, NCH], F32)
    w_ = sb("w_", [128, NCH], F32)
    dsc = sb("dscr", [128, 96], F32)
    sumt = sb("sumt", [128, 8], F32)

    PA = ps("PA", [128, T])
    PB = ps("PB", [128, T])
    PC = ps("PC", [128, T])
    PM = ps("PM", [128, T])
    PST = [PA, PB, PC]
    SL = lambda s: PST[s // 2][:, (s % 2) * 512:(s % 2 + 1) * 512]
    slu = [0] * 6

    nst = {"pm_w": 0, "pm_r": 0}

    def pm_write_group(mm_emits):
        em.wait("t", "evpm", nst["pm_r"])
        for i, f in enumerate(mm_emits):
            em.emit("t", f, inc=("mmpm" if i == len(mm_emits) - 1 else None))
        nst["pm_w"] += 1

    def pm_read_chain(eng, fns):
        em.wait(eng, "mmpm", nst["pm_w"])
        for i, f in enumerate(fns):
            em.emit(eng, f, inc=("evpm" if i == len(fns) - 1 else None))
        nst["pm_r"] += 1
        return nst["pm_r"]

    for name in ("td", "tf", "tmk", "tmv", "tmr"):
        em.dma("y", vecs[name][:, :], di[name][:, :], "sx")
    em.dma("g", e8t[:, :], di["e8"][:, :], "wge")
    em.dma("y", ident[:, :], di["ident"][:, :], "sx")
    em.dma("y", selt[:, :], di["sel"][:, :], "sx")
    em.dma("y", hmaskt[:, :], di["hmask"][:, :], "sx")
    em.dma("y", gkf.rearrange("p (j e) -> p j e", j=NCH),
           di["gk"].rearrange("(j p) e -> p j e", p=128), "wg")
    em.dma("y", x_res.rearrange("p (j t) -> p j t", j=NCH),
           di["x"].rearrange("(j p) t -> p j t", p=128), "sx")
    for e_ in ("v", "s", "t"):
        em.wait(e_, "sx")
    em.wait("t", "wg")
    em.wait("t", "wge")
    em.wait("v", "wg")
    em.emit("v", lambda v: v.memset(onev[:, :], 1.0))
    em.emit("v", lambda v: v.memset(onevb[:, :], 1.0))
    em.emit("v", lambda v: v.memset(ones1[:, :], 1.0), inc="vc")
    em.wait("t", "vc")
    em.emit("s", lambda s: s.activation(w_[:, :], vecs["td"][:, :], AF.Exp), inc="sc")
    em.wait("v", "sc")
    em.emit("v", lambda v: v.tensor_scalar(w_[:, :], w_[:, :], -1.0, None, ALU.mult),
            inc="vc")
    em.wait("s", "vc")
    em.emit("s", lambda s: s.activation(dec[:, :], w_[:, :], AF.Exp))
    em.emit("s", lambda s: s.activation(d1024[:, :], w_[:, :], AF.Exp, scale=1024.0))
    em.emit("s", lambda s: s.activation(eu[:, :], vecs["tf"][:, :], AF.Exp), inc="sc")
    for nm, om in (("tmk", omk), ("tmv", omv), ("tmr", omr)):
        em.emit("v", lambda v, nm=nm, om=om: v.tensor_scalar(
            om[:, :], vecs[nm][:, :], -1.0, 1.0, ALU.mult, ALU.add), inc="vc")

    mms = [lambda t, j=j: t.matmul(PM[0:1, 0:E], onev[:, :],
                                   gkf[:, j * E:(j + 1) * E],
                                   start=(j == 0), stop=(j == NCH - 1))
           for j in range(NCH)]
    pm_write_group(mms)
    pm_read_chain("s", [lambda s: s.activation(tmp[0:1, 0:E], PM[0:1, 0:E],
                                               AF.Copy)])
    em.wait("t", "evpm", nst["pm_r"])
    pm_write_group([lambda t: t.matmul(PM[:, 0:E], ones1[:, :], tmp[0:1, 0:E],
                                       start=True, stop=True)])
    pm_read_chain("s", [lambda s: s.activation(gksb[:, :], PM[:, 0:E], AF.Copy)])

    def ln_stats(src_ap, cols, lnw=None, lnsem="evoa"):
        em.wait("t", "evpm", nst["pm_r"])
        tbm = tmp.bitcast(BF16)
        nsq = 0
        for half in range(2):
            for j in range(NCH):
                sl = slice((nsq % 4) * 512, (nsq % 4) * 512 + 512)
                if nsq >= 4:
                    em.wait("s", "mmsq", em.val("mmsq") - 3)
                if lnw is not None and half == 0:
                    em.wait("s", lnsem, lnw(j))
                em.emit("s", lambda s, j=j, half=half, sl=sl: s.activation(
                    tbm[:, sl], src_ap[:, cols(j)][:, half * 512:(half + 1) * 512],
                    AF.Copy), inc="sq")
                em.wait("t", "sq")
                em.emit("t", lambda t, j=j, half=half, sl=sl: t.matmul(
                    PM[0:1, half * 512:(half + 1) * 512],
                    onevb[:, :], tbm[:, sl],
                    start=(j == 0), stop=(j == NCH - 1)),
                    inc="mmsq")
                if half == 1 and j == NCH - 1:
                    em.emit("t", lambda t: t.matmul(
                        PC[0:1, 256:512], onevb[:, :],
                        tbm[:, 0:256], start=True, stop=True), inc="mmpm")
                nsq += 1
        nst["pm_w"] += 1
        pm_read_chain("s", [
            lambda s: s.activation(mean[0:1, 0:512], PM[0:1, 0:512],
                                   AF.Copy, scale=1.0 / D),
            lambda s: s.activation(mean[0:1, 512:1024], PM[0:1, 512:1024],
                                   AF.Copy, scale=1.0 / D)])
        em.wait("t", "evpm", nst["pm_r"])
        tb = tmp.bitcast(BF16)
        nsq = 0
        for half in range(2):
            for j in range(NCH):
                sl = slice((nsq % 4) * 512, (nsq % 4) * 512 + 512)
                if nsq >= 4:
                    em.wait("s", "mmsq", em.val("mmsq") - 3)
                if lnw is not None and half == 0:
                    em.wait("s", lnsem, lnw(j))
                em.emit("s", lambda s, j=j, half=half, sl=sl: s.activation(
                    tb[:, sl], src_ap[:, cols(j)][:, half * 512:(half + 1) * 512],
                    AF.Square), inc="sq")
                em.wait("t", "sq")
                fin = (half == 1 and j == NCH - 1)
                em.emit("t", lambda t, j=j, half=half, sl=sl: t.matmul(
                    PM[0:1, half * 512:(half + 1) * 512],
                    onevb[:, :], tb[:, sl],
                    start=(j == 0), stop=(j == NCH - 1)), inc="mmsq")
                if fin:
                    em.emit("t", lambda t: t.matmul(
                        PC[0:1, 0:256], onevb[:, :],
                        tb[:, 0:256], start=True, stop=True),
                        inc="mmpm")
                nsq += 1
        nst["pm_w"] += 1
        pm_read_chain("v", [
            lambda v: v.tensor_scalar(ex2[0:1, 0:512], PM[0:1, 0:512],
                                      1.0 / D, None, ALU.mult),
            lambda v: v.tensor_scalar(ex2[0:1, 512:1024], PM[0:1, 512:1024],
                                      1.0 / D, None, ALU.mult)])
        em.gemit("v", lambda v: v.tensor_tensor(tmp[0:1, 0:T], mean, mean, ALU.mult))
        em.gemit("v", lambda v: v.tensor_tensor(var, ex2, tmp[0:1, 0:T], ALU.subtract))
        em.emit("v", lambda v: v.tensor_scalar(var, var, EPS, None, ALU.add), inc="vc")
        em.wait("s", "vc")
        em.gemit("s", lambda s: s.activation(tmp[0:1, 0:T], var, AF.Ln))
        em.emit("s", lambda s: s.activation(inv, tmp[0:1, 0:T], AF.Exp,
                                            scale=-0.5), inc="sc")
        em.wait("v", "sc")
        em.gemit("v", lambda v: v.tensor_tensor(nm2, mean, inv, ALU.mult))
        em.emit("v", lambda v: v.tensor_scalar(nm2, nm2, -1.0, None, ALU.mult),
                inc="vc")
        em.wait("t", "vc")
        for vsrc, vdst in ((inv, invb), (nm2, nm2b)):
            mms = []
            for half in range(2):
                mms.append(lambda t, half=half, vsrc=vsrc: t.matmul(
                    PM[:, half * 512:(half + 1) * 512],
                    ones1[:, :],
                    vsrc[0:1, half * 512:(half + 1) * 512],
                    start=True, stop=True))
            pm_write_group(mms)
            pm_read_chain("s", [
                lambda s, vdst=vdst: s.activation(vdst[:, 0:512], PM[:, 0:512], AF.Copy),
                lambda s, vdst=vdst: s.activation(vdst[:, 512:1024], PM[:, 512:1024],
                                                  AF.Copy)])
        em.wait("v", "evpm", nst["pm_r"])

    def ln_apply(dst_cols, src_ap, cols, save_cols=False):
        for j in range(NCH):
            em.emit("v", lambda v, j=j: v.tensor_tensor(
                tmp[:, :], src_ap[:, cols(j)], invb[:, :], ALU.mult))
            em.emit("v", lambda v, j=j: v.tensor_tensor(
                dst_cols(j), tmp[:, :], nm2b[:, :], ALU.add), inc="lnap")
            if save_cols:
                em.emit("v", lambda v, j=j: v.tensor_tensor(
                    z0[:, j:j + 1], tmp[:, 0:1], nm2b[:, 0:1], ALU.add))
                em.emit("v", lambda v, j=j: v.tensor_tensor(
                    halo_o[:, j:j + 1], tmp[:, T - 1:T], nm2b[:, T - 1:T],
                    ALU.add), inc="lnap")

    def gating(src_ap, cols, mid_emit=None):
        X = tmp
        mms = []
        for tc in range(8):
            for j in range(NCH):
                mms.append(lambda t, tc=tc, j=j: t.matmul(
                    PM[:, tc * 8:(tc + 1) * 8],
                    src_ap[:, cols(j)][:, tc * 128:(tc + 1) * 128],
                    gkf[:, j * E:(j + 1) * E],
                    start=(j == 0), stop=(j == NCH - 1)))
        for tc in range(8):
            mms.append(lambda t, tc=tc: t.matmul(
                PM[:, 512 + tc:513 + tc], inv[0:1, tc * 128:(tc + 1) * 128],
                ones1[0:1, 0:1], start=True, stop=True))
            mms.append(lambda t, tc=tc: t.matmul(
                PM[:, 520 + tc:521 + tc], nm2[0:1, tc * 128:(tc + 1) * 128],
                ones1[0:1, 0:1], start=True, stop=True))
        pm_write_group(mms)
        if mid_emit is not None:
            mid_emit()
        em.wait("s", "mmpm", nst["pm_w"])
        em.gemit("s", lambda s: s.activation(invcol[:, 0:8], PM[:, 512:520],
                                             AF.Copy))
        em.emit("s", lambda s: s.activation(nm2col[:, 0:8], PM[:, 520:528],
                                            AF.Copy), inc="scst")
        r3 = lambda ap: ap.rearrange("p (tc e) -> p tc e", e=8)
        bc = lambda ap: ap.rearrange("p (tc one) -> p tc one", one=1).broadcast_to(
            [128, 8, 8])
        l_all = X[:, 0:64]; t2 = X[:, 64:128]; m1 = X[:, 128:136]
        m2 = X[:, 136:144]; w1 = X[:, 144:152]; w2 = X[:, 152:160]
        mk1 = X[:, 160:224]; l2 = X[:, 224:288]; mk2 = X[:, 288:352]
        u1 = X[:, 352:416]; cw_all = X[:, 416:480]
        gksb3 = gksb.rearrange("p (one e) -> p one e", one=1).broadcast_to(
            [128, 8, 8])
        em.wait("v", "mmpm", nst["pm_w"])
        em.wait("v", "scst")
        em.gemit("v", lambda v: v.tensor_tensor(r3(t2), gksb3, bc(nm2col[:, 0:8]),
                                                ALU.mult))
        em.gemit("v", lambda v: v.tensor_tensor(r3(l_all), r3(PM[:, 0:64]),
                                                bc(invcol[:, 0:8]), ALU.mult))
        em.emit("v", lambda v: v.tensor_tensor(l_all, l_all, t2, ALU.add),
                inc="evpm")
        em.gemit("v", lambda v: v.engine_nop())
        nst["pm_r"] += 1
        em.gemit("v", lambda v: v.tensor_reduce(
            m1.rearrange("p (tc one) -> p tc one", one=1), r3(l_all),
            axis=AXX, op=ALU.max))
        em.gemit("v", lambda v: v.tensor_tensor(r3(mk1), r3(l_all), bc(m1),
                                                ALU.is_equal))
        em.gemit("v", lambda v: v.scalar_tensor_tensor(l2, mk1, -1e30, l_all,
                                                       ALU.mult, ALU.add))
        em.gemit("v", lambda v: v.tensor_reduce(
            m2.rearrange("p (tc one) -> p tc one", one=1), r3(l2),
            axis=AXX, op=ALU.max))
        em.gemit("v", lambda v: v.tensor_tensor(r3(mk2), r3(l2), bc(m2),
                                                ALU.is_equal))
        em.emit("v", lambda v: v.tensor_tensor(w1, m1, m2, ALU.subtract),
                inc="vc")
        em.wait("s", "vc")
        em.emit("s", lambda s: s.activation(w1, w1, AF.Sigmoid), inc="scst")
        em.wait("v", "scst")
        em.gemit("v", lambda v: v.tensor_scalar(w2, w1, -1.0, 1.0, ALU.mult,
                                                ALU.add))
        em.gemit("v", lambda v: v.tensor_tensor(r3(u1), r3(mk1), bc(w1),
                                                ALU.mult))
        em.gemit("v", lambda v: v.tensor_tensor(r3(cw_all), r3(mk2), bc(w2),
                                                ALU.mult))
        em.emit("v", lambda v: v.tensor_tensor(cw_all, cw_all, u1, ALU.add),
                inc="vc")
        em.wait("t", "vc")
        for grp in range(2):
            mms = []
            for q in range(4):
                tc = grp * 4 + q
                mms.append(lambda t, tc=tc, q=q, grp=grp: t.transpose(
                    PM[0:8, grp * 512 + q * 128:grp * 512 + (q + 1) * 128],
                    cw_all[:, tc * 8:(tc + 1) * 8], ident[:, :]))
            pm_write_group(mms)
            pm_read_chain("s", [lambda s, grp=grp: s.activation(
                cwT[:, grp * 512:(grp + 1) * 512],
                PM[0:8, grp * 512:(grp + 1) * 512], AF.Copy)])

    def cw_broadcast(e, scalar_wait):
        mms = []
        for half in range(2):
            mms.append(lambda t, e=e, half=half: t.matmul(
                PM[:, half * 512:(half + 1) * 512],
                e8t[:, e * 128:(e + 1) * 128],
                cwT[:, half * 512:(half + 1) * 512],
                start=True, stop=True))
        pm_write_group(mms)
        if scalar_wait is not None:
            em.wait("s", scalar_wait[0], scalar_wait[1])
        return pm_read_chain("s", [
            lambda s: s.activation(cwb[:, 0:512], PM[:, 0:512], AF.Copy),
            lambda s: s.activation(cwb[:, 512:1024], PM[:, 512:1024], AF.Copy)])

    def ffn_pass(xn_cols, tag):
        wsem, wosem = "wf" + tag, "wq" + tag
        mmh, mmo = "mmh" + tag, "mmo" + tag
        evg, evo = "evg" + tag, "evo" + tag
        wi_cols = lambda b, m_: slice((b * 2 + m_) * (NCH * MBLK),
                                      (b * 2 + m_ + 1) * (NCH * MBLK))
        wo_cols = lambda b: slice(b * (NCH * MBLK), (b + 1) * (NCH * MBLK))

        def load_wi(nuse):
            e, blk = divmod(nuse, NBLK)
            b = nuse % 2
            if nuse == 0 and tag == "b":
                em.wait("g", "mmr", 16)
            if nuse >= 2:
                em.wait("g", mmh, 4 * (nuse - 1))
            for m_, w in ((0, di["wi0"]), (1, di["wi1"])):
                src_ = w[e, :, blk * MBLK:(blk + 1) * MBLK].rearrange(
                    "(j p) m -> p j m", p=128)
                dst = wib[:, wi_cols(b, m_)].rearrange("p (j m) -> p j m", j=NCH)
                em.dma("g", dst, src_, wsem)

        def load_wo(nuse):
            e, blk = divmod(nuse, NBLK)
            b = nuse % 2
            if nuse == 0 and tag == "b":
                em.wait("g", "mmwo", 16)
            if nuse >= 2:
                em.wait("g", mmo, 4 * (nuse - 1))
            src_ = di["wo2"][e, :, blk * MBLK:(blk + 1) * MBLK].rearrange(
                "(j p) m -> p j m", p=128)
            dst = wob[:, wo_cols(b)].rearrange("p (j m) -> p j m", j=NCH)
            em.dma("g", dst, src_, wosem)

        load_wi(0)
        load_wo(0)
        em.wait("t", "lnap")
        sil_gate = em.val("vc")
        n_h = 0
        n_o = 0
        n_g = 0
        for e in range(E):
            cwb_rdy = cw_broadcast(e, ((evg, 32 * e) if e > 0 else None))
            for blk in range(NBLK):
                nuse = e * NBLK + blk
                b = nuse % 2
                if nuse + 1 < E * NBLK:
                    load_wi(nuse + 1)
                em.wait("t", wsem, 16 * 2 * (nuse + 1))
                for mc in range(2):
                    mi = blk * 2 + mc
                    for half in range(2):
                        hs = slice(half * 512, (half + 1) * 512)
                        s0 = (2 * n_h) % 6
                        s1 = (2 * n_h + 1) % 6
                        em.wait("t", f"evsl{s0}", slu[s0])
                        em.wait("t", f"evsl{s1}", slu[s1])
                        slu[s0] += 1; slu[s1] += 1
                        for j in range(NCH):
                            em.emit("t", lambda t, b=b, mc=mc, j=j, hs=hs, s0=s0:
                                    t.matmul(SL(s0),
                                             wib[:, wi_cols(b, 0)][:, j * MBLK + mc * 128:
                                                                   j * MBLK + (mc + 1) * 128],
                                             xn_cols(j)[:, hs],
                                             start=(j == 0), stop=(j == NCH - 1)))
                        for j in range(NCH):
                            em.emit("t", lambda t, b=b, mc=mc, j=j, hs=hs, s1=s1:
                                    t.matmul(SL(s1),
                                             wib[:, wi_cols(b, 1)][:, j * MBLK + mc * 128:
                                                                   j * MBLK + (mc + 1) * 128],
                                             xn_cols(j)[:, hs],
                                             start=(j == 0), stop=(j == NCH - 1)),
                                    inc=(mmh if j == NCH - 1 else None))
                        n_h += 1
                        th = slice((n_h % 2) * 512, (n_h % 2) * 512 + 512)
                        em.wait("s", mmh, n_h)
                        if n_h == 1:
                            em.wait("s", "vc", sil_gate)
                        if n_g >= 2:
                            em.wait("s", evg, n_g - 1)
                        em.emit("s", lambda s, th=th, s0=s0: s.activation(
                            tmp[:, th], SL(s0), AF.Silu), inc=f"evsl{s0}")
                        em.wait("v", f"evsl{s0}", slu[s0])
                        if n_h == 1:
                            em.wait("v", "evpm", cwb_rdy)
                        em.emit("v", lambda v, th=th, s1=s1: v.tensor_tensor(
                            tmp[:, th], tmp[:, th], SL(s1), ALU.mult),
                            inc=f"evsl{s1}")
                        em.emit("v", lambda v, mi=mi, hs=hs, th=th: v.tensor_tensor(
                            g_[:, mi * T:(mi + 1) * T][:, hs], tmp[:, th],
                            cwb[:, hs], ALU.mult), inc=evg)
                        n_g += 1
            last_w = [32 * e + 2]
            def gwait(j, e=e, last_w=last_w):
                tgt = 32 * e + 2 * (j + 1)
                if tgt > last_w[0]:
                    em.wait("t", evg, tgt)
                    last_w[0] = tgt
            for blk in range(NBLK):
                nuse = e * NBLK + blk
                b = nuse % 2
                if nuse + 1 < E * NBLK:
                    load_wo(nuse + 1)
                em.wait("t", wosem, 16 * (nuse + 1))
                for dc in range(2):
                    dii = blk * 2 + dc
                    for half in range(2):
                        hs = slice(half * 512, (half + 1) * 512)
                        s0 = n_o % 6
                        em.wait("t", f"evsl{s0}", slu[s0])
                        slu[s0] += 1
                        for j in range(NCH):
                            gwait(j)
                            em.emit("t", lambda t, b=b, dc=dc, j=j, hs=hs, s0=s0:
                                    t.matmul(SL(s0),
                                             wob[:, wo_cols(b)][:, j * MBLK + dc * 128:
                                                                j * MBLK + (dc + 1) * 128],
                                             g_[:, j * T:(j + 1) * T][:, hs],
                                             start=(j == 0), stop=(j == NCH - 1)),
                                    inc=(mmo if j == NCH - 1 else None))
                        n_o += 1
                        em.wait("v", mmo, n_o)
                        em.emit("v", lambda v, dii=dii, hs=hs, s0=s0:
                                v.scalar_tensor_tensor(
                                    x_res[:, dii * T:(dii + 1) * T][:, hs], SL(s0),
                                    0.5, x_res[:, dii * T:(dii + 1) * T][:, hs],
                                    ALU.mult, ALU.add), inc=f"evsl{s0}")
                        em.emit("v", lambda v: v.engine_nop(), inc=evo)
        em.wait("y", evo, n_o)
        em.wait("g", evo, n_o)
        em.wait("v", evo, n_o)
        if tag != "a":
            em.wait("t", evo, n_o)
            em.wait("s", evo, n_o)

    def attn_pass():
        xk_cols = lambda j: xkT[:, j * T:(j + 1) * T]
        xv_cols = lambda j: xvT[:, j * T:(j + 1) * T]
        xr_cols = lambda j: xrT[:, j * T:(j + 1) * T]
        y_cols = lambda j: y_[:, j * T:(j + 1) * T]
        projk = x_res.bitcast(BF16)
        SE = [scr[:, 0:T], scr[:, 2 * T:3 * T]]
        SB = [scr[:, T:2 * T], scr[:, 3 * T:4 * T]]
        SC4 = scr[:, 4 * T:5 * T]

        em.dma("y", spill.rearrange("(j p) t -> p j t", p=128),
               x_res.rearrange("p (j t) -> p j t", j=NCH), "spl")
        lnw = lambda j: 224 + (j // 2) * 4 + (j % 2) * 2 + 2
        ln_stats(x_res, x_cols, lnw=lnw)
        x3 = x_res.rearrange("p (j t) -> p j t", j=NCH)
        ivl = invb[:, T - 1:T].broadcast_to([128, NCH])
        nml = nm2b[:, T - 1:T].broadcast_to([128, NCH])
        iv0 = invb[:, 0:1].broadcast_to([128, NCH])
        nm0 = nm2b[:, 0:1].broadcast_to([128, NCH])
        em.gemit("v", lambda v: v.tensor_tensor(
            halo_o[:, :], x3[:, :, T - 1], ivl, ALU.mult))
        em.emit("v", lambda v: v.tensor_tensor(
            halo_o[:, :], halo_o[:, :], nml, ALU.add))
        em.gemit("v", lambda v: v.tensor_tensor(
            z0[:, :], x3[:, :, 0], iv0, ALU.mult))
        em.emit("v", lambda v: v.tensor_tensor(
            z0[:, :], z0[:, :], nm0, ALU.add), inc="mixz")
        em.wait("y", "mixz")
        em.dma("y", cc_h_in[:, :], halo_o[:, :], "chi")
        em.wait("g", "chi")
        em.emit_cc(lambda g: g.collective_compute(
            "AllGather", ALU.bypass, replica_groups=RG,
            ins=[cc_h_in.opt()], outs=[cc_h_out.opt()]), "agh")
        em.wait("y", "agh", 1)
        em.dma("y", scr[:, 2 * T:2 * T + 8 * NCH].rearrange(
                   "p (r c) -> p r c", r=8),
               cc_h_out.rearrange("(r p) c -> p r c", p=128), "cho")
        em.wait("v", "spl")
        em.wait("g", "spl")
        zb = tmp.bitcast(BF16)[:, 0:T]
        db = tmp.bitcast(BF16)[:, T:2 * T]
        for j in range(NCH):
            em.emit("v", lambda v, j=j: v.tensor_tensor(
                SC4[:, :], x_res[:, x_cols(j)], invb[:, :], ALU.mult))
            em.emit("v", lambda v, j=j: v.tensor_tensor(
                zb, SC4[:, :], nm2b[:, :], ALU.add))
            em.emit("v", lambda v: v.tensor_tensor(
                db[0:128, 0:T - 1], zb[0:128, 1:T], zb[0:128, 0:T - 1],
                ALU.subtract))
            for tm, dst in (("tmk", xk_cols), ("tmv", xv_cols), ("tmr", xr_cols)):
                em.emit("v", lambda v, j=j, tm=tm, dst=dst: v.scalar_tensor_tensor(
                    dst(j)[:, 1:T], db[0:128, 0:T - 1], vecs[tm][:, j:j + 1],
                    zb[0:128, 0:T - 1], ALU.mult, ALU.add),
                    inc=("mixv" if tm == "tmr" else None))
        em.wait("v", "cho")
        em.gemit("v", lambda v: v.memset(halo_i[:, :], 0.0))
        for r in range(8):
            em.gemit("v", lambda v, r=r: v.scalar_tensor_tensor(
                halo_i[:, :], scr[:, 2 * T + r * NCH:2 * T + (r + 1) * NCH],
                selt[:, r:r + 1], halo_i[:, :], ALU.mult, ALU.add))
        for tm, om, dst in (("tmk", omk, xk_cols), ("tmv", omv, xv_cols),
                            ("tmr", omr, xr_cols)):
            em.emit("v", lambda v, om=om: v.tensor_tensor(
                tmp[:, 128:128 + NCH], halo_i[:, :], om[:, :], ALU.mult))
            em.gemit("v", lambda v, tm=tm: v.tensor_tensor(
                tmp[:, 160:160 + NCH], z0[:, :], vecs[tm][:, :], ALU.mult))
            em.gemit("v", lambda v: v.tensor_tensor(
                tmp[:, 192:192 + NCH], tmp[:, 128:128 + NCH],
                tmp[:, 160:160 + NCH], ALU.add))
            for j in range(NCH):
                em.emit("v", lambda v, j=j, dst=dst: v.tensor_copy(
                    dst(j)[:, 0:1], tmp[:, 192 + j:193 + j]),
                    inc=("mixf" if j == NCH - 1 else None))
        em.wait("v", "mmpm", nst["pm_w"])
        em.wait("v", "evpm", nst["pm_r"])

        pk_cols = lambda b: slice(16384 + b * (NCH * MBLK),
                                  16384 + (b + 1) * (NCH * MBLK))
        pv_cols = lambda b: slice(16384 + (2 + b) * (NCH * MBLK),
                                  16384 + (3 + b) * (NCH * MBLK))
        pr_cols = lambda b: slice(b * (NCH * MBLK), (b + 1) * (NCH * MBLK))

        def load_proj(blk):
            b = blk % 2
            if blk >= 2:
                em.wait("g", "mmr", 2 * (blk - 1))
            if blk == 0:
                em.wait("g", "spl")
            for w, cols, buf, mw in ((di["wkT"], pk_cols, projk, 10 + 2 * b),
                                     (di["wvT"], pv_cols, projk, 14 + 2 * b),
                                     (di["wrT"], pr_cols, wob, None)):
                if blk < 2 and mw is not None:
                    em.wait("g", "mixv", mw)
                src_ = w[:, blk * MBLK:(blk + 1) * MBLK].rearrange(
                    "(j p) m -> p j m", p=128)
                em.dma("g", buf[:, cols(b)].rearrange("p (j m) -> p j m", j=NCH),
                       src_, "wpj")
        load_proj(0)
        load_proj(1)

        def emit_tensor(dj):
            blk, dc = divmod(dj, 2)
            b = blk % 2
            if dj % 2 == 0 and 2 <= dj and blk + 1 < 8:
                load_proj(blk + 1)
            em.wait("t", "wpj", 16 * 3 * (blk + 1))
            for s in range(6):
                em.wait("t", f"evsl{s}", slu[s])
                slu[s] += 1
            for buf, cols_, xc, mmn, P_ in ((projk, pk_cols, xk_cols, "mmk", PA),
                                            (projk, pv_cols, xv_cols, "mmv", PB),
                                            (wob, pr_cols, xr_cols, "mmr", PC)):
                for half in range(2):
                    hs = slice(half * 512, (half + 1) * 512)
                    for j in range(NCH):
                        em.emit("t", lambda t, b=b, dc=dc, j=j, hs=hs, buf=buf,
                                cols_=cols_, xc=xc, P_=P_: t.matmul(
                            P_[:, hs],
                            buf[:, cols_(b)][:, j * MBLK + dc * 128:
                                             j * MBLK + (dc + 1) * 128],
                            xc(j)[:, hs],
                            start=(j == 0), stop=(j == NCH - 1)),
                            inc=(mmn if (j == NCH - 1 and half == 1) else None))

        def emit_pre(dj):
            p = dj % 2
            e_t, b_t = SE[p], SB[p]
            so = 4 * p
            dbc = dec[:, dj:dj + 1].broadcast_to([128, T])
            em.wait("s", "mmk", dj + 1)
            if dj >= 2:
                em.wait("s", "sfree", dj - 1)
            em.emit("s", lambda s, e_t=e_t: s.activation(
                e_t[:, 0:512], PA[:, 0:512], AF.Exp), inc="evsl0")
            em.emit("s", lambda s, e_t=e_t: s.activation(
                e_t[:, 512:1024], PA[:, 512:1024], AF.Exp), inc="evsl1")
            em.wait("s", "mmr", dj + 1)
            em.emit("s", lambda s: s.activation(
                PC[:, 0:512], PC[:, 0:512], AF.Exp, scale=-1.0))
            em.emit("s", lambda s: s.activation(
                PC[:, 512:1024], PC[:, 512:1024], AF.Exp, scale=-1.0))
            em.emit("s", lambda s: s.activation(
                PC[:, 0:512], PC[:, 0:512], AF.Ln, bias=1.0))
            em.emit("s", lambda s: s.activation(
                PC[:, 512:1024], PC[:, 512:1024], AF.Ln, bias=1.0))
            em.emit("s", lambda s, dj=dj: s.activation(
                y_cols(dj)[:, 0:512], PC[:, 0:512], AF.Exp, scale=-1.0),
                inc="evsl4")
            em.emit("s", lambda s, dj=dj: s.activation(
                y_cols(dj)[:, 512:1024], PC[:, 512:1024], AF.Exp, scale=-1.0),
                inc="evsl5")
            em.wait("v", "evsl1", slu[1])
            em.wait("v", "mmv", dj + 1)
            em.emit("v", lambda v, e_t=e_t, b_t=b_t: v.tensor_tensor(
                b_t[:, 0:512], e_t[:, 0:512], PB[:, 0:512], ALU.mult),
                inc="evsl2")
            em.emit("v", lambda v, e_t=e_t, b_t=b_t: v.tensor_tensor(
                b_t[:, 512:1024], e_t[:, 512:1024], PB[:, 512:1024], ALU.mult),
                inc="evsl3")
            if dj >= 2:
                em.wait("v", "cin", 16 * (dj - 1))
            em.gemit("v", lambda v, dbc=dbc, b_t=b_t: v.tensor_tensor_scan(
                PM[:, :], dbc, b_t[:, :], 0.0, ALU.mult, ALU.add))
            em.gemit("v", lambda v, so=so: v.tensor_copy(
                sumt[:, so:so + 1], PM[:, T - 1:T]))
            em.gemit("v", lambda v, dbc=dbc, e_t=e_t: v.tensor_tensor_scan(
                PM[:, :], dbc, e_t[:, :], 0.0, ALU.mult, ALU.add))
            em.emit("v", lambda v, so=so: v.tensor_copy(
                sumt[:, so + 1:so + 2], PM[:, T - 1:T]), inc="sumv")
            em.wait("y", "sumv", dj + 1)
            em.dma("y", cc_s_in[dj][:, :], sumt[:, so:so + 2], "cin")
            em.wait("g", "cin", 16 * (dj + 1))
            em.emit_cc(lambda g, dj=dj: g.collective_compute(
                "AllGather", ALU.bypass, replica_groups=RG,
                ins=[cc_s_in[dj].opt()], outs=[cc_s_out[dj].opt()]), f"ag{dj}")
            em.wait("y", f"ag{dj}", 1)
            em.wait("y", "rsdone", max(0, dj - 1))
            dbase = 48 * (dj % 2)
            em.dma("y", dsc[:, dbase:dbase + 16].rearrange("p (r c) -> p r c", c=2),
                   cc_s_out[dj].rearrange("(r p) c -> p r c", p=128), "bdma")

        def emit_post(dj):
            p = dj % 2
            e_t, b_t = SE[p], SB[p]
            so = 4 * p
            dbc = dec[:, dj:dj + 1].broadcast_to([128, T])
            d1b = d1024[:, dj:dj + 1].broadcast_to([128, 8])
            dbase = 48 * (dj % 2)
            bnum_r = dsc[:, dbase:dbase + 16].rearrange(
                "p (r c) -> p c r", c=2)[:, 0, :]
            bden_r = dsc[:, dbase:dbase + 16].rearrange(
                "p (r c) -> p c r", c=2)[:, 1, :]
            em.wait("v", "bdma", 16 * (dj + 1))
            em.gemit("v", lambda v, d1b=d1b, bnum_r=bnum_r, dbase=dbase:
                     v.tensor_tensor_scan(
                dsc[:, dbase + 24:dbase + 32], d1b, bnum_r, 0.0,
                ALU.mult, ALU.add))
            em.gemit("v", lambda v, dbase=dbase: v.tensor_tensor(
                dsc[:, dbase + 32:dbase + 40], dsc[:, dbase + 24:dbase + 32],
                selt[:, :], ALU.mult))
            em.gemit("v", lambda v, so=so, dbase=dbase: v.reduce_sum(
                sumt[:, so + 2:so + 3], dsc[:, dbase + 32:dbase + 40], axis=AXX))
            em.gemit("v", lambda v, d1b=d1b, bden_r=bden_r, dbase=dbase:
                     v.tensor_tensor_scan(
                dsc[:, dbase + 24:dbase + 32], d1b, bden_r, 0.0,
                ALU.mult, ALU.add))
            em.gemit("v", lambda v, dbase=dbase: v.tensor_tensor(
                dsc[:, dbase + 32:dbase + 40], dsc[:, dbase + 24:dbase + 32],
                selt[:, :], ALU.mult))
            em.gemit("v", lambda v, so=so, dbase=dbase: v.reduce_sum(
                sumt[:, so + 3:so + 4], dsc[:, dbase + 32:dbase + 40], axis=AXX))
            em.emit("v", lambda v: v.engine_nop(), inc="sden")
            em.emit("v", lambda v: v.engine_nop(), inc="rsdone")
            eub = eu[:, dj:dj + 1].broadcast_to([128, T])
            em.emit("v", lambda v, b_t=b_t, eub=eub: v.tensor_tensor(
                tmp[:, :], b_t[:, :], eub, ALU.mult))
            em.emit("v", lambda v, e_t=e_t, eub=eub: v.tensor_tensor(
                SC4[:, :], e_t[:, :], eub, ALU.mult))
            em.gemit("v", lambda v, dbc=dbc, b_t=b_t, so=so: v.tensor_tensor_scan(
                b_t[:, :], dbc, b_t[:, :], sumt[:, so + 2:so + 3],
                ALU.mult, ALU.add))
            em.gemit("v", lambda v, dbc=dbc, e_t=e_t, so=so: v.tensor_tensor_scan(
                e_t[:, :], dbc, e_t[:, :], sumt[:, so + 3:so + 4],
                ALU.mult, ALU.add))
            em.emit("v", lambda v, b_t=b_t: v.tensor_tensor(
                tmp[:, 1:T], tmp[:, 1:T], b_t[:, 0:T - 1], ALU.add))
            em.emit("v", lambda v, e_t=e_t: v.tensor_tensor(
                SC4[:, 1:T], SC4[:, 1:T], e_t[:, 0:T - 1], ALU.add))
            em.gemit("v", lambda v, so=so: v.tensor_tensor(
                tmp[:, 0:1], tmp[:, 0:1], sumt[:, so + 2:so + 3], ALU.add))
            em.gemit("v", lambda v, so=so: v.tensor_tensor(
                SC4[:, 0:1], SC4[:, 0:1], sumt[:, so + 3:so + 4], ALU.add))
            em.emit("v", lambda v: v.engine_nop(), inc="dden")
            em.wait("s", "dden", dj + 1)
            em.emit("s", lambda s: s.activation(SC4[:, :], SC4[:, :], AF.Ln))
            em.emit("s", lambda s, e_t=e_t: s.activation(
                e_t[:, :], SC4[:, :], AF.Exp, scale=-1.0), inc="srec")
            em.wait("v", "srec", dj + 1)
            em.emit("v", lambda v, e_t=e_t: v.tensor_tensor(
                tmp[:, :], tmp[:, :], e_t[:, :], ALU.mult))
            em.wait("v", "evsl5", slu[5])
            em.emit("v", lambda v, dj=dj: v.tensor_tensor(
                y_cols(dj), tmp[:, :], y_cols(dj), ALU.mult), inc="sfree")

        em.wait("t", "mixf")
        for dj in range(NCH):
            emit_tensor(dj)
            emit_pre(dj)
            if dj >= 1:
                emit_post(dj - 1)
        emit_post(NCH - 1)
        nst["pm_r"] += 1
        em.emit("v", lambda v: v.engine_nop(), inc="evpm")

        em.wait("y", "mmr", 16)
        em.dma("y", x_res.rearrange("p (j t) -> p j t", j=NCH),
               spill.rearrange("(j p) t -> p j t", p=128), "rld")

        def load_wo2(blk):
            b = blk % 2
            if blk >= 2:
                em.wait("g", "mmwo", 2 * (blk - 1))
            else:
                em.wait("g", "mmr", 16)
            src_ = di["woT"][:, blk * MBLK:(blk + 1) * MBLK].rearrange(
                "(j p) m -> p j m", p=128)
            em.dma("g", wob[:, pr_cols(b)].rearrange("p (j m) -> p j m", j=NCH),
                   src_, "wwo")
        load_wo2(0)
        n_wo = 0
        for blk in range(8):
            b = blk % 2
            if blk + 1 < 8:
                load_wo2(blk + 1)
            em.wait("t", "wwo", 16 * (blk + 1))
            for cc_ in range(2):
                ci = blk * 2 + cc_
                em.wait("t", "evsl0", slu[0])
                em.wait("t", "evsl1", slu[1])
                slu[0] += 1; slu[1] += 1
                for half in range(2):
                    hs = slice(half * 512, (half + 1) * 512)
                    for j in range(NCH):
                        if n_wo == 0 and half == 0:
                            em.wait("t", "sfree", j + 1)
                        em.emit("t", lambda t, b=b, cc_=cc_, j=j, hs=hs: t.matmul(
                            PA[:, hs],
                            wob[:, pr_cols(b)][:, j * MBLK + cc_ * 128:
                                               j * MBLK + (cc_ + 1) * 128],
                            y_cols(j)[:, hs],
                            start=(j == 0), stop=(j == NCH - 1)),
                            inc=("mmwo" if (j == NCH - 1 and half == 1) else None))
                n_wo += 1
                em.wait("v", "mmwo", n_wo)
                if ci == 0:
                    em.wait("v", "rld")
                em.emit("v", lambda v, ci=ci: v.tensor_tensor(
                    x_res[:, ci * T:(ci + 1) * T][:, 0:512], PA[:, 0:512],
                    x_res[:, ci * T:(ci + 1) * T][:, 0:512], ALU.add),
                    inc="evsl0")
                em.emit("v", lambda v, ci=ci: v.tensor_tensor(
                    x_res[:, ci * T:(ci + 1) * T][:, 512:1024], PA[:, 512:1024],
                    x_res[:, ci * T:(ci + 1) * T][:, 512:1024], ALU.add),
                    inc="evsl1")
                em.emit("v", lambda v: v.engine_nop(), inc="evwo")
        em.wait("v", "evwo", n_wo)
        em.wait("y", "evwo", n_wo)
        em.wait("v", "mmwo", n_wo)

    xn_cols = lambda j: xn[:, j * T:(j + 1) * T]
    x_cols = lambda j: slice(j * T, (j + 1) * T)

    ln_stats(x_res, x_cols)
    if upto in ("ln1",):
        ln_apply(xn_cols, x_res, x_cols)
    if dbg == "xn1":
        em.wait("g", "lnap")
        for j in range(NCH):
            em.dma("g", ddbg[j * 128:(j + 1) * 128, :], xn_cols(j), "dbo")
        em.wait("y", "dbo")
    if upto not in ("ln1",):
        gating(x_res, x_cols,
               mid_emit=lambda: ln_apply(xn_cols, x_res, x_cols))
        if dbg == "cw":
            em.wait("g", "evpm", nst["pm_r"])
            em.dma("g", ddbg[0:8, :], cwT[:, :], "dbo")
            em.wait("y", "dbo")
        ffn_pass(xn_cols, "a")
        if dbg == "ffn1":
            em.dma("y", ddbg.rearrange("(j p) t -> p j t", p=128),
                   x_res.rearrange("p (j t) -> p j t", j=NCH), "dbo")
            em.wait("y", "dbo")
    if upto in ("attn", "full"):
        attn_pass()
        if dbg == "attn":
            em.dma("y", ddbg.rearrange("(j p) t -> p j t", p=128),
                   x_res.rearrange("p (j t) -> p j t", j=NCH), "dbo")
            em.wait("y", "dbo")
    if upto == "full":
        em.wait("v", "mmr", 16)
        ln_stats(x_res, x_cols, lnw=lambda j: j + 1, lnsem="evwo")
        gating(x_res, x_cols,
               mid_emit=lambda: ln_apply(xn_cols, x_res, x_cols))
        ffn_pass(xn_cols, "b")

    if upto == "full":
        for j in range(NCH):
            em.wait("y", "evob", 224 + (j // 2) * 4 + (j % 2) * 2 + 2)
            em.dma("y", dout[j * 128:(j + 1) * 128, :], x_res[:, j * T:(j + 1) * T],
                   "so")
    else:
        em.dma("y", dout.rearrange("(j p) t -> p j t", p=128),
               x_res.rearrange("p (j t) -> p j t", j=NCH), "so")
    em.wait("y", "so")

    with nc.Block() as block:
        block.tensor(lambda t: [f(t) for f in em.q["t"]])
        block.vector(lambda v: [f(v) for f in em.q["v"]])
        block.scalar(lambda s: [f(s) for f in em.q["s"]])
        block.gpsimd(lambda gq: [f(gq) for f in em.q["g"]])
        block.sync(lambda sy: [f(sy) for f in em.q["y"]])
    stack.close()
    return nc


def make_inputs(ins, core):
    x = np.asarray(ins["L0_attn_x"])[0]
    xT = np.ascontiguousarray(x.T)
    cvec = lambda v: np.ascontiguousarray(
        np.asarray(v, np.float32).reshape(NCH, 128).T)
    e8 = np.zeros((E, E * 128), np.float32)
    for e in range(E):
        e8[e, e * 128:(e + 1) * 128] = 1.0
    sel = np.zeros((128, 8), np.float32)
    if core > 0:
        sel[:, core - 1] = 1.0
    hmask = np.full((128, 1), 0.0 if core == 0 else 1.0, np.float32)
    return {
        "x": np.ascontiguousarray(xT[:, core * T:(core + 1) * T]),
        "wkT": np.ascontiguousarray(np.asarray(ins["L0_attn_Wk"]).T),
        "wvT": np.ascontiguousarray(np.asarray(ins["L0_attn_Wv"]).T),
        "wrT": np.ascontiguousarray(np.asarray(ins["L0_attn_Wr"]).T),
        "woT": np.ascontiguousarray(np.asarray(ins["L0_attn_Wo"]).T),
        "wi0": np.ascontiguousarray(np.asarray(ins["L0_ffn_wi_0"])),
        "wi1": np.ascontiguousarray(np.asarray(ins["L0_ffn_wi_1"])),
        "wo2": np.ascontiguousarray(np.asarray(ins["L0_ffn_wo"])),
        "gk": np.ascontiguousarray(np.asarray(ins["L0_ffn_gate_kernel"])),
        "td": cvec(ins["L0_attn_time_decay"]),
        "tf": cvec(ins["L0_attn_time_first"]),
        "tmk": cvec(ins["L0_attn_time_mix_k"]),
        "tmv": cvec(ins["L0_attn_time_mix_v"]),
        "tmr": cvec(ins["L0_attn_time_mix_r"]),
        "e8": e8, "ident": np.eye(128, dtype=np.float32),
        "sel": sel, "hmask": hmask,
    }


from concourse.bass_utils import run_bass_kernel_spmd

_NC_CACHE = {}

def kernel(**inputs):
    if "nc" not in _NC_CACHE:
        _NC_CACHE["nc"] = build(upto="full", dbg=None)
    nc = _NC_CACHE["nc"]
    in_maps = [make_inputs(inputs, c) for c in range(8)]
    res = run_bass_kernel_spmd(nc, in_maps, core_ids=list(range(8)))
    out = np.concatenate([res.results[c]["out"].T for c in range(8)], axis=0)
    return np.ascontiguousarray(out, dtype=np.float32).reshape(1, 8192, 2048)


# version 21
# speedup vs baseline: 1.0057x; 1.0001x over previous
import contextlib
import numpy as np
import concourse.bass as bass
import concourse.mybir as mybir

F32 = mybir.dt.float32
F32R = mybir.dt.float32r
BF16 = mybir.dt.bfloat16
AF = mybir.ActivationFunctionType
ALU = mybir.AluOpType
AXX = mybir.AxisListType.X

D = 2048
M = 2048
E = 8
T = 1024
NCH = 16
MBLK = 256
NBLK = M // MBLK
EPS = 1e-5
RG = [[0, 1, 2, 3, 4, 5, 6, 7]]


class Em:
    def __init__(self, nc, stack):
        self.nc = nc
        self.stack = stack
        self.q = {k: [] for k in ("t", "v", "s", "g", "y")}
        self.sems = {}
        self.cnt = {}

    def _sem(self, name):
        if name not in self.sems:
            self.sems[name] = self.stack.enter_context(self.nc.semaphore(name))
            self.cnt[name] = 0
        return self.sems[name]

    def emit(self, eng, fn, inc=None, inc_n=1):
        sem_h = None
        if inc is not None:
            sem_h = self._sem(inc)
            self.cnt[inc] += inc_n
        def run(engine, fn=fn, sem_h=sem_h, inc_n=inc_n):
            ins = fn(engine)
            if sem_h is not None:
                ins.then_inc(sem_h, inc_n)
            return ins
        self.q[eng].append(run)

    def emit_cc(self, fn, inc):
        sem_h = self._sem(inc)
        self.cnt[inc] += 1
        def run(engine, fn=fn, sem_h=sem_h):
            return fn(engine).then_inc(sem_h)
        self.q["g"].append(run)

    def wait(self, eng, sem, value=None):
        v = self.cnt[sem] if value is None else value
        h = self._sem(sem)
        self.q[eng].append(lambda engine, h=h, v=v: engine.wait_ge(h, v))

    def val(self, sem):
        self._sem(sem)
        return self.cnt[sem]

    def gemit(self, eng, fn):
        self.emit(eng, fn, inc="chain_" + eng)
        self.wait(eng, "chain_" + eng)

    def dma(self, eng, out, in_, sem):
        self.emit(eng, lambda q, o=out, i=in_: q.dma_start(out=o, in_=i),
                  inc=sem, inc_n=16)


def build(upto="full", dbg=None):
    nc = bass.Bass("TRN2", target_bir_lowering=False)
    stack = contextlib.ExitStack()

    di = {}
    di["x"] = nc.dram_tensor("x", [D, T], F32, kind="ExternalInput").ap()
    for w in ("wkT", "wvT", "wrT", "woT"):
        di[w] = nc.dram_tensor(w, [D, D], F32, kind="ExternalInput").ap()
    di["wi0"] = nc.dram_tensor("wi0", [E, D, M], F32, kind="ExternalInput").ap()
    di["wi1"] = nc.dram_tensor("wi1", [E, D, M], F32, kind="ExternalInput").ap()
    di["wo2"] = nc.dram_tensor("wo2", [E, M, D], F32, kind="ExternalInput").ap()
    di["gk"] = nc.dram_tensor("gk", [D, E], F32, kind="ExternalInput").ap()
    for w in ("td", "tf", "tmk", "tmv", "tmr"):
        di[w] = nc.dram_tensor(w, [128, NCH], F32, kind="ExternalInput").ap()
    di["e8"] = nc.dram_tensor("e8", [E, E * 128], F32, kind="ExternalInput").ap()
    di["ident"] = nc.dram_tensor("ident", [128, 128], F32, kind="ExternalInput").ap()
    di["sel"] = nc.dram_tensor("sel", [128, 8], F32, kind="ExternalInput").ap()
    di["hmask"] = nc.dram_tensor("hmask", [128, 1], F32, kind="ExternalInput").ap()

    dout = nc.dram_tensor("out", [D, T], F32, kind="ExternalOutput").ap()
    ddbg = nc.dram_tensor("dbg", [D, T], F32, kind="ExternalOutput").ap() \
        if dbg is not None else None

    spill = nc.dram_tensor("spill", [D, T], F32, kind="Internal").ap()
    cc_h_in = nc.dram_tensor("cc_h_in", [128, NCH], F32, kind="Internal").ap()
    cc_h_out = nc.dram_tensor("cc_h_out", [8 * 128, NCH], F32, kind="Internal").ap()
    cc_s_in = [nc.dram_tensor(f"cc_s_in{j}", [128, 2], F32, kind="Internal").ap()
               for j in range(NCH)]
    cc_s_out = [nc.dram_tensor(f"cc_s_out{j}", [8 * 128, 2], F32, kind="Internal").ap()
                for j in range(NCH)]

    em = Em(nc, stack)
    sb = lambda name, shape, dtype: stack.enter_context(
        nc.sbuf_tensor(name, shape, dtype)).ap()
    ps = lambda name, shape: stack.enter_context(
        nc.psum_tensor(name, shape, F32)).ap()

    x_res = sb("x_res", [128, NCH * T], F32)
    ar2 = sb("ar2", [128, NCH * T // 2], F32)
    ar3 = sb("ar3", [128, NCH * T // 2], F32)
    ar4 = sb("ar4", [128, NCH * T // 2], F32)
    ar5 = sb("ar5", [128, NCH * T // 4], F32)
    scr = sb("scr", [128, 5 * T], F32)
    tmp = sb("tmpt", [128, T], F32)

    xn = ar2.bitcast(BF16)
    xrT = ar2.bitcast(BF16)
    g_ = ar3.bitcast(BF16)
    y_ = ar3.bitcast(BF16)
    wib = ar4.bitcast(BF16)
    xkT = ar4.bitcast(BF16)
    xvT = x_res.bitcast(BF16)
    wob = ar5.bitcast(BF16)

    vecs = {w: sb("v_" + w, [128, NCH], F32) for w in ("td", "tf", "tmk", "tmv", "tmr")}
    onev = sb("onev", [128, 1], F32)
    onevb = sb("onevb", [128, 1], BF16)
    ones1 = sb("ones1", [1, 128], F32)
    e8t = sb("e8t", [E, E * 128], BF16)
    ident = sb("identt", [128, 128], F32)
    selt = sb("selt", [128, 8], F32)
    hmaskt = sb("hmaskt", [128, 1], F32)
    gkf = sb("gkf", [128, NCH * E], F32)
    gksb = sb("gksb", [128, E], F32)
    invcol = sb("invcol", [128, 8], F32)
    nm2col = sb("nm2col", [128, 8], F32)
    cwT = sb("cwT", [E, T], BF16)
    invb = scr[:, 0:T]
    nm2b = scr[:, T:2 * T]
    cwb = scr[:, 2 * T:3 * T]
    mean = scr[0:1, 3 * T:4 * T]
    ex2 = scr[0:1, 4 * T:5 * T]
    var = ex2
    inv = ex2
    nm2 = mean
    halo_o = sb("halo_o", [128, NCH], F32)
    halo_i = sb("halo_i", [128, NCH], F32)
    z0 = sb("z0", [128, NCH], F32)
    eu = sb("eu", [128, NCH], F32)
    omk = sb("omk", [128, NCH], F32)
    omv = sb("omv", [128, NCH], F32)
    omr = sb("omr", [128, NCH], F32)
    dec = sb("dec", [128, NCH], F32)
    d1024 = sb("d1024", [128, NCH], F32)
    w_ = sb("w_", [128, NCH], F32)
    dsc = sb("dscr", [128, 96], F32)
    sumt = sb("sumt", [128, 8], F32)

    PA = ps("PA", [128, T])
    PB = ps("PB", [128, T])
    PC = ps("PC", [128, T])
    PM = ps("PM", [128, T])
    PST = [PA, PB, PC]
    SL = lambda s: PST[s // 2][:, (s % 2) * 512:(s % 2 + 1) * 512]
    slu = [0] * 6

    nst = {"pm_w": 0, "pm_r": 0}

    def pm_write_group(mm_emits):
        em.wait("t", "evpm", nst["pm_r"])
        for i, f in enumerate(mm_emits):
            em.emit("t", f, inc=("mmpm" if i == len(mm_emits) - 1 else None))
        nst["pm_w"] += 1

    def pm_read_chain(eng, fns):
        em.wait(eng, "mmpm", nst["pm_w"])
        for i, f in enumerate(fns):
            em.emit(eng, f, inc=("evpm" if i == len(fns) - 1 else None))
        nst["pm_r"] += 1
        return nst["pm_r"]

    for name in ("td", "tf", "tmk", "tmv", "tmr"):
        em.dma("y", vecs[name][:, :], di[name][:, :], "sx")
    em.dma("g", e8t[:, :], di["e8"][:, :], "wge")
    em.dma("y", ident[:, :], di["ident"][:, :], "sx")
    em.dma("y", selt[:, :], di["sel"][:, :], "sx")
    em.dma("y", hmaskt[:, :], di["hmask"][:, :], "sx")
    em.dma("y", gkf.rearrange("p (j e) -> p j e", j=NCH),
           di["gk"].rearrange("(j p) e -> p j e", p=128), "wg")
    em.dma("y", x_res.rearrange("p (j t) -> p j t", j=NCH),
           di["x"].rearrange("(j p) t -> p j t", p=128), "sx")
    for e_ in ("v", "s", "t"):
        em.wait(e_, "sx")
    em.wait("t", "wg")
    em.wait("t", "wge")
    em.wait("v", "wg")
    em.emit("v", lambda v: v.memset(onev[:, :], 1.0))
    em.emit("v", lambda v: v.memset(onevb[:, :], 1.0))
    em.emit("v", lambda v: v.memset(ones1[:, :], 1.0), inc="vc")
    em.wait("t", "vc")
    em.emit("s", lambda s: s.activation(w_[:, :], vecs["td"][:, :], AF.Exp), inc="sc")
    em.wait("v", "sc")
    em.emit("v", lambda v: v.tensor_scalar(w_[:, :], w_[:, :], -1.0, None, ALU.mult),
            inc="vc")
    em.wait("s", "vc")
    em.emit("s", lambda s: s.activation(dec[:, :], w_[:, :], AF.Exp))
    em.emit("s", lambda s: s.activation(d1024[:, :], w_[:, :], AF.Exp, scale=1024.0))
    em.emit("s", lambda s: s.activation(eu[:, :], vecs["tf"][:, :], AF.Exp), inc="sc")
    for nm, om in (("tmk", omk), ("tmv", omv), ("tmr", omr)):
        em.emit("v", lambda v, nm=nm, om=om: v.tensor_scalar(
            om[:, :], vecs[nm][:, :], -1.0, 1.0, ALU.mult, ALU.add), inc="vc")

    mms = [lambda t, j=j: t.matmul(PM[0:1, 0:E], onev[:, :],
                                   gkf[:, j * E:(j + 1) * E],
                                   start=(j == 0), stop=(j == NCH - 1))
           for j in range(NCH)]
    pm_write_group(mms)
    pm_read_chain("s", [lambda s: s.activation(tmp[0:1, 0:E], PM[0:1, 0:E],
                                               AF.Copy)])
    em.wait("t", "evpm", nst["pm_r"])
    pm_write_group([lambda t: t.matmul(PM[:, 0:E], ones1[:, :], tmp[0:1, 0:E],
                                       start=True, stop=True)])
    pm_read_chain("s", [lambda s: s.activation(gksb[:, :], PM[:, 0:E], AF.Copy)])

    def ln_stats(src_ap, cols, lnw=None, lnsem="evoa"):
        em.wait("t", "evpm", nst["pm_r"])
        tbm = tmp.bitcast(BF16)
        nsq = 0
        for half in range(2):
            for j in range(NCH):
                sl = slice((nsq % 4) * 512, (nsq % 4) * 512 + 512)
                if nsq >= 4:
                    em.wait("s", "mmsq", em.val("mmsq") - 3)
                if lnw is not None and half == 0:
                    em.wait("s", lnsem, lnw(j))
                em.emit("s", lambda s, j=j, half=half, sl=sl: s.activation(
                    tbm[:, sl], src_ap[:, cols(j)][:, half * 512:(half + 1) * 512],
                    AF.Copy), inc="sq")
                em.wait("t", "sq")
                em.emit("t", lambda t, j=j, half=half, sl=sl: t.matmul(
                    PM[0:1, half * 512:(half + 1) * 512],
                    onevb[:, :], tbm[:, sl],
                    start=(j == 0), stop=(j == NCH - 1)),
                    inc="mmsq")
                if half == 1 and j == NCH - 1:
                    em.emit("t", lambda t: t.matmul(
                        PC[0:1, 256:512], onevb[:, :],
                        tbm[:, 0:256], start=True, stop=True), inc="mmpm")
                nsq += 1
        nst["pm_w"] += 1
        pm_read_chain("s", [
            lambda s: s.activation(mean[0:1, 0:512], PM[0:1, 0:512],
                                   AF.Copy, scale=1.0 / D),
            lambda s: s.activation(mean[0:1, 512:1024], PM[0:1, 512:1024],
                                   AF.Copy, scale=1.0 / D)])
        em.wait("t", "evpm", nst["pm_r"])
        tb = tmp.bitcast(BF16)
        nsq = 0
        for half in range(2):
            for j in range(NCH):
                sl = slice((nsq % 4) * 512, (nsq % 4) * 512 + 512)
                if nsq >= 4:
                    em.wait("s", "mmsq", em.val("mmsq") - 3)
                if lnw is not None and half == 0:
                    em.wait("s", lnsem, lnw(j))
                em.emit("s", lambda s, j=j, half=half, sl=sl: s.activation(
                    tb[:, sl], src_ap[:, cols(j)][:, half * 512:(half + 1) * 512],
                    AF.Square), inc="sq")
                em.wait("t", "sq")
                fin = (half == 1 and j == NCH - 1)
                em.emit("t", lambda t, j=j, half=half, sl=sl: t.matmul(
                    PM[0:1, half * 512:(half + 1) * 512],
                    onevb[:, :], tb[:, sl],
                    start=(j == 0), stop=(j == NCH - 1)), inc="mmsq")
                if fin:
                    em.emit("t", lambda t: t.matmul(
                        PC[0:1, 0:256], onevb[:, :],
                        tb[:, 0:256], start=True, stop=True),
                        inc="mmpm")
                nsq += 1
        nst["pm_w"] += 1
        pm_read_chain("v", [
            lambda v: v.tensor_scalar(ex2[0:1, 0:512], PM[0:1, 0:512],
                                      1.0 / D, None, ALU.mult),
            lambda v: v.tensor_scalar(ex2[0:1, 512:1024], PM[0:1, 512:1024],
                                      1.0 / D, None, ALU.mult)])
        em.gemit("v", lambda v: v.tensor_tensor(tmp[0:1, 0:T], mean, mean, ALU.mult))
        em.gemit("v", lambda v: v.tensor_tensor(var, ex2, tmp[0:1, 0:T], ALU.subtract))
        em.emit("v", lambda v: v.tensor_scalar(var, var, EPS, None, ALU.add), inc="vc")
        em.wait("s", "vc")
        em.gemit("s", lambda s: s.activation(tmp[0:1, 0:T], var, AF.Ln))
        em.emit("s", lambda s: s.activation(inv, tmp[0:1, 0:T], AF.Exp,
                                            scale=-0.5), inc="sc")
        em.wait("v", "sc")
        em.gemit("v", lambda v: v.tensor_tensor(nm2, mean, inv, ALU.mult))
        em.emit("v", lambda v: v.tensor_scalar(nm2, nm2, -1.0, None, ALU.mult),
                inc="vc")
        em.wait("t", "vc")
        for vsrc, vdst in ((inv, invb), (nm2, nm2b)):
            mms = []
            for half in range(2):
                mms.append(lambda t, half=half, vsrc=vsrc: t.matmul(
                    PM[:, half * 512:(half + 1) * 512],
                    ones1[:, :],
                    vsrc[0:1, half * 512:(half + 1) * 512],
                    start=True, stop=True))
            pm_write_group(mms)
            pm_read_chain("s", [
                lambda s, vdst=vdst: s.activation(vdst[:, 0:512], PM[:, 0:512], AF.Copy),
                lambda s, vdst=vdst: s.activation(vdst[:, 512:1024], PM[:, 512:1024],
                                                  AF.Copy)])
        em.wait("v", "evpm", nst["pm_r"])

    def ln_apply(dst_cols, src_ap, cols, save_cols=False):
        for j in range(NCH):
            em.emit("v", lambda v, j=j: v.tensor_tensor(
                tmp[:, :], src_ap[:, cols(j)], invb[:, :], ALU.mult))
            em.emit("v", lambda v, j=j: v.tensor_tensor(
                dst_cols(j), tmp[:, :], nm2b[:, :], ALU.add), inc="lnap")
            if save_cols:
                em.emit("v", lambda v, j=j: v.tensor_tensor(
                    z0[:, j:j + 1], tmp[:, 0:1], nm2b[:, 0:1], ALU.add))
                em.emit("v", lambda v, j=j: v.tensor_tensor(
                    halo_o[:, j:j + 1], tmp[:, T - 1:T], nm2b[:, T - 1:T],
                    ALU.add), inc="lnap")

    def gating(src_ap, cols, mid_emit=None):
        X = tmp
        mms = []
        for tc in range(8):
            for j in range(NCH):
                mms.append(lambda t, tc=tc, j=j: t.matmul(
                    PM[:, tc * 8:(tc + 1) * 8],
                    src_ap[:, cols(j)][:, tc * 128:(tc + 1) * 128],
                    gkf[:, j * E:(j + 1) * E],
                    start=(j == 0), stop=(j == NCH - 1)))
        for tc in range(8):
            mms.append(lambda t, tc=tc: t.matmul(
                PM[:, 512 + tc:513 + tc], inv[0:1, tc * 128:(tc + 1) * 128],
                ones1[0:1, 0:1], start=True, stop=True))
            mms.append(lambda t, tc=tc: t.matmul(
                PM[:, 520 + tc:521 + tc], nm2[0:1, tc * 128:(tc + 1) * 128],
                ones1[0:1, 0:1], start=True, stop=True))
        pm_write_group(mms)
        if mid_emit is not None:
            mid_emit()
        em.wait("s", "mmpm", nst["pm_w"])
        em.gemit("s", lambda s: s.activation(invcol[:, 0:8], PM[:, 512:520],
                                             AF.Copy))
        em.emit("s", lambda s: s.activation(nm2col[:, 0:8], PM[:, 520:528],
                                            AF.Copy), inc="scst")
        r3 = lambda ap: ap.rearrange("p (tc e) -> p tc e", e=8)
        bc = lambda ap: ap.rearrange("p (tc one) -> p tc one", one=1).broadcast_to(
            [128, 8, 8])
        l_all = X[:, 0:64]; t2 = X[:, 64:128]; m1 = X[:, 128:136]
        m2 = X[:, 136:144]; w1 = X[:, 144:152]; w2 = X[:, 152:160]
        mk1 = X[:, 160:224]; l2 = X[:, 224:288]; mk2 = X[:, 288:352]
        u1 = X[:, 352:416]; cw_all = X[:, 416:480]
        gksb3 = gksb.rearrange("p (one e) -> p one e", one=1).broadcast_to(
            [128, 8, 8])
        em.wait("v", "mmpm", nst["pm_w"])
        em.wait("v", "scst")
        em.gemit("v", lambda v: v.tensor_tensor(r3(t2), gksb3, bc(nm2col[:, 0:8]),
                                                ALU.mult))
        em.gemit("v", lambda v: v.tensor_tensor(r3(l_all), r3(PM[:, 0:64]),
                                                bc(invcol[:, 0:8]), ALU.mult))
        em.emit("v", lambda v: v.tensor_tensor(l_all, l_all, t2, ALU.add),
                inc="evpm")
        em.gemit("v", lambda v: v.engine_nop())
        nst["pm_r"] += 1
        em.gemit("v", lambda v: v.tensor_reduce(
            m1.rearrange("p (tc one) -> p tc one", one=1), r3(l_all),
            axis=AXX, op=ALU.max))
        em.gemit("v", lambda v: v.tensor_tensor(r3(mk1), r3(l_all), bc(m1),
                                                ALU.is_equal))
        em.gemit("v", lambda v: v.scalar_tensor_tensor(l2, mk1, -1e30, l_all,
                                                       ALU.mult, ALU.add))
        em.gemit("v", lambda v: v.tensor_reduce(
            m2.rearrange("p (tc one) -> p tc one", one=1), r3(l2),
            axis=AXX, op=ALU.max))
        em.gemit("v", lambda v: v.tensor_tensor(r3(mk2), r3(l2), bc(m2),
                                                ALU.is_equal))
        em.emit("v", lambda v: v.tensor_tensor(w1, m1, m2, ALU.subtract),
                inc="vc")
        em.wait("s", "vc")
        em.emit("s", lambda s: s.activation(w1, w1, AF.Sigmoid), inc="scst")
        em.wait("v", "scst")
        em.gemit("v", lambda v: v.tensor_scalar(w2, w1, -1.0, 1.0, ALU.mult,
                                                ALU.add))
        em.gemit("v", lambda v: v.tensor_tensor(r3(u1), r3(mk1), bc(w1),
                                                ALU.mult))
        em.gemit("v", lambda v: v.tensor_tensor(r3(cw_all), r3(mk2), bc(w2),
                                                ALU.mult))
        em.emit("v", lambda v: v.tensor_tensor(cw_all, cw_all, u1, ALU.add),
                inc="vc")
        em.wait("t", "vc")
        for grp in range(2):
            mms = []
            for q in range(4):
                tc = grp * 4 + q
                mms.append(lambda t, tc=tc, q=q, grp=grp: t.transpose(
                    PM[0:8, grp * 512 + q * 128:grp * 512 + (q + 1) * 128],
                    cw_all[:, tc * 8:(tc + 1) * 8], ident[:, :]))
            pm_write_group(mms)
            pm_read_chain("s", [lambda s, grp=grp: s.activation(
                cwT[:, grp * 512:(grp + 1) * 512],
                PM[0:8, grp * 512:(grp + 1) * 512], AF.Copy)])

    def cw_broadcast(e, scalar_wait):
        mms = []
        for half in range(2):
            mms.append(lambda t, e=e, half=half: t.matmul(
                PM[:, half * 512:(half + 1) * 512],
                e8t[:, e * 128:(e + 1) * 128],
                cwT[:, half * 512:(half + 1) * 512],
                start=True, stop=True))
        pm_write_group(mms)
        if scalar_wait is not None:
            em.wait("s", scalar_wait[0], scalar_wait[1])
        return pm_read_chain("s", [
            lambda s: s.activation(cwb[:, 0:512], PM[:, 0:512], AF.Copy),
            lambda s: s.activation(cwb[:, 512:1024], PM[:, 512:1024], AF.Copy)])

    def ffn_pass(xn_cols, tag):
        wsem, wosem = "wf" + tag, "wq" + tag
        mmh, mmo = "mmh" + tag, "mmo" + tag
        evg, evo = "evg" + tag, "evo" + tag
        wi_cols = lambda b, m_: slice((b * 2 + m_) * (NCH * MBLK),
                                      (b * 2 + m_ + 1) * (NCH * MBLK))
        wo_cols = lambda b: slice(b * (NCH * MBLK), (b + 1) * (NCH * MBLK))

        def load_wi(nuse):
            e, blk = divmod(nuse, NBLK)
            b = nuse % 2
            if nuse == 0 and tag == "b":
                em.wait("g", "mmr", 16)
            if nuse >= 2:
                em.wait("g", mmh, 4 * (nuse - 1))
            for m_, w in ((0, di["wi0"]), (1, di["wi1"])):
                src_ = w[e, :, blk * MBLK:(blk + 1) * MBLK].rearrange(
                    "(j p) m -> p j m", p=128)
                dst = wib[:, wi_cols(b, m_)].rearrange("p (j m) -> p j m", j=NCH)
                em.dma("g", dst, src_, wsem)

        def load_wo(nuse):
            e, blk = divmod(nuse, NBLK)
            b = nuse % 2
            if nuse == 0 and tag == "b":
                em.wait("g", "mmwo", 16)
            if nuse >= 2:
                em.wait("g", mmo, 4 * (nuse - 1))
            src_ = di["wo2"][e, :, blk * MBLK:(blk + 1) * MBLK].rearrange(
                "(j p) m -> p j m", p=128)
            dst = wob[:, wo_cols(b)].rearrange("p (j m) -> p j m", j=NCH)
            em.dma("g", dst, src_, wosem)

        load_wi(0)
        load_wo(0)
        em.wait("t", "lnap")
        sil_gate = em.val("vc")
        n_h = 0
        n_o = 0
        n_g = 0
        for e in range(E):
            cwb_rdy = cw_broadcast(e, ((evg, 32 * e) if e > 0 else None))
            for blk in range(NBLK):
                nuse = e * NBLK + blk
                b = nuse % 2
                if nuse + 1 < E * NBLK:
                    load_wi(nuse + 1)
                em.wait("t", wsem, 16 * 2 * (nuse + 1))
                for mc in range(2):
                    mi = blk * 2 + mc
                    for half in range(2):
                        hs = slice(half * 512, (half + 1) * 512)
                        s0 = (2 * n_h) % 6
                        s1 = (2 * n_h + 1) % 6
                        em.wait("t", f"evsl{s0}", slu[s0])
                        em.wait("t", f"evsl{s1}", slu[s1])
                        slu[s0] += 1; slu[s1] += 1
                        for j in range(NCH):
                            em.emit("t", lambda t, b=b, mc=mc, j=j, hs=hs, s0=s0:
                                    t.matmul(SL(s0),
                                             wib[:, wi_cols(b, 0)][:, j * MBLK + mc * 128:
                                                                   j * MBLK + (mc + 1) * 128],
                                             xn_cols(j)[:, hs],
                                             start=(j == 0), stop=(j == NCH - 1)))
                        for j in range(NCH):
                            em.emit("t", lambda t, b=b, mc=mc, j=j, hs=hs, s1=s1:
                                    t.matmul(SL(s1),
                                             wib[:, wi_cols(b, 1)][:, j * MBLK + mc * 128:
                                                                   j * MBLK + (mc + 1) * 128],
                                             xn_cols(j)[:, hs],
                                             start=(j == 0), stop=(j == NCH - 1)),
                                    inc=(mmh if j == NCH - 1 else None))
                        n_h += 1
                        th = slice((n_h % 2) * 512, (n_h % 2) * 512 + 512)
                        em.wait("s", mmh, n_h)
                        if n_h == 1:
                            em.wait("s", "vc", sil_gate)
                        if n_g >= 2:
                            em.wait("s", evg, n_g - 1)
                        em.emit("s", lambda s, th=th, s0=s0: s.activation(
                            tmp[:, th], SL(s0), AF.Silu), inc=f"evsl{s0}")
                        em.wait("v", f"evsl{s0}", slu[s0])
                        if n_h == 1:
                            em.wait("v", "evpm", cwb_rdy)
                        em.emit("v", lambda v, th=th, s1=s1: v.tensor_tensor(
                            tmp[:, th], tmp[:, th], SL(s1), ALU.mult),
                            inc=f"evsl{s1}")
                        em.emit("v", lambda v, mi=mi, hs=hs, th=th: v.tensor_tensor(
                            g_[:, mi * T:(mi + 1) * T][:, hs], tmp[:, th],
                            cwb[:, hs], ALU.mult), inc=evg)
                        n_g += 1
            last_w = [32 * e + 2]
            def gwait(j, e=e, last_w=last_w):
                tgt = 32 * e + 2 * (j + 1)
                if tgt > last_w[0]:
                    em.wait("t", evg, tgt)
                    last_w[0] = tgt
            for blk in range(NBLK):
                nuse = e * NBLK + blk
                b = nuse % 2
                if nuse + 1 < E * NBLK:
                    load_wo(nuse + 1)
                em.wait("t", wosem, 16 * (nuse + 1))
                for dc in range(2):
                    dii = blk * 2 + dc
                    for half in range(2):
                        hs = slice(half * 512, (half + 1) * 512)
                        s0 = n_o % 6
                        em.wait("t", f"evsl{s0}", slu[s0])
                        slu[s0] += 1
                        for j in range(NCH):
                            gwait(j)
                            em.emit("t", lambda t, b=b, dc=dc, j=j, hs=hs, s0=s0:
                                    t.matmul(SL(s0),
                                             wob[:, wo_cols(b)][:, j * MBLK + dc * 128:
                                                                j * MBLK + (dc + 1) * 128],
                                             g_[:, j * T:(j + 1) * T][:, hs],
                                             start=(j == 0), stop=(j == NCH - 1)),
                                    inc=(mmo if j == NCH - 1 else None))
                        n_o += 1
                        em.wait("v", mmo, n_o)
                        em.emit("v", lambda v, dii=dii, hs=hs, s0=s0:
                                v.scalar_tensor_tensor(
                                    x_res[:, dii * T:(dii + 1) * T][:, hs], SL(s0),
                                    0.5, x_res[:, dii * T:(dii + 1) * T][:, hs],
                                    ALU.mult, ALU.add), inc=f"evsl{s0}")
                        em.emit("v", lambda v: v.engine_nop(), inc=evo)
        em.wait("y", evo, n_o)
        em.wait("g", evo, n_o)
        em.wait("v", evo, n_o)
        if tag != "a":
            em.wait("t", evo, n_o)
            em.wait("s", evo, n_o)

    def attn_pass():
        xk_cols = lambda j: xkT[:, j * T:(j + 1) * T]
        xv_cols = lambda j: xvT[:, j * T:(j + 1) * T]
        xr_cols = lambda j: xrT[:, j * T:(j + 1) * T]
        y_cols = lambda j: y_[:, j * T:(j + 1) * T]
        projk = x_res.bitcast(BF16)
        SE = [scr[:, 0:T], scr[:, 2 * T:3 * T]]
        SB = [scr[:, T:2 * T], scr[:, 3 * T:4 * T]]
        SC4 = scr[:, 4 * T:5 * T]

        em.dma("y", spill.rearrange("(j p) t -> p j t", p=128),
               x_res.rearrange("p (j t) -> p j t", j=NCH), "spl")
        lnw = lambda j: 224 + (j // 2) * 4 + (j % 2) * 2 + 2
        ln_stats(x_res, x_cols, lnw=lnw)
        x3 = x_res.rearrange("p (j t) -> p j t", j=NCH)
        ivl = invb[:, T - 1:T].broadcast_to([128, NCH])
        nml = nm2b[:, T - 1:T].broadcast_to([128, NCH])
        iv0 = invb[:, 0:1].broadcast_to([128, NCH])
        nm0 = nm2b[:, 0:1].broadcast_to([128, NCH])
        em.gemit("v", lambda v: v.tensor_tensor(
            halo_o[:, :], x3[:, :, T - 1], ivl, ALU.mult))
        em.emit("v", lambda v: v.tensor_tensor(
            halo_o[:, :], halo_o[:, :], nml, ALU.add))
        em.gemit("v", lambda v: v.tensor_tensor(
            z0[:, :], x3[:, :, 0], iv0, ALU.mult))
        em.emit("v", lambda v: v.tensor_tensor(
            z0[:, :], z0[:, :], nm0, ALU.add), inc="mixz")
        em.wait("y", "mixz")
        em.dma("y", cc_h_in[:, :], halo_o[:, :], "chi")
        em.wait("g", "chi")
        em.emit_cc(lambda g: g.collective_compute(
            "AllGather", ALU.bypass, replica_groups=RG,
            ins=[cc_h_in.opt()], outs=[cc_h_out.opt()]), "agh")
        em.wait("y", "agh", 1)
        em.dma("y", scr[:, 2 * T:2 * T + 8 * NCH].rearrange(
                   "p (r c) -> p r c", r=8),
               cc_h_out.rearrange("(r p) c -> p r c", p=128), "cho")
        em.wait("v", "spl")
        em.wait("g", "spl")
        zb = tmp.bitcast(BF16)[:, 0:T]
        db = tmp.bitcast(BF16)[:, T:2 * T]
        for j in range(NCH):
            em.emit("v", lambda v, j=j: v.tensor_tensor(
                SC4[:, :], x_res[:, x_cols(j)], invb[:, :], ALU.mult))
            em.emit("v", lambda v, j=j: v.tensor_tensor(
                zb, SC4[:, :], nm2b[:, :], ALU.add))
            em.emit("v", lambda v: v.tensor_tensor(
                db[0:128, 0:T - 1], zb[0:128, 1:T], zb[0:128, 0:T - 1],
                ALU.subtract))
            for tm, dst in (("tmk", xk_cols), ("tmv", xv_cols), ("tmr", xr_cols)):
                em.emit("v", lambda v, j=j, tm=tm, dst=dst: v.scalar_tensor_tensor(
                    dst(j)[:, 1:T], db[0:128, 0:T - 1], vecs[tm][:, j:j + 1],
                    zb[0:128, 0:T - 1], ALU.mult, ALU.add),
                    inc=("mixv" if tm == "tmr" else None))
        em.wait("v", "cho")
        em.gemit("v", lambda v: v.memset(halo_i[:, :], 0.0))
        for r in range(8):
            em.gemit("v", lambda v, r=r: v.scalar_tensor_tensor(
                halo_i[:, :], scr[:, 2 * T + r * NCH:2 * T + (r + 1) * NCH],
                selt[:, r:r + 1], halo_i[:, :], ALU.mult, ALU.add))
        for tm, om, dst in (("tmk", omk, xk_cols), ("tmv", omv, xv_cols),
                            ("tmr", omr, xr_cols)):
            em.emit("v", lambda v, om=om: v.tensor_tensor(
                tmp[:, 128:128 + NCH], halo_i[:, :], om[:, :], ALU.mult))
            em.gemit("v", lambda v, tm=tm: v.tensor_tensor(
                tmp[:, 160:160 + NCH], z0[:, :], vecs[tm][:, :], ALU.mult))
            em.gemit("v", lambda v: v.tensor_tensor(
                tmp[:, 192:192 + NCH], tmp[:, 128:128 + NCH],
                tmp[:, 160:160 + NCH], ALU.add))
            for j in range(NCH):
                em.emit("v", lambda v, j=j, dst=dst: v.tensor_copy(
                    dst(j)[:, 0:1], tmp[:, 192 + j:193 + j]),
                    inc=("mixf" if j == NCH - 1 else None))
        em.wait("v", "mmpm", nst["pm_w"])
        em.wait("v", "evpm", nst["pm_r"])

        pk_cols = lambda b: slice(16384 + b * (NCH * MBLK),
                                  16384 + (b + 1) * (NCH * MBLK))
        pv_cols = lambda b: slice(16384 + (2 + b) * (NCH * MBLK),
                                  16384 + (3 + b) * (NCH * MBLK))
        pr_cols = lambda b: slice(b * (NCH * MBLK), (b + 1) * (NCH * MBLK))

        def load_proj(blk):
            b = blk % 2
            if blk >= 2:
                em.wait("g", "mmr", 2 * (blk - 1))
            if blk == 0:
                em.wait("g", "spl")
            for w, cols, buf, mw in ((di["wkT"], pk_cols, projk, 10 + 2 * b),
                                     (di["wvT"], pv_cols, projk, 14 + 2 * b),
                                     (di["wrT"], pr_cols, wob, None)):
                if blk < 2 and mw is not None:
                    em.wait("g", "mixv", mw)
                src_ = w[:, blk * MBLK:(blk + 1) * MBLK].rearrange(
                    "(j p) m -> p j m", p=128)
                em.dma("g", buf[:, cols(b)].rearrange("p (j m) -> p j m", j=NCH),
                       src_, "wpj")
        load_proj(0)
        load_proj(1)

        def emit_tensor(dj):
            blk, dc = divmod(dj, 2)
            b = blk % 2
            if dj % 2 == 0 and 2 <= dj and blk + 1 < 8:
                load_proj(blk + 1)
            em.wait("t", "wpj", 16 * 3 * (blk + 1))
            for s in range(6):
                em.wait("t", f"evsl{s}", slu[s])
                slu[s] += 1
            for buf, cols_, xc, mmn, P_ in ((projk, pk_cols, xk_cols, "mmk", PA),
                                            (projk, pv_cols, xv_cols, "mmv", PB),
                                            (wob, pr_cols, xr_cols, "mmr", PC)):
                for half in range(2):
                    hs = slice(half * 512, (half + 1) * 512)
                    for j in range(NCH):
                        em.emit("t", lambda t, b=b, dc=dc, j=j, hs=hs, buf=buf,
                                cols_=cols_, xc=xc, P_=P_: t.matmul(
                            P_[:, hs],
                            buf[:, cols_(b)][:, j * MBLK + dc * 128:
                                             j * MBLK + (dc + 1) * 128],
                            xc(j)[:, hs],
                            start=(j == 0), stop=(j == NCH - 1)),
                            inc=(mmn if (j == NCH - 1 and half == 1) else None))

        def emit_pre(dj):
            p = dj % 2
            e_t, b_t = SE[p], SB[p]
            so = 4 * p
            dbc = dec[:, dj:dj + 1].broadcast_to([128, T])
            em.wait("s", "mmk", dj + 1)
            if dj >= 2:
                em.wait("s", "sfree", dj - 1)
            em.emit("s", lambda s, e_t=e_t: s.activation(
                e_t[:, 0:512], PA[:, 0:512], AF.Exp), inc="evsl0")
            em.emit("s", lambda s, e_t=e_t: s.activation(
                e_t[:, 512:1024], PA[:, 512:1024], AF.Exp), inc="evsl1")
            em.wait("s", "mmr", dj + 1)
            em.emit("s", lambda s: s.activation(
                PC[:, 0:512], PC[:, 0:512], AF.Exp, scale=-1.0))
            em.emit("s", lambda s: s.activation(
                PC[:, 512:1024], PC[:, 512:1024], AF.Exp, scale=-1.0))
            em.emit("s", lambda s: s.activation(
                PC[:, 0:512], PC[:, 0:512], AF.Ln, bias=1.0))
            em.emit("s", lambda s: s.activation(
                PC[:, 512:1024], PC[:, 512:1024], AF.Ln, bias=1.0))
            em.emit("s", lambda s, dj=dj: s.activation(
                y_cols(dj)[:, 0:512], PC[:, 0:512], AF.Exp, scale=-1.0),
                inc="evsl4")
            em.emit("s", lambda s, dj=dj: s.activation(
                y_cols(dj)[:, 512:1024], PC[:, 512:1024], AF.Exp, scale=-1.0),
                inc="evsl5")
            em.wait("v", "evsl1", slu[1])
            em.wait("v", "mmv", dj + 1)
            em.emit("v", lambda v, e_t=e_t, b_t=b_t: v.tensor_tensor(
                b_t[:, 0:512], e_t[:, 0:512], PB[:, 0:512], ALU.mult),
                inc="evsl2")
            em.emit("v", lambda v, e_t=e_t, b_t=b_t: v.tensor_tensor(
                b_t[:, 512:1024], e_t[:, 512:1024], PB[:, 512:1024], ALU.mult),
                inc="evsl3")
            if dj >= 2:
                em.wait("v", "cin", 16 * (dj - 1))
            em.gemit("v", lambda v, dbc=dbc, b_t=b_t: v.tensor_tensor_scan(
                PM[:, :], dbc, b_t[:, :], 0.0, ALU.mult, ALU.add))
            em.gemit("v", lambda v, so=so: v.tensor_copy(
                sumt[:, so:so + 1], PM[:, T - 1:T]))
            em.gemit("v", lambda v, dbc=dbc, e_t=e_t: v.tensor_tensor_scan(
                PM[:, :], dbc, e_t[:, :], 0.0, ALU.mult, ALU.add))
            em.emit("v", lambda v, so=so: v.tensor_copy(
                sumt[:, so + 1:so + 2], PM[:, T - 1:T]), inc="sumv")
            em.wait("y", "sumv", dj + 1)
            em.dma("y", cc_s_in[dj][:, :], sumt[:, so:so + 2], "cin")
            em.wait("g", "cin", 16 * (dj + 1))
            em.emit_cc(lambda g, dj=dj: g.collective_compute(
                "AllGather", ALU.bypass, replica_groups=RG,
                ins=[cc_s_in[dj].opt()], outs=[cc_s_out[dj].opt()]), f"ag{dj}")
            em.wait("y", f"ag{dj}", 1)
            em.wait("y", "rsdone", max(0, dj - 1))
            dbase = 48 * (dj % 2)
            em.dma("y", dsc[:, dbase:dbase + 16].rearrange("p (r c) -> p r c", c=2),
                   cc_s_out[dj].rearrange("(r p) c -> p r c", p=128), "bdma")

        def emit_post(dj):
            p = dj % 2
            e_t, b_t = SE[p], SB[p]
            so = 4 * p
            dbc = dec[:, dj:dj + 1].broadcast_to([128, T])
            d1b = d1024[:, dj:dj + 1].broadcast_to([128, 8])
            dbase = 48 * (dj % 2)
            bnum_r = dsc[:, dbase:dbase + 16].rearrange(
                "p (r c) -> p c r", c=2)[:, 0, :]
            bden_r = dsc[:, dbase:dbase + 16].rearrange(
                "p (r c) -> p c r", c=2)[:, 1, :]
            em.wait("v", "bdma", 16 * (dj + 1))
            em.gemit("v", lambda v, d1b=d1b, bnum_r=bnum_r, dbase=dbase:
                     v.tensor_tensor_scan(
                dsc[:, dbase + 24:dbase + 32], d1b, bnum_r, 0.0,
                ALU.mult, ALU.add))
            em.gemit("v", lambda v, dbase=dbase: v.tensor_tensor(
                dsc[:, dbase + 32:dbase + 40], dsc[:, dbase + 24:dbase + 32],
                selt[:, :], ALU.mult))
            em.gemit("v", lambda v, so=so, dbase=dbase: v.reduce_sum(
                sumt[:, so + 2:so + 3], dsc[:, dbase + 32:dbase + 40], axis=AXX))
            em.gemit("v", lambda v, d1b=d1b, bden_r=bden_r, dbase=dbase:
                     v.tensor_tensor_scan(
                dsc[:, dbase + 24:dbase + 32], d1b, bden_r, 0.0,
                ALU.mult, ALU.add))
            em.gemit("v", lambda v, dbase=dbase: v.tensor_tensor(
                dsc[:, dbase + 32:dbase + 40], dsc[:, dbase + 24:dbase + 32],
                selt[:, :], ALU.mult))
            em.gemit("v", lambda v, so=so, dbase=dbase: v.reduce_sum(
                sumt[:, so + 3:so + 4], dsc[:, dbase + 32:dbase + 40], axis=AXX))
            em.emit("v", lambda v: v.engine_nop(), inc="sden")
            em.emit("v", lambda v: v.engine_nop(), inc="rsdone")
            eub = eu[:, dj:dj + 1].broadcast_to([128, T])
            em.emit("v", lambda v, b_t=b_t, eub=eub: v.tensor_tensor(
                tmp[:, :], b_t[:, :], eub, ALU.mult))
            em.emit("v", lambda v, e_t=e_t, eub=eub: v.tensor_tensor(
                SC4[:, :], e_t[:, :], eub, ALU.mult))
            em.gemit("v", lambda v, dbc=dbc, b_t=b_t, so=so: v.tensor_tensor_scan(
                b_t[:, :], dbc, b_t[:, :], sumt[:, so + 2:so + 3],
                ALU.mult, ALU.add))
            em.gemit("v", lambda v, dbc=dbc, e_t=e_t, so=so: v.tensor_tensor_scan(
                e_t[:, :], dbc, e_t[:, :], sumt[:, so + 3:so + 4],
                ALU.mult, ALU.add))
            em.emit("v", lambda v, b_t=b_t: v.tensor_tensor(
                tmp[:, 1:T], tmp[:, 1:T], b_t[:, 0:T - 1], ALU.add))
            em.emit("v", lambda v, e_t=e_t: v.tensor_tensor(
                SC4[:, 1:T], SC4[:, 1:T], e_t[:, 0:T - 1], ALU.add))
            em.gemit("v", lambda v, so=so: v.tensor_tensor(
                tmp[:, 0:1], tmp[:, 0:1], sumt[:, so + 2:so + 3], ALU.add))
            em.gemit("v", lambda v, so=so: v.tensor_tensor(
                SC4[:, 0:1], SC4[:, 0:1], sumt[:, so + 3:so + 4], ALU.add))
            em.emit("v", lambda v: v.engine_nop(), inc="dden")
            em.wait("s", "dden", dj + 1)
            em.emit("s", lambda s: s.activation(SC4[:, :], SC4[:, :], AF.Ln))
            em.emit("s", lambda s, e_t=e_t: s.activation(
                e_t[:, :], SC4[:, :], AF.Exp, scale=-1.0), inc="srec")
            em.wait("v", "srec", dj + 1)
            em.emit("v", lambda v, e_t=e_t: v.tensor_tensor(
                tmp[:, :], tmp[:, :], e_t[:, :], ALU.mult))
            em.wait("v", "evsl5", slu[5])
            em.emit("v", lambda v, dj=dj: v.tensor_tensor(
                y_cols(dj), tmp[:, :], y_cols(dj), ALU.mult), inc="sfree")

        em.wait("t", "mixf")
        for dj in range(NCH):
            emit_tensor(dj)
            emit_pre(dj)
            if dj >= 1:
                emit_post(dj - 1)
        emit_post(NCH - 1)
        nst["pm_r"] += 1
        em.emit("v", lambda v: v.engine_nop(), inc="evpm")

        em.wait("y", "mmr", 16)
        for q in range(4):
            em.dma("y", x_res[:, q * 4 * T:(q + 1) * 4 * T].rearrange(
                       "p (j t) -> p j t", j=4),
                   spill[q * 512:(q + 1) * 512, :].rearrange(
                       "(j p) t -> p j t", p=128), "rld")

        def load_wo2(blk):
            b = blk % 2
            if blk >= 2:
                em.wait("g", "mmwo", 2 * (blk - 1))
            else:
                em.wait("g", "mmr", 16)
            src_ = di["woT"][:, blk * MBLK:(blk + 1) * MBLK].rearrange(
                "(j p) m -> p j m", p=128)
            em.dma("g", wob[:, pr_cols(b)].rearrange("p (j m) -> p j m", j=NCH),
                   src_, "wwo")
        load_wo2(0)
        n_wo = 0
        for blk in range(8):
            b = blk % 2
            if blk + 1 < 8:
                load_wo2(blk + 1)
            em.wait("t", "wwo", 16 * (blk + 1))
            for cc_ in range(2):
                ci = blk * 2 + cc_
                em.wait("t", "evsl0", slu[0])
                em.wait("t", "evsl1", slu[1])
                slu[0] += 1; slu[1] += 1
                for half in range(2):
                    hs = slice(half * 512, (half + 1) * 512)
                    for j in range(NCH):
                        if n_wo == 0 and half == 0:
                            em.wait("t", "sfree", j + 1)
                        em.emit("t", lambda t, b=b, cc_=cc_, j=j, hs=hs: t.matmul(
                            PA[:, hs],
                            wob[:, pr_cols(b)][:, j * MBLK + cc_ * 128:
                                               j * MBLK + (cc_ + 1) * 128],
                            y_cols(j)[:, hs],
                            start=(j == 0), stop=(j == NCH - 1)),
                            inc=("mmwo" if (j == NCH - 1 and half == 1) else None))
                n_wo += 1
                em.wait("v", "mmwo", n_wo)
                if ci % 4 == 0:
                    em.wait("v", "rld", 16 * (ci // 4 + 1))
                em.emit("v", lambda v, ci=ci: v.tensor_tensor(
                    x_res[:, ci * T:(ci + 1) * T][:, 0:512], PA[:, 0:512],
                    x_res[:, ci * T:(ci + 1) * T][:, 0:512], ALU.add),
                    inc="evsl0")
                em.emit("v", lambda v, ci=ci: v.tensor_tensor(
                    x_res[:, ci * T:(ci + 1) * T][:, 512:1024], PA[:, 512:1024],
                    x_res[:, ci * T:(ci + 1) * T][:, 512:1024], ALU.add),
                    inc="evsl1")
                em.emit("v", lambda v: v.engine_nop(), inc="evwo")
        em.wait("v", "evwo", n_wo)
        em.wait("y", "evwo", n_wo)
        em.wait("v", "mmwo", n_wo)

    xn_cols = lambda j: xn[:, j * T:(j + 1) * T]
    x_cols = lambda j: slice(j * T, (j + 1) * T)

    ln_stats(x_res, x_cols)
    if upto in ("ln1",):
        ln_apply(xn_cols, x_res, x_cols)
    if dbg == "xn1":
        em.wait("g", "lnap")
        for j in range(NCH):
            em.dma("g", ddbg[j * 128:(j + 1) * 128, :], xn_cols(j), "dbo")
        em.wait("y", "dbo")
    if upto not in ("ln1",):
        gating(x_res, x_cols,
               mid_emit=lambda: ln_apply(xn_cols, x_res, x_cols))
        if dbg == "cw":
            em.wait("g", "evpm", nst["pm_r"])
            em.dma("g", ddbg[0:8, :], cwT[:, :], "dbo")
            em.wait("y", "dbo")
        ffn_pass(xn_cols, "a")
        if dbg == "ffn1":
            em.dma("y", ddbg.rearrange("(j p) t -> p j t", p=128),
                   x_res.rearrange("p (j t) -> p j t", j=NCH), "dbo")
            em.wait("y", "dbo")
    if upto in ("attn", "full"):
        attn_pass()
        if dbg == "attn":
            em.dma("y", ddbg.rearrange("(j p) t -> p j t", p=128),
                   x_res.rearrange("p (j t) -> p j t", j=NCH), "dbo")
            em.wait("y", "dbo")
    if upto == "full":
        em.wait("v", "mmr", 16)
        ln_stats(x_res, x_cols, lnw=lambda j: j + 1, lnsem="evwo")
        gating(x_res, x_cols,
               mid_emit=lambda: ln_apply(xn_cols, x_res, x_cols))
        ffn_pass(xn_cols, "b")

    if upto == "full":
        for j in range(NCH):
            em.wait("y", "evob", 224 + (j // 2) * 4 + (j % 2) * 2 + 2)
            em.dma("y", dout[j * 128:(j + 1) * 128, :], x_res[:, j * T:(j + 1) * T],
                   "so")
    else:
        em.dma("y", dout.rearrange("(j p) t -> p j t", p=128),
               x_res.rearrange("p (j t) -> p j t", j=NCH), "so")
    em.wait("y", "so")

    with nc.Block() as block:
        block.tensor(lambda t: [f(t) for f in em.q["t"]])
        block.vector(lambda v: [f(v) for f in em.q["v"]])
        block.scalar(lambda s: [f(s) for f in em.q["s"]])
        block.gpsimd(lambda gq: [f(gq) for f in em.q["g"]])
        block.sync(lambda sy: [f(sy) for f in em.q["y"]])
    stack.close()
    return nc


def make_inputs(ins, core):
    x = np.asarray(ins["L0_attn_x"])[0]
    xT = np.ascontiguousarray(x.T)
    cvec = lambda v: np.ascontiguousarray(
        np.asarray(v, np.float32).reshape(NCH, 128).T)
    e8 = np.zeros((E, E * 128), np.float32)
    for e in range(E):
        e8[e, e * 128:(e + 1) * 128] = 1.0
    sel = np.zeros((128, 8), np.float32)
    if core > 0:
        sel[:, core - 1] = 1.0
    hmask = np.full((128, 1), 0.0 if core == 0 else 1.0, np.float32)
    return {
        "x": np.ascontiguousarray(xT[:, core * T:(core + 1) * T]),
        "wkT": np.ascontiguousarray(np.asarray(ins["L0_attn_Wk"]).T),
        "wvT": np.ascontiguousarray(np.asarray(ins["L0_attn_Wv"]).T),
        "wrT": np.ascontiguousarray(np.asarray(ins["L0_attn_Wr"]).T),
        "woT": np.ascontiguousarray(np.asarray(ins["L0_attn_Wo"]).T),
        "wi0": np.ascontiguousarray(np.asarray(ins["L0_ffn_wi_0"])),
        "wi1": np.ascontiguousarray(np.asarray(ins["L0_ffn_wi_1"])),
        "wo2": np.ascontiguousarray(np.asarray(ins["L0_ffn_wo"])),
        "gk": np.ascontiguousarray(np.asarray(ins["L0_ffn_gate_kernel"])),
        "td": cvec(ins["L0_attn_time_decay"]),
        "tf": cvec(ins["L0_attn_time_first"]),
        "tmk": cvec(ins["L0_attn_time_mix_k"]),
        "tmv": cvec(ins["L0_attn_time_mix_v"]),
        "tmr": cvec(ins["L0_attn_time_mix_r"]),
        "e8": e8, "ident": np.eye(128, dtype=np.float32),
        "sel": sel, "hmask": hmask,
    }


from concourse.bass_utils import run_bass_kernel_spmd

_NC_CACHE = {}

def kernel(**inputs):
    if "nc" not in _NC_CACHE:
        _NC_CACHE["nc"] = build(upto="full", dbg=None)
    nc = _NC_CACHE["nc"]
    in_maps = [make_inputs(inputs, c) for c in range(8)]
    res = run_bass_kernel_spmd(nc, in_maps, core_ids=list(range(8)))
    out = np.concatenate([res.results[c]["out"].T for c in range(8)], axis=0)
    return np.ascontiguousarray(out, dtype=np.float32).reshape(1, 8192, 2048)


# version 22
# speedup vs baseline: 1.0066x; 1.0009x over previous
import contextlib
import numpy as np
import concourse.bass as bass
import concourse.mybir as mybir

F32 = mybir.dt.float32
F32R = mybir.dt.float32r
BF16 = mybir.dt.bfloat16
AF = mybir.ActivationFunctionType
ALU = mybir.AluOpType
AXX = mybir.AxisListType.X

D = 2048
M = 2048
E = 8
T = 1024
NCH = 16
MBLK = 256
NBLK = M // MBLK
EPS = 1e-5
RG = [[0, 1, 2, 3, 4, 5, 6, 7]]


class Em:
    def __init__(self, nc, stack):
        self.nc = nc
        self.stack = stack
        self.q = {k: [] for k in ("t", "v", "s", "g", "y")}
        self.sems = {}
        self.cnt = {}

    def _sem(self, name):
        if name not in self.sems:
            self.sems[name] = self.stack.enter_context(self.nc.semaphore(name))
            self.cnt[name] = 0
        return self.sems[name]

    def emit(self, eng, fn, inc=None, inc_n=1):
        sem_h = None
        if inc is not None:
            sem_h = self._sem(inc)
            self.cnt[inc] += inc_n
        def run(engine, fn=fn, sem_h=sem_h, inc_n=inc_n):
            ins = fn(engine)
            if sem_h is not None:
                ins.then_inc(sem_h, inc_n)
            return ins
        self.q[eng].append(run)

    def emit_cc(self, fn, inc):
        sem_h = self._sem(inc)
        self.cnt[inc] += 1
        def run(engine, fn=fn, sem_h=sem_h):
            return fn(engine).then_inc(sem_h)
        self.q["g"].append(run)

    def wait(self, eng, sem, value=None):
        v = self.cnt[sem] if value is None else value
        h = self._sem(sem)
        self.q[eng].append(lambda engine, h=h, v=v: engine.wait_ge(h, v))

    def val(self, sem):
        self._sem(sem)
        return self.cnt[sem]

    def gemit(self, eng, fn):
        self.emit(eng, fn, inc="chain_" + eng)
        self.wait(eng, "chain_" + eng)

    def dma(self, eng, out, in_, sem):
        self.emit(eng, lambda q, o=out, i=in_: q.dma_start(out=o, in_=i),
                  inc=sem, inc_n=16)


def build(upto="full", dbg=None):
    nc = bass.Bass("TRN2", target_bir_lowering=False)
    stack = contextlib.ExitStack()

    di = {}
    di["x"] = nc.dram_tensor("x", [D, T], F32, kind="ExternalInput").ap()
    for w in ("wkT", "wvT", "wrT", "woT"):
        di[w] = nc.dram_tensor(w, [D, D], F32, kind="ExternalInput").ap()
    di["wi0"] = nc.dram_tensor("wi0", [E, D, M], F32, kind="ExternalInput").ap()
    di["wi1"] = nc.dram_tensor("wi1", [E, D, M], F32, kind="ExternalInput").ap()
    di["wo2"] = nc.dram_tensor("wo2", [E, M, D], F32, kind="ExternalInput").ap()
    di["gk"] = nc.dram_tensor("gk", [D, E], F32, kind="ExternalInput").ap()
    for w in ("td", "tf", "tmk", "tmv", "tmr"):
        di[w] = nc.dram_tensor(w, [128, NCH], F32, kind="ExternalInput").ap()
    di["e8"] = nc.dram_tensor("e8", [E, E * 128], F32, kind="ExternalInput").ap()
    di["ident"] = nc.dram_tensor("ident", [128, 128], F32, kind="ExternalInput").ap()
    di["sel"] = nc.dram_tensor("sel", [128, 8], F32, kind="ExternalInput").ap()
    di["hmask"] = nc.dram_tensor("hmask", [128, 1], F32, kind="ExternalInput").ap()

    dout = nc.dram_tensor("out", [D, T], F32, kind="ExternalOutput").ap()
    ddbg = nc.dram_tensor("dbg", [D, T], F32, kind="ExternalOutput").ap() \
        if dbg is not None else None

    spill = nc.dram_tensor("spill", [D, T], F32, kind="Internal").ap()
    cc_h_in = nc.dram_tensor("cc_h_in", [128, NCH], F32, kind="Internal").ap()
    cc_h_out = nc.dram_tensor("cc_h_out", [8 * 128, NCH], F32, kind="Internal").ap()
    cc_s_in = [nc.dram_tensor(f"cc_s_in{j}", [128, 2], F32, kind="Internal").ap()
               for j in range(NCH)]
    cc_s_out = [nc.dram_tensor(f"cc_s_out{j}", [8 * 128, 2], F32, kind="Internal").ap()
                for j in range(NCH)]

    em = Em(nc, stack)
    sb = lambda name, shape, dtype: stack.enter_context(
        nc.sbuf_tensor(name, shape, dtype)).ap()
    ps = lambda name, shape: stack.enter_context(
        nc.psum_tensor(name, shape, F32)).ap()

    x_res = sb("x_res", [128, NCH * T], F32)
    ar2 = sb("ar2", [128, NCH * T // 2], F32)
    ar3 = sb("ar3", [128, NCH * T // 2], F32)
    ar4 = sb("ar4", [128, NCH * T // 2], F32)
    ar5 = sb("ar5", [128, NCH * T // 4], F32)
    scr = sb("scr", [128, 5 * T], F32)
    tmp = sb("tmpt", [128, T], F32)

    xn = ar2.bitcast(BF16)
    xrT = ar2.bitcast(BF16)
    g_ = ar3.bitcast(BF16)
    y_ = ar3.bitcast(BF16)
    wib = ar4.bitcast(BF16)
    xkT = ar4.bitcast(BF16)
    xvT = x_res.bitcast(BF16)
    wob = ar5.bitcast(BF16)

    vecs = {w: sb("v_" + w, [128, NCH], F32) for w in ("td", "tf", "tmk", "tmv", "tmr")}
    onev = sb("onev", [128, 1], F32)
    onevb = sb("onevb", [128, 1], BF16)
    ones1 = sb("ones1", [1, 128], F32)
    e8t = sb("e8t", [E, E * 128], BF16)
    ident = sb("identt", [128, 128], F32)
    selt = sb("selt", [128, 8], F32)
    hmaskt = sb("hmaskt", [128, 1], F32)
    gkf = sb("gkf", [128, NCH * E], F32)
    gksb = sb("gksb", [128, E], F32)
    invcol = sb("invcol", [128, 8], F32)
    nm2col = sb("nm2col", [128, 8], F32)
    cwT = sb("cwT", [E, T], BF16)
    invb = scr[:, 0:T]
    nm2b = scr[:, T:2 * T]
    cwb = scr[:, 2 * T:3 * T]
    mean = scr[0:1, 3 * T:4 * T]
    ex2 = scr[0:1, 4 * T:5 * T]
    var = ex2
    inv = ex2
    nm2 = mean
    halo_o = sb("halo_o", [128, NCH], F32)
    halo_i = sb("halo_i", [128, NCH], F32)
    z0 = sb("z0", [128, NCH], F32)
    eu = sb("eu", [128, NCH], F32)
    omk = sb("omk", [128, NCH], F32)
    omv = sb("omv", [128, NCH], F32)
    omr = sb("omr", [128, NCH], F32)
    dec = sb("dec", [128, NCH], F32)
    d1024 = sb("d1024", [128, NCH], F32)
    w_ = sb("w_", [128, NCH], F32)
    dsc = sb("dscr", [128, 96], F32)
    sumt = sb("sumt", [128, 8], F32)

    PA = ps("PA", [128, T])
    PB = ps("PB", [128, T])
    PC = ps("PC", [128, T])
    PM = ps("PM", [128, T])
    PST = [PA, PB, PC]
    SL = lambda s: PST[s // 2][:, (s % 2) * 512:(s % 2 + 1) * 512]
    slu = [0] * 6

    nst = {"pm_w": 0, "pm_r": 0}

    def pm_write_group(mm_emits):
        em.wait("t", "evpm", nst["pm_r"])
        for i, f in enumerate(mm_emits):
            em.emit("t", f, inc=("mmpm" if i == len(mm_emits) - 1 else None))
        nst["pm_w"] += 1

    def pm_read_chain(eng, fns):
        em.wait(eng, "mmpm", nst["pm_w"])
        for i, f in enumerate(fns):
            em.emit(eng, f, inc=("evpm" if i == len(fns) - 1 else None))
        nst["pm_r"] += 1
        return nst["pm_r"]

    for name in ("td", "tf", "tmk", "tmv", "tmr"):
        em.dma("y", vecs[name][:, :], di[name][:, :], "sx")
    em.dma("g", e8t[:, :], di["e8"][:, :], "wge")
    em.dma("y", ident[:, :], di["ident"][:, :], "sx")
    em.dma("y", selt[:, :], di["sel"][:, :], "sx")
    em.dma("y", hmaskt[:, :], di["hmask"][:, :], "sx")
    em.dma("y", gkf.rearrange("p (j e) -> p j e", j=NCH),
           di["gk"].rearrange("(j p) e -> p j e", p=128), "wg")
    em.dma("y", x_res.rearrange("p (j t) -> p j t", j=NCH),
           di["x"].rearrange("(j p) t -> p j t", p=128), "sx")
    for e_ in ("v", "s", "t"):
        em.wait(e_, "sx")
    em.wait("t", "wg")
    em.wait("t", "wge")
    em.wait("v", "wg")
    em.emit("v", lambda v: v.memset(onev[:, :], 1.0))
    em.emit("v", lambda v: v.memset(onevb[:, :], 1.0))
    em.emit("v", lambda v: v.memset(hmaskt[:, :], EPS))
    em.emit("v", lambda v: v.memset(ones1[:, :], 1.0), inc="vc")
    em.wait("t", "vc")
    em.emit("s", lambda s: s.activation(w_[:, :], vecs["td"][:, :], AF.Exp), inc="sc")
    em.wait("v", "sc")
    em.emit("v", lambda v: v.tensor_scalar(w_[:, :], w_[:, :], -1.0, None, ALU.mult),
            inc="vc")
    em.wait("s", "vc")
    em.emit("s", lambda s: s.activation(dec[:, :], w_[:, :], AF.Exp))
    em.emit("s", lambda s: s.activation(d1024[:, :], w_[:, :], AF.Exp, scale=1024.0))
    em.emit("s", lambda s: s.activation(eu[:, :], vecs["tf"][:, :], AF.Exp), inc="sc")
    for nm, om in (("tmk", omk), ("tmv", omv), ("tmr", omr)):
        em.emit("v", lambda v, nm=nm, om=om: v.tensor_scalar(
            om[:, :], vecs[nm][:, :], -1.0, 1.0, ALU.mult, ALU.add), inc="vc")

    mms = [lambda t, j=j: t.matmul(PM[0:1, 0:E], onev[:, :],
                                   gkf[:, j * E:(j + 1) * E],
                                   start=(j == 0), stop=(j == NCH - 1))
           for j in range(NCH)]
    pm_write_group(mms)
    pm_read_chain("s", [lambda s: s.activation(tmp[0:1, 0:E], PM[0:1, 0:E],
                                               AF.Copy)])
    em.wait("t", "evpm", nst["pm_r"])
    pm_write_group([lambda t: t.matmul(PM[:, 0:E], ones1[:, :], tmp[0:1, 0:E],
                                       start=True, stop=True)])
    pm_read_chain("s", [lambda s: s.activation(gksb[:, :], PM[:, 0:E], AF.Copy)])

    def ln_stats(src_ap, cols, lnw=None, lnsem="evoa"):
        em.wait("t", "evpm", nst["pm_r"])
        tbm = tmp.bitcast(BF16)
        nsq = 0
        for half in range(2):
            for j in range(NCH):
                sl = slice((nsq % 4) * 512, (nsq % 4) * 512 + 512)
                if nsq >= 4:
                    em.wait("s", "mmsq", em.val("mmsq") - 3)
                if lnw is not None and half == 0:
                    em.wait("s", lnsem, lnw(j))
                em.emit("s", lambda s, j=j, half=half, sl=sl: s.activation(
                    tbm[:, sl], src_ap[:, cols(j)][:, half * 512:(half + 1) * 512],
                    AF.Copy), inc="sq")
                em.wait("t", "sq")
                em.emit("t", lambda t, j=j, half=half, sl=sl: t.matmul(
                    PM[0:1, half * 512:(half + 1) * 512],
                    onevb[:, :], tbm[:, sl],
                    start=(j == 0), stop=(j == NCH - 1)),
                    inc="mmsq")
                if half == 1 and j == NCH - 1:
                    em.emit("t", lambda t: t.matmul(
                        PC[0:1, 256:512], onevb[:, :],
                        tbm[:, 0:256], start=True, stop=True), inc="mmpm")
                nsq += 1
        nst["pm_w"] += 1
        pm_read_chain("s", [
            lambda s: s.activation(mean[0:1, 0:512], PM[0:1, 0:512],
                                   AF.Copy, scale=1.0 / D),
            lambda s: s.activation(mean[0:1, 512:1024], PM[0:1, 512:1024],
                                   AF.Copy, scale=1.0 / D)])
        em.wait("t", "evpm", nst["pm_r"])
        tb = tmp.bitcast(BF16)
        nsq = 0
        for half in range(2):
            for j in range(NCH):
                sl = slice((nsq % 4) * 512, (nsq % 4) * 512 + 512)
                if nsq >= 4:
                    em.wait("s", "mmsq", em.val("mmsq") - 3)
                if lnw is not None and half == 0:
                    em.wait("s", lnsem, lnw(j))
                em.emit("s", lambda s, j=j, half=half, sl=sl: s.activation(
                    tb[:, sl], src_ap[:, cols(j)][:, half * 512:(half + 1) * 512],
                    AF.Square), inc="sq")
                em.wait("t", "sq")
                fin = (half == 1 and j == NCH - 1)
                em.emit("t", lambda t, j=j, half=half, sl=sl: t.matmul(
                    PM[0:1, half * 512:(half + 1) * 512],
                    onevb[:, :], tb[:, sl],
                    start=(j == 0), stop=(j == NCH - 1)), inc="mmsq")
                if fin:
                    em.emit("t", lambda t: t.matmul(
                        PC[0:1, 0:256], onevb[:, :],
                        tb[:, 0:256], start=True, stop=True),
                        inc="mmpm")
                nsq += 1
        nst["pm_w"] += 1
        pm_read_chain("v", [
            lambda v: v.tensor_scalar(ex2[0:1, 0:512], PM[0:1, 0:512],
                                      1.0 / D, None, ALU.mult),
            lambda v: v.tensor_scalar(ex2[0:1, 512:1024], PM[0:1, 512:1024],
                                      1.0 / D, None, ALU.mult)])
        em.gemit("v", lambda v: v.tensor_tensor(tmp[0:1, 0:T], mean, mean, ALU.mult))
        em.emit("v", lambda v: v.tensor_tensor(var, ex2, tmp[0:1, 0:T],
                                               ALU.subtract), inc="vc")
        em.wait("s", "vc")
        em.gemit("s", lambda s: s.activation(tmp[0:1, 0:T], var, AF.Ln,
                                     bias=hmaskt[0:1, 0:1]))
        em.emit("s", lambda s: s.activation(inv, tmp[0:1, 0:T], AF.Exp,
                                            scale=-0.5), inc="sc")
        em.wait("v", "sc")
        em.gemit("v", lambda v: v.tensor_tensor(nm2, mean, inv, ALU.mult))
        em.emit("v", lambda v: v.tensor_scalar(nm2, nm2, -1.0, None, ALU.mult),
                inc="vc")
        em.wait("t", "vc")
        for vsrc, vdst in ((inv, invb), (nm2, nm2b)):
            mms = []
            for half in range(2):
                mms.append(lambda t, half=half, vsrc=vsrc: t.matmul(
                    PM[:, half * 512:(half + 1) * 512],
                    ones1[:, :],
                    vsrc[0:1, half * 512:(half + 1) * 512],
                    start=True, stop=True))
            pm_write_group(mms)
            pm_read_chain("s", [
                lambda s, vdst=vdst: s.activation(vdst[:, 0:512], PM[:, 0:512], AF.Copy),
                lambda s, vdst=vdst: s.activation(vdst[:, 512:1024], PM[:, 512:1024],
                                                  AF.Copy)])
        em.wait("v", "evpm", nst["pm_r"])

    def ln_apply(dst_cols, src_ap, cols, save_cols=False):
        for j in range(NCH):
            em.emit("v", lambda v, j=j: v.tensor_tensor(
                tmp[:, :], src_ap[:, cols(j)], invb[:, :], ALU.mult))
            em.emit("v", lambda v, j=j: v.tensor_tensor(
                dst_cols(j), tmp[:, :], nm2b[:, :], ALU.add), inc="lnap")
            if save_cols:
                em.emit("v", lambda v, j=j: v.tensor_tensor(
                    z0[:, j:j + 1], tmp[:, 0:1], nm2b[:, 0:1], ALU.add))
                em.emit("v", lambda v, j=j: v.tensor_tensor(
                    halo_o[:, j:j + 1], tmp[:, T - 1:T], nm2b[:, T - 1:T],
                    ALU.add), inc="lnap")

    def gating(src_ap, cols, mid_emit=None):
        X = tmp
        mms = []
        for tc in range(8):
            for j in range(NCH):
                mms.append(lambda t, tc=tc, j=j: t.matmul(
                    PM[:, tc * 8:(tc + 1) * 8],
                    src_ap[:, cols(j)][:, tc * 128:(tc + 1) * 128],
                    gkf[:, j * E:(j + 1) * E],
                    start=(j == 0), stop=(j == NCH - 1)))
        for tc in range(8):
            mms.append(lambda t, tc=tc: t.matmul(
                PM[:, 512 + tc:513 + tc], inv[0:1, tc * 128:(tc + 1) * 128],
                ones1[0:1, 0:1], start=True, stop=True))
            mms.append(lambda t, tc=tc: t.matmul(
                PM[:, 520 + tc:521 + tc], nm2[0:1, tc * 128:(tc + 1) * 128],
                ones1[0:1, 0:1], start=True, stop=True))
        pm_write_group(mms)
        if mid_emit is not None:
            mid_emit()
        em.wait("s", "mmpm", nst["pm_w"])
        em.gemit("s", lambda s: s.activation(invcol[:, 0:8], PM[:, 512:520],
                                             AF.Copy))
        em.emit("s", lambda s: s.activation(nm2col[:, 0:8], PM[:, 520:528],
                                            AF.Copy), inc="scst")
        r3 = lambda ap: ap.rearrange("p (tc e) -> p tc e", e=8)
        bc = lambda ap: ap.rearrange("p (tc one) -> p tc one", one=1).broadcast_to(
            [128, 8, 8])
        l_all = X[:, 0:64]; t2 = X[:, 64:128]; m1 = X[:, 128:136]
        m2 = X[:, 136:144]; w1 = X[:, 144:152]; w2 = X[:, 152:160]
        mk1 = X[:, 160:224]; l2 = X[:, 224:288]; mk2 = X[:, 288:352]
        u1 = X[:, 352:416]; cw_all = X[:, 416:480]
        gksb3 = gksb.rearrange("p (one e) -> p one e", one=1).broadcast_to(
            [128, 8, 8])
        em.wait("v", "mmpm", nst["pm_w"])
        em.wait("v", "scst")
        em.gemit("v", lambda v: v.tensor_tensor(r3(t2), gksb3, bc(nm2col[:, 0:8]),
                                                ALU.mult))
        em.gemit("v", lambda v: v.tensor_tensor(r3(l_all), r3(PM[:, 0:64]),
                                                bc(invcol[:, 0:8]), ALU.mult))
        em.emit("v", lambda v: v.tensor_tensor(l_all, l_all, t2, ALU.add),
                inc="evpm")
        em.gemit("v", lambda v: v.engine_nop())
        nst["pm_r"] += 1
        em.gemit("v", lambda v: v.tensor_reduce(
            m1.rearrange("p (tc one) -> p tc one", one=1), r3(l_all),
            axis=AXX, op=ALU.max))
        em.gemit("v", lambda v: v.tensor_tensor(r3(mk1), r3(l_all), bc(m1),
                                                ALU.is_equal))
        em.gemit("v", lambda v: v.scalar_tensor_tensor(l2, mk1, -1e30, l_all,
                                                       ALU.mult, ALU.add))
        em.gemit("v", lambda v: v.tensor_reduce(
            m2.rearrange("p (tc one) -> p tc one", one=1), r3(l2),
            axis=AXX, op=ALU.max))
        em.gemit("v", lambda v: v.tensor_tensor(r3(mk2), r3(l2), bc(m2),
                                                ALU.is_equal))
        em.emit("v", lambda v: v.tensor_tensor(w1, m1, m2, ALU.subtract),
                inc="vc")
        em.wait("s", "vc")
        em.emit("s", lambda s: s.activation(w1, w1, AF.Sigmoid), inc="scst")
        em.wait("v", "scst")
        em.gemit("v", lambda v: v.tensor_scalar(w2, w1, -1.0, 1.0, ALU.mult,
                                                ALU.add))
        em.gemit("v", lambda v: v.tensor_tensor(r3(u1), r3(mk1), bc(w1),
                                                ALU.mult))
        em.gemit("v", lambda v: v.tensor_tensor(r3(cw_all), r3(mk2), bc(w2),
                                                ALU.mult))
        em.emit("v", lambda v: v.tensor_tensor(cw_all, cw_all, u1, ALU.add),
                inc="vc")
        em.wait("t", "vc")
        for grp in range(2):
            mms = []
            for q in range(4):
                tc = grp * 4 + q
                mms.append(lambda t, tc=tc, q=q, grp=grp: t.transpose(
                    PM[0:8, grp * 512 + q * 128:grp * 512 + (q + 1) * 128],
                    cw_all[:, tc * 8:(tc + 1) * 8], ident[:, :]))
            pm_write_group(mms)
            pm_read_chain("s", [lambda s, grp=grp: s.activation(
                cwT[:, grp * 512:(grp + 1) * 512],
                PM[0:8, grp * 512:(grp + 1) * 512], AF.Copy)])

    def cw_broadcast(e, scalar_wait):
        mms = []
        for half in range(2):
            mms.append(lambda t, e=e, half=half: t.matmul(
                PM[:, half * 512:(half + 1) * 512],
                e8t[:, e * 128:(e + 1) * 128],
                cwT[:, half * 512:(half + 1) * 512],
                start=True, stop=True))
        pm_write_group(mms)
        if scalar_wait is not None:
            em.wait("s", scalar_wait[0], scalar_wait[1])
        return pm_read_chain("s", [
            lambda s: s.activation(cwb[:, 0:512], PM[:, 0:512], AF.Copy),
            lambda s: s.activation(cwb[:, 512:1024], PM[:, 512:1024], AF.Copy)])

    def ffn_pass(xn_cols, tag):
        wsem, wosem = "wf" + tag, "wq" + tag
        mmh, mmo = "mmh" + tag, "mmo" + tag
        evg, evo = "evg" + tag, "evo" + tag
        wi_cols = lambda b, m_: slice((b * 2 + m_) * (NCH * MBLK),
                                      (b * 2 + m_ + 1) * (NCH * MBLK))
        wo_cols = lambda b: slice(b * (NCH * MBLK), (b + 1) * (NCH * MBLK))

        def load_wi(nuse):
            e, blk = divmod(nuse, NBLK)
            b = nuse % 2
            if nuse == 0 and tag == "b":
                em.wait("g", "mmr", 16)
            if nuse >= 2:
                em.wait("g", mmh, 4 * (nuse - 1))
            for m_, w in ((0, di["wi0"]), (1, di["wi1"])):
                src_ = w[e, :, blk * MBLK:(blk + 1) * MBLK].rearrange(
                    "(j p) m -> p j m", p=128)
                dst = wib[:, wi_cols(b, m_)].rearrange("p (j m) -> p j m", j=NCH)
                em.dma("g", dst, src_, wsem)

        def load_wo(nuse):
            e, blk = divmod(nuse, NBLK)
            b = nuse % 2
            if nuse == 0 and tag == "b":
                em.wait("g", "mmwo", 16)
            if nuse >= 2:
                em.wait("g", mmo, 4 * (nuse - 1))
            src_ = di["wo2"][e, :, blk * MBLK:(blk + 1) * MBLK].rearrange(
                "(j p) m -> p j m", p=128)
            dst = wob[:, wo_cols(b)].rearrange("p (j m) -> p j m", j=NCH)
            em.dma("g", dst, src_, wosem)

        load_wi(0)
        load_wo(0)
        em.wait("t", "lnap")
        sil_gate = em.val("vc")
        n_h = 0
        n_o = 0
        n_g = 0
        for e in range(E):
            cwb_rdy = cw_broadcast(e, ((evg, 32 * e) if e > 0 else None))
            for blk in range(NBLK):
                nuse = e * NBLK + blk
                b = nuse % 2
                if nuse + 1 < E * NBLK:
                    load_wi(nuse + 1)
                em.wait("t", wsem, 16 * 2 * (nuse + 1))
                for mc in range(2):
                    mi = blk * 2 + mc
                    for half in range(2):
                        hs = slice(half * 512, (half + 1) * 512)
                        s0 = (2 * n_h) % 6
                        s1 = (2 * n_h + 1) % 6
                        em.wait("t", f"evsl{s0}", slu[s0])
                        em.wait("t", f"evsl{s1}", slu[s1])
                        slu[s0] += 1; slu[s1] += 1
                        for j in range(NCH):
                            em.emit("t", lambda t, b=b, mc=mc, j=j, hs=hs, s0=s0:
                                    t.matmul(SL(s0),
                                             wib[:, wi_cols(b, 0)][:, j * MBLK + mc * 128:
                                                                   j * MBLK + (mc + 1) * 128],
                                             xn_cols(j)[:, hs],
                                             start=(j == 0), stop=(j == NCH - 1)))
                        for j in range(NCH):
                            em.emit("t", lambda t, b=b, mc=mc, j=j, hs=hs, s1=s1:
                                    t.matmul(SL(s1),
                                             wib[:, wi_cols(b, 1)][:, j * MBLK + mc * 128:
                                                                   j * MBLK + (mc + 1) * 128],
                                             xn_cols(j)[:, hs],
                                             start=(j == 0), stop=(j == NCH - 1)),
                                    inc=(mmh if j == NCH - 1 else None))
                        n_h += 1
                        th = slice((n_h % 2) * 512, (n_h % 2) * 512 + 512)
                        em.wait("s", mmh, n_h)
                        if n_h == 1:
                            em.wait("s", "vc", sil_gate)
                        if n_g >= 2:
                            em.wait("s", evg, n_g - 1)
                        em.emit("s", lambda s, th=th, s0=s0: s.activation(
                            tmp[:, th], SL(s0), AF.Silu), inc=f"evsl{s0}")
                        em.wait("v", f"evsl{s0}", slu[s0])
                        if n_h == 1:
                            em.wait("v", "evpm", cwb_rdy)
                        em.emit("v", lambda v, th=th, s1=s1: v.tensor_tensor(
                            tmp[:, th], tmp[:, th], SL(s1), ALU.mult),
                            inc=f"evsl{s1}")
                        em.emit("v", lambda v, mi=mi, hs=hs, th=th: v.tensor_tensor(
                            g_[:, mi * T:(mi + 1) * T][:, hs], tmp[:, th],
                            cwb[:, hs], ALU.mult), inc=evg)
                        n_g += 1
            last_w = [32 * e + 2]
            def gwait(j, e=e, last_w=last_w):
                tgt = 32 * e + 2 * (j + 1)
                if tgt > last_w[0]:
                    em.wait("t", evg, tgt)
                    last_w[0] = tgt
            for blk in range(NBLK):
                nuse = e * NBLK + blk
                b = nuse % 2
                if nuse + 1 < E * NBLK:
                    load_wo(nuse + 1)
                em.wait("t", wosem, 16 * (nuse + 1))
                for dc in range(2):
                    dii = blk * 2 + dc
                    for half in range(2):
                        hs = slice(half * 512, (half + 1) * 512)
                        s0 = n_o % 6
                        em.wait("t", f"evsl{s0}", slu[s0])
                        slu[s0] += 1
                        for j in range(NCH):
                            gwait(j)
                            em.emit("t", lambda t, b=b, dc=dc, j=j, hs=hs, s0=s0:
                                    t.matmul(SL(s0),
                                             wob[:, wo_cols(b)][:, j * MBLK + dc * 128:
                                                                j * MBLK + (dc + 1) * 128],
                                             g_[:, j * T:(j + 1) * T][:, hs],
                                             start=(j == 0), stop=(j == NCH - 1)),
                                    inc=(mmo if j == NCH - 1 else None))
                        n_o += 1
                        em.wait("v", mmo, n_o)
                        em.emit("v", lambda v, dii=dii, hs=hs, s0=s0:
                                v.scalar_tensor_tensor(
                                    x_res[:, dii * T:(dii + 1) * T][:, hs], SL(s0),
                                    0.5, x_res[:, dii * T:(dii + 1) * T][:, hs],
                                    ALU.mult, ALU.add), inc=f"evsl{s0}")
                        em.emit("v", lambda v: v.engine_nop(), inc=evo)
        em.wait("y", evo, n_o)
        em.wait("g", evo, n_o)
        em.wait("v", evo, n_o)
        if tag != "a":
            em.wait("t", evo, n_o)
            em.wait("s", evo, n_o)

    def attn_pass():
        xk_cols = lambda j: xkT[:, j * T:(j + 1) * T]
        xv_cols = lambda j: xvT[:, j * T:(j + 1) * T]
        xr_cols = lambda j: xrT[:, j * T:(j + 1) * T]
        y_cols = lambda j: y_[:, j * T:(j + 1) * T]
        projk = x_res.bitcast(BF16)
        SE = [scr[:, 0:T], scr[:, 2 * T:3 * T]]
        SB = [scr[:, T:2 * T], scr[:, 3 * T:4 * T]]
        SC4 = scr[:, 4 * T:5 * T]

        em.dma("y", spill.rearrange("(j p) t -> p j t", p=128),
               x_res.rearrange("p (j t) -> p j t", j=NCH), "spl")
        lnw = lambda j: 224 + (j // 2) * 4 + (j % 2) * 2 + 2
        ln_stats(x_res, x_cols, lnw=lnw)
        x3 = x_res.rearrange("p (j t) -> p j t", j=NCH)
        ivl = invb[:, T - 1:T].broadcast_to([128, NCH])
        nml = nm2b[:, T - 1:T].broadcast_to([128, NCH])
        iv0 = invb[:, 0:1].broadcast_to([128, NCH])
        nm0 = nm2b[:, 0:1].broadcast_to([128, NCH])
        em.gemit("v", lambda v: v.tensor_tensor(
            halo_o[:, :], x3[:, :, T - 1], ivl, ALU.mult))
        em.emit("v", lambda v: v.tensor_tensor(
            halo_o[:, :], halo_o[:, :], nml, ALU.add))
        em.gemit("v", lambda v: v.tensor_tensor(
            z0[:, :], x3[:, :, 0], iv0, ALU.mult))
        em.emit("v", lambda v: v.tensor_tensor(
            z0[:, :], z0[:, :], nm0, ALU.add), inc="mixz")
        em.wait("y", "mixz")
        em.dma("y", cc_h_in[:, :], halo_o[:, :], "chi")
        em.wait("g", "chi")
        em.emit_cc(lambda g: g.collective_compute(
            "AllGather", ALU.bypass, replica_groups=RG,
            ins=[cc_h_in.opt()], outs=[cc_h_out.opt()]), "agh")
        em.wait("y", "agh", 1)
        em.dma("y", scr[:, 2 * T:2 * T + 8 * NCH].rearrange(
                   "p (r c) -> p r c", r=8),
               cc_h_out.rearrange("(r p) c -> p r c", p=128), "cho")
        em.wait("v", "spl")
        em.wait("g", "spl")
        zb = tmp.bitcast(BF16)[:, 0:T]
        db = tmp.bitcast(BF16)[:, T:2 * T]
        for j in range(NCH):
            em.emit("v", lambda v, j=j: v.tensor_tensor(
                SC4[:, :], x_res[:, x_cols(j)], invb[:, :], ALU.mult))
            em.emit("v", lambda v, j=j: v.tensor_tensor(
                zb, SC4[:, :], nm2b[:, :], ALU.add))
            em.emit("v", lambda v: v.tensor_tensor(
                db[0:128, 0:T - 1], zb[0:128, 1:T], zb[0:128, 0:T - 1],
                ALU.subtract))
            for tm, dst in (("tmk", xk_cols), ("tmv", xv_cols), ("tmr", xr_cols)):
                em.emit("v", lambda v, j=j, tm=tm, dst=dst: v.scalar_tensor_tensor(
                    dst(j)[:, 1:T], db[0:128, 0:T - 1], vecs[tm][:, j:j + 1],
                    zb[0:128, 0:T - 1], ALU.mult, ALU.add),
                    inc=("mixv" if tm == "tmr" else None))
        em.wait("v", "cho")
        em.gemit("v", lambda v: v.memset(halo_i[:, :], 0.0))
        for r in range(8):
            em.gemit("v", lambda v, r=r: v.scalar_tensor_tensor(
                halo_i[:, :], scr[:, 2 * T + r * NCH:2 * T + (r + 1) * NCH],
                selt[:, r:r + 1], halo_i[:, :], ALU.mult, ALU.add))
        for tm, om, dst in (("tmk", omk, xk_cols), ("tmv", omv, xv_cols),
                            ("tmr", omr, xr_cols)):
            em.emit("v", lambda v, om=om: v.tensor_tensor(
                tmp[:, 128:128 + NCH], halo_i[:, :], om[:, :], ALU.mult))
            em.gemit("v", lambda v, tm=tm: v.tensor_tensor(
                tmp[:, 160:160 + NCH], z0[:, :], vecs[tm][:, :], ALU.mult))
            em.gemit("v", lambda v: v.tensor_tensor(
                tmp[:, 192:192 + NCH], tmp[:, 128:128 + NCH],
                tmp[:, 160:160 + NCH], ALU.add))
            for j in range(NCH):
                em.emit("v", lambda v, j=j, dst=dst: v.tensor_copy(
                    dst(j)[:, 0:1], tmp[:, 192 + j:193 + j]),
                    inc=("mixf" if j == NCH - 1 else None))
        em.wait("v", "mmpm", nst["pm_w"])
        em.wait("v", "evpm", nst["pm_r"])

        pk_cols = lambda b: slice(16384 + b * (NCH * MBLK),
                                  16384 + (b + 1) * (NCH * MBLK))
        pv_cols = lambda b: slice(16384 + (2 + b) * (NCH * MBLK),
                                  16384 + (3 + b) * (NCH * MBLK))
        pr_cols = lambda b: slice(b * (NCH * MBLK), (b + 1) * (NCH * MBLK))

        def load_proj(blk):
            b = blk % 2
            if blk >= 2:
                em.wait("g", "mmr", 2 * (blk - 1))
            if blk == 0:
                em.wait("g", "spl")
            for w, cols, buf, mw in ((di["wkT"], pk_cols, projk, 10 + 2 * b),
                                     (di["wvT"], pv_cols, projk, 14 + 2 * b),
                                     (di["wrT"], pr_cols, wob, None)):
                if blk < 2 and mw is not None:
                    em.wait("g", "mixv", mw)
                src_ = w[:, blk * MBLK:(blk + 1) * MBLK].rearrange(
                    "(j p) m -> p j m", p=128)
                em.dma("g", buf[:, cols(b)].rearrange("p (j m) -> p j m", j=NCH),
                       src_, "wpj")
        load_proj(0)
        load_proj(1)

        def emit_tensor(dj):
            blk, dc = divmod(dj, 2)
            b = blk % 2
            if dj % 2 == 0 and 2 <= dj and blk + 1 < 8:
                load_proj(blk + 1)
            em.wait("t", "wpj", 16 * 3 * (blk + 1))
            for s in range(6):
                em.wait("t", f"evsl{s}", slu[s])
                slu[s] += 1
            for buf, cols_, xc, mmn, P_ in ((projk, pk_cols, xk_cols, "mmk", PA),
                                            (projk, pv_cols, xv_cols, "mmv", PB),
                                            (wob, pr_cols, xr_cols, "mmr", PC)):
                for half in range(2):
                    hs = slice(half * 512, (half + 1) * 512)
                    for j in range(NCH):
                        em.emit("t", lambda t, b=b, dc=dc, j=j, hs=hs, buf=buf,
                                cols_=cols_, xc=xc, P_=P_: t.matmul(
                            P_[:, hs],
                            buf[:, cols_(b)][:, j * MBLK + dc * 128:
                                             j * MBLK + (dc + 1) * 128],
                            xc(j)[:, hs],
                            start=(j == 0), stop=(j == NCH - 1)),
                            inc=(mmn if (j == NCH - 1 and half == 1) else None))

        def emit_pre(dj):
            p = dj % 2
            e_t, b_t = SE[p], SB[p]
            so = 4 * p
            dbc = dec[:, dj:dj + 1].broadcast_to([128, T])
            em.wait("s", "mmk", dj + 1)
            if dj >= 2:
                em.wait("s", "sfree", dj - 1)
            em.emit("s", lambda s, e_t=e_t: s.activation(
                e_t[:, 0:512], PA[:, 0:512], AF.Exp), inc="evsl0")
            em.emit("s", lambda s, e_t=e_t: s.activation(
                e_t[:, 512:1024], PA[:, 512:1024], AF.Exp), inc="evsl1")
            em.wait("s", "mmr", dj + 1)
            em.emit("s", lambda s: s.activation(
                PC[:, 0:512], PC[:, 0:512], AF.Exp, scale=-1.0))
            em.emit("s", lambda s: s.activation(
                PC[:, 512:1024], PC[:, 512:1024], AF.Exp, scale=-1.0))
            em.emit("s", lambda s: s.activation(
                PC[:, 0:512], PC[:, 0:512], AF.Ln, bias=1.0))
            em.emit("s", lambda s: s.activation(
                PC[:, 512:1024], PC[:, 512:1024], AF.Ln, bias=1.0))
            em.emit("s", lambda s, dj=dj: s.activation(
                y_cols(dj)[:, 0:512], PC[:, 0:512], AF.Exp, scale=-1.0),
                inc="evsl4")
            em.emit("s", lambda s, dj=dj: s.activation(
                y_cols(dj)[:, 512:1024], PC[:, 512:1024], AF.Exp, scale=-1.0),
                inc="evsl5")
            em.wait("v", "evsl1", slu[1])
            em.wait("v", "mmv", dj + 1)
            em.emit("v", lambda v, e_t=e_t, b_t=b_t: v.tensor_tensor(
                b_t[:, 0:512], e_t[:, 0:512], PB[:, 0:512], ALU.mult),
                inc="evsl2")
            em.emit("v", lambda v, e_t=e_t, b_t=b_t: v.tensor_tensor(
                b_t[:, 512:1024], e_t[:, 512:1024], PB[:, 512:1024], ALU.mult),
                inc="evsl3")
            if dj >= 2:
                em.wait("v", "cin", 16 * (dj - 1))
            em.gemit("v", lambda v, dbc=dbc, b_t=b_t: v.tensor_tensor_scan(
                PM[:, :], dbc, b_t[:, :], 0.0, ALU.mult, ALU.add))
            em.gemit("v", lambda v, so=so: v.tensor_copy(
                sumt[:, so:so + 1], PM[:, T - 1:T]))
            em.gemit("v", lambda v, dbc=dbc, e_t=e_t: v.tensor_tensor_scan(
                PM[:, :], dbc, e_t[:, :], 0.0, ALU.mult, ALU.add))
            em.emit("v", lambda v, so=so: v.tensor_copy(
                sumt[:, so + 1:so + 2], PM[:, T - 1:T]), inc="sumv")
            em.wait("y", "sumv", dj + 1)
            em.dma("y", cc_s_in[dj][:, :], sumt[:, so:so + 2], "cin")
            em.wait("g", "cin", 16 * (dj + 1))
            em.emit_cc(lambda g, dj=dj: g.collective_compute(
                "AllGather", ALU.bypass, replica_groups=RG,
                ins=[cc_s_in[dj].opt()], outs=[cc_s_out[dj].opt()]), f"ag{dj}")
            em.wait("y", f"ag{dj}", 1)
            em.wait("y", "rsdone", max(0, dj - 1))
            dbase = 48 * (dj % 2)
            em.dma("y", dsc[:, dbase:dbase + 16].rearrange("p (r c) -> p r c", c=2),
                   cc_s_out[dj].rearrange("(r p) c -> p r c", p=128), "bdma")

        def emit_post(dj):
            p = dj % 2
            e_t, b_t = SE[p], SB[p]
            so = 4 * p
            dbc = dec[:, dj:dj + 1].broadcast_to([128, T])
            d1b = d1024[:, dj:dj + 1].broadcast_to([128, 8])
            dbase = 48 * (dj % 2)
            bnum_r = dsc[:, dbase:dbase + 16].rearrange(
                "p (r c) -> p c r", c=2)[:, 0, :]
            bden_r = dsc[:, dbase:dbase + 16].rearrange(
                "p (r c) -> p c r", c=2)[:, 1, :]
            em.wait("v", "bdma", 16 * (dj + 1))
            em.gemit("v", lambda v, d1b=d1b, bnum_r=bnum_r, dbase=dbase:
                     v.tensor_tensor_scan(
                dsc[:, dbase + 24:dbase + 32], d1b, bnum_r, 0.0,
                ALU.mult, ALU.add))
            em.gemit("v", lambda v, dbase=dbase: v.tensor_tensor(
                dsc[:, dbase + 32:dbase + 40], dsc[:, dbase + 24:dbase + 32],
                selt[:, :], ALU.mult))
            em.gemit("v", lambda v, so=so, dbase=dbase: v.reduce_sum(
                sumt[:, so + 2:so + 3], dsc[:, dbase + 32:dbase + 40], axis=AXX))
            em.gemit("v", lambda v, d1b=d1b, bden_r=bden_r, dbase=dbase:
                     v.tensor_tensor_scan(
                dsc[:, dbase + 24:dbase + 32], d1b, bden_r, 0.0,
                ALU.mult, ALU.add))
            em.gemit("v", lambda v, dbase=dbase: v.tensor_tensor(
                dsc[:, dbase + 32:dbase + 40], dsc[:, dbase + 24:dbase + 32],
                selt[:, :], ALU.mult))
            em.gemit("v", lambda v, so=so, dbase=dbase: v.reduce_sum(
                sumt[:, so + 3:so + 4], dsc[:, dbase + 32:dbase + 40], axis=AXX))
            em.emit("v", lambda v: v.engine_nop(), inc="sden")
            em.emit("v", lambda v: v.engine_nop(), inc="rsdone")
            eub = eu[:, dj:dj + 1].broadcast_to([128, T])
            em.emit("v", lambda v, b_t=b_t, eub=eub: v.tensor_tensor(
                tmp[:, :], b_t[:, :], eub, ALU.mult))
            em.emit("v", lambda v, e_t=e_t, eub=eub: v.tensor_tensor(
                SC4[:, :], e_t[:, :], eub, ALU.mult))
            em.gemit("v", lambda v, dbc=dbc, b_t=b_t, so=so: v.tensor_tensor_scan(
                b_t[:, :], dbc, b_t[:, :], sumt[:, so + 2:so + 3],
                ALU.mult, ALU.add))
            em.gemit("v", lambda v, dbc=dbc, e_t=e_t, so=so: v.tensor_tensor_scan(
                e_t[:, :], dbc, e_t[:, :], sumt[:, so + 3:so + 4],
                ALU.mult, ALU.add))
            em.emit("v", lambda v, b_t=b_t: v.tensor_tensor(
                tmp[:, 1:T], tmp[:, 1:T], b_t[:, 0:T - 1], ALU.add))
            em.emit("v", lambda v, e_t=e_t: v.tensor_tensor(
                SC4[:, 1:T], SC4[:, 1:T], e_t[:, 0:T - 1], ALU.add))
            em.gemit("v", lambda v, so=so: v.tensor_tensor(
                tmp[:, 0:1], tmp[:, 0:1], sumt[:, so + 2:so + 3], ALU.add))
            em.gemit("v", lambda v, so=so: v.tensor_tensor(
                SC4[:, 0:1], SC4[:, 0:1], sumt[:, so + 3:so + 4], ALU.add))
            em.emit("v", lambda v: v.engine_nop(), inc="dden")
            em.wait("s", "dden", dj + 1)
            em.emit("s", lambda s: s.activation(SC4[:, :], SC4[:, :], AF.Ln))
            em.emit("s", lambda s, e_t=e_t: s.activation(
                e_t[:, :], SC4[:, :], AF.Exp, scale=-1.0), inc="srec")
            em.wait("v", "srec", dj + 1)
            em.emit("v", lambda v, e_t=e_t: v.tensor_tensor(
                tmp[:, :], tmp[:, :], e_t[:, :], ALU.mult))
            em.wait("v", "evsl5", slu[5])
            em.emit("v", lambda v, dj=dj: v.tensor_tensor(
                y_cols(dj), tmp[:, :], y_cols(dj), ALU.mult), inc="sfree")

        em.wait("t", "mixf")
        for dj in range(NCH):
            emit_tensor(dj)
            emit_pre(dj)
            if dj >= 1:
                emit_post(dj - 1)
        emit_post(NCH - 1)
        nst["pm_r"] += 1
        em.emit("v", lambda v: v.engine_nop(), inc="evpm")

        em.wait("y", "mmr", 16)
        for q in range(4):
            em.dma("y", x_res[:, q * 4 * T:(q + 1) * 4 * T].rearrange(
                       "p (j t) -> p j t", j=4),
                   spill[q * 512:(q + 1) * 512, :].rearrange(
                       "(j p) t -> p j t", p=128), "rld")

        def load_wo2(blk):
            b = blk % 2
            if blk >= 2:
                em.wait("g", "mmwo", 2 * (blk - 1))
            else:
                em.wait("g", "mmr", 16)
            src_ = di["woT"][:, blk * MBLK:(blk + 1) * MBLK].rearrange(
                "(j p) m -> p j m", p=128)
            em.dma("g", wob[:, pr_cols(b)].rearrange("p (j m) -> p j m", j=NCH),
                   src_, "wwo")
        load_wo2(0)
        n_wo = 0
        for blk in range(8):
            b = blk % 2
            if blk + 1 < 8:
                load_wo2(blk + 1)
            em.wait("t", "wwo", 16 * (blk + 1))
            for cc_ in range(2):
                ci = blk * 2 + cc_
                em.wait("t", "evsl0", slu[0])
                em.wait("t", "evsl1", slu[1])
                slu[0] += 1; slu[1] += 1
                for half in range(2):
                    hs = slice(half * 512, (half + 1) * 512)
                    for j in range(NCH):
                        if n_wo == 0 and half == 0:
                            em.wait("t", "sfree", j + 1)
                        em.emit("t", lambda t, b=b, cc_=cc_, j=j, hs=hs: t.matmul(
                            PA[:, hs],
                            wob[:, pr_cols(b)][:, j * MBLK + cc_ * 128:
                                               j * MBLK + (cc_ + 1) * 128],
                            y_cols(j)[:, hs],
                            start=(j == 0), stop=(j == NCH - 1)),
                            inc=("mmwo" if (j == NCH - 1 and half == 1) else None))
                n_wo += 1
                em.wait("v", "mmwo", n_wo)
                if ci % 4 == 0:
                    em.wait("v", "rld", 16 * (ci // 4 + 1))
                em.emit("v", lambda v, ci=ci: v.tensor_tensor(
                    x_res[:, ci * T:(ci + 1) * T][:, 0:512], PA[:, 0:512],
                    x_res[:, ci * T:(ci + 1) * T][:, 0:512], ALU.add),
                    inc="evsl0")
                em.emit("v", lambda v, ci=ci: v.tensor_tensor(
                    x_res[:, ci * T:(ci + 1) * T][:, 512:1024], PA[:, 512:1024],
                    x_res[:, ci * T:(ci + 1) * T][:, 512:1024], ALU.add),
                    inc="evsl1")
                em.emit("v", lambda v: v.engine_nop(), inc="evwo")
        em.wait("v", "evwo", n_wo)
        em.wait("y", "evwo", n_wo)
        em.wait("v", "mmwo", n_wo)

    xn_cols = lambda j: xn[:, j * T:(j + 1) * T]
    x_cols = lambda j: slice(j * T, (j + 1) * T)

    ln_stats(x_res, x_cols)
    if upto in ("ln1",):
        ln_apply(xn_cols, x_res, x_cols)
    if dbg == "xn1":
        em.wait("g", "lnap")
        for j in range(NCH):
            em.dma("g", ddbg[j * 128:(j + 1) * 128, :], xn_cols(j), "dbo")
        em.wait("y", "dbo")
    if upto not in ("ln1",):
        gating(x_res, x_cols,
               mid_emit=lambda: ln_apply(xn_cols, x_res, x_cols))
        if dbg == "cw":
            em.wait("g", "evpm", nst["pm_r"])
            em.dma("g", ddbg[0:8, :], cwT[:, :], "dbo")
            em.wait("y", "dbo")
        ffn_pass(xn_cols, "a")
        if dbg == "ffn1":
            em.dma("y", ddbg.rearrange("(j p) t -> p j t", p=128),
                   x_res.rearrange("p (j t) -> p j t", j=NCH), "dbo")
            em.wait("y", "dbo")
    if upto in ("attn", "full"):
        attn_pass()
        if dbg == "attn":
            em.dma("y", ddbg.rearrange("(j p) t -> p j t", p=128),
                   x_res.rearrange("p (j t) -> p j t", j=NCH), "dbo")
            em.wait("y", "dbo")
    if upto == "full":
        em.wait("v", "mmr", 16)
        ln_stats(x_res, x_cols, lnw=lambda j: j + 1, lnsem="evwo")
        gating(x_res, x_cols,
               mid_emit=lambda: ln_apply(xn_cols, x_res, x_cols))
        ffn_pass(xn_cols, "b")

    if upto == "full":
        for j in range(NCH):
            em.wait("y", "evob", 224 + (j // 2) * 4 + (j % 2) * 2 + 2)
            em.dma("y", dout[j * 128:(j + 1) * 128, :], x_res[:, j * T:(j + 1) * T],
                   "so")
    else:
        em.dma("y", dout.rearrange("(j p) t -> p j t", p=128),
               x_res.rearrange("p (j t) -> p j t", j=NCH), "so")
    em.wait("y", "so")

    with nc.Block() as block:
        block.tensor(lambda t: [f(t) for f in em.q["t"]])
        block.vector(lambda v: [f(v) for f in em.q["v"]])
        block.scalar(lambda s: [f(s) for f in em.q["s"]])
        block.gpsimd(lambda gq: [f(gq) for f in em.q["g"]])
        block.sync(lambda sy: [f(sy) for f in em.q["y"]])
    stack.close()
    return nc


def make_inputs(ins, core):
    x = np.asarray(ins["L0_attn_x"])[0]
    xT = np.ascontiguousarray(x.T)
    cvec = lambda v: np.ascontiguousarray(
        np.asarray(v, np.float32).reshape(NCH, 128).T)
    e8 = np.zeros((E, E * 128), np.float32)
    for e in range(E):
        e8[e, e * 128:(e + 1) * 128] = 1.0
    sel = np.zeros((128, 8), np.float32)
    if core > 0:
        sel[:, core - 1] = 1.0
    hmask = np.full((128, 1), 0.0 if core == 0 else 1.0, np.float32)
    return {
        "x": np.ascontiguousarray(xT[:, core * T:(core + 1) * T]),
        "wkT": np.ascontiguousarray(np.asarray(ins["L0_attn_Wk"]).T),
        "wvT": np.ascontiguousarray(np.asarray(ins["L0_attn_Wv"]).T),
        "wrT": np.ascontiguousarray(np.asarray(ins["L0_attn_Wr"]).T),
        "woT": np.ascontiguousarray(np.asarray(ins["L0_attn_Wo"]).T),
        "wi0": np.ascontiguousarray(np.asarray(ins["L0_ffn_wi_0"])),
        "wi1": np.ascontiguousarray(np.asarray(ins["L0_ffn_wi_1"])),
        "wo2": np.ascontiguousarray(np.asarray(ins["L0_ffn_wo"])),
        "gk": np.ascontiguousarray(np.asarray(ins["L0_ffn_gate_kernel"])),
        "td": cvec(ins["L0_attn_time_decay"]),
        "tf": cvec(ins["L0_attn_time_first"]),
        "tmk": cvec(ins["L0_attn_time_mix_k"]),
        "tmv": cvec(ins["L0_attn_time_mix_v"]),
        "tmr": cvec(ins["L0_attn_time_mix_r"]),
        "e8": e8, "ident": np.eye(128, dtype=np.float32),
        "sel": sel, "hmask": hmask,
    }


from concourse.bass_utils import run_bass_kernel_spmd

_NC_CACHE = {}

def kernel(**inputs):
    if "nc" not in _NC_CACHE:
        _NC_CACHE["nc"] = build(upto="full", dbg=None)
    nc = _NC_CACHE["nc"]
    in_maps = [make_inputs(inputs, c) for c in range(8)]
    res = run_bass_kernel_spmd(nc, in_maps, core_ids=list(range(8)))
    out = np.concatenate([res.results[c]["out"].T for c in range(8)], axis=0)
    return np.ascontiguousarray(out, dtype=np.float32).reshape(1, 8192, 2048)


# version 23 (final)
# speedup vs baseline: 1.0082x; 1.0016x over previous
import contextlib
import numpy as np
import concourse.bass as bass
import concourse.mybir as mybir

F32 = mybir.dt.float32
F32R = mybir.dt.float32r
BF16 = mybir.dt.bfloat16
AF = mybir.ActivationFunctionType
ALU = mybir.AluOpType
AXX = mybir.AxisListType.X

D = 2048
M = 2048
E = 8
T = 1024
NCH = 16
MBLK = 256
NBLK = M // MBLK
EPS = 1e-5
RG = [[0, 1, 2, 3, 4, 5, 6, 7]]


class Em:
    def __init__(self, nc, stack):
        self.nc = nc
        self.stack = stack
        self.q = {k: [] for k in ("t", "v", "s", "g", "y")}
        self.sems = {}
        self.cnt = {}

    def _sem(self, name):
        if name not in self.sems:
            self.sems[name] = self.stack.enter_context(self.nc.semaphore(name))
            self.cnt[name] = 0
        return self.sems[name]

    def emit(self, eng, fn, inc=None, inc_n=1):
        sem_h = None
        if inc is not None:
            sem_h = self._sem(inc)
            self.cnt[inc] += inc_n
        def run(engine, fn=fn, sem_h=sem_h, inc_n=inc_n):
            ins = fn(engine)
            if sem_h is not None:
                ins.then_inc(sem_h, inc_n)
            return ins
        self.q[eng].append(run)

    def emit_cc(self, fn, inc):
        sem_h = self._sem(inc)
        self.cnt[inc] += 1
        def run(engine, fn=fn, sem_h=sem_h):
            return fn(engine).then_inc(sem_h)
        self.q["g"].append(run)

    def wait(self, eng, sem, value=None):
        v = self.cnt[sem] if value is None else value
        h = self._sem(sem)
        self.q[eng].append(lambda engine, h=h, v=v: engine.wait_ge(h, v))

    def val(self, sem):
        self._sem(sem)
        return self.cnt[sem]

    def gemit(self, eng, fn):
        self.emit(eng, fn, inc="chain_" + eng)
        self.wait(eng, "chain_" + eng)

    def dma(self, eng, out, in_, sem):
        self.emit(eng, lambda q, o=out, i=in_: q.dma_start(out=o, in_=i),
                  inc=sem, inc_n=16)


def build(upto="full", dbg=None):
    nc = bass.Bass("TRN2", target_bir_lowering=False)
    stack = contextlib.ExitStack()

    di = {}
    di["x"] = nc.dram_tensor("x", [D, T], F32, kind="ExternalInput").ap()
    for w in ("wkT", "wvT", "wrT", "woT"):
        di[w] = nc.dram_tensor(w, [D, D], F32, kind="ExternalInput").ap()
    di["wi0"] = nc.dram_tensor("wi0", [E, D, M], F32, kind="ExternalInput").ap()
    di["wi1"] = nc.dram_tensor("wi1", [E, D, M], F32, kind="ExternalInput").ap()
    di["wo2"] = nc.dram_tensor("wo2", [E, M, D], F32, kind="ExternalInput").ap()
    di["gk"] = nc.dram_tensor("gk", [D, E], F32, kind="ExternalInput").ap()
    for w in ("td", "tf", "tmk", "tmv", "tmr"):
        di[w] = nc.dram_tensor(w, [128, NCH], F32, kind="ExternalInput").ap()
    di["e8"] = nc.dram_tensor("e8", [E, E * 128], F32, kind="ExternalInput").ap()
    di["ident"] = nc.dram_tensor("ident", [128, 128], F32, kind="ExternalInput").ap()
    di["sel"] = nc.dram_tensor("sel", [128, 8], F32, kind="ExternalInput").ap()
    di["hmask"] = nc.dram_tensor("hmask", [128, 1], F32, kind="ExternalInput").ap()

    dout = nc.dram_tensor("out", [D, T], F32, kind="ExternalOutput").ap()
    ddbg = nc.dram_tensor("dbg", [D, T], F32, kind="ExternalOutput").ap() \
        if dbg is not None else None

    spill = nc.dram_tensor("spill", [D, T], F32, kind="Internal").ap()
    cc_h_in = nc.dram_tensor("cc_h_in", [128, NCH], F32, kind="Internal").ap()
    cc_h_out = nc.dram_tensor("cc_h_out", [8 * 128, NCH], F32, kind="Internal").ap()
    cc_s_in = [nc.dram_tensor(f"cc_s_in{j}", [128, 2], F32, kind="Internal").ap()
               for j in range(NCH)]
    cc_s_out = [nc.dram_tensor(f"cc_s_out{j}", [8 * 128, 2], F32, kind="Internal").ap()
                for j in range(NCH)]

    em = Em(nc, stack)
    sb = lambda name, shape, dtype: stack.enter_context(
        nc.sbuf_tensor(name, shape, dtype)).ap()
    ps = lambda name, shape: stack.enter_context(
        nc.psum_tensor(name, shape, F32)).ap()

    x_res = sb("x_res", [128, NCH * T], F32)
    ar2 = sb("ar2", [128, NCH * T // 2], F32)
    ar3 = sb("ar3", [128, NCH * T // 2], F32)
    ar4 = sb("ar4", [128, NCH * T // 2], F32)
    ar5 = sb("ar5", [128, NCH * T // 4], F32)
    scr = sb("scr", [128, 5 * T], F32)
    tmp = sb("tmpt", [128, T], F32)

    xn = ar2.bitcast(BF16)
    xrT = ar2.bitcast(BF16)
    g_ = ar3.bitcast(BF16)
    y_ = ar3.bitcast(BF16)
    wib = ar4.bitcast(BF16)
    xkT = ar4.bitcast(BF16)
    xvT = x_res.bitcast(BF16)
    wob = ar5.bitcast(BF16)

    vecs = {w: sb("v_" + w, [128, NCH], F32) for w in ("td", "tf", "tmk", "tmv", "tmr")}
    onev = sb("onev", [128, 1], F32)
    onevb = sb("onevb", [128, 1], BF16)
    ones1 = sb("ones1", [1, 128], F32)
    e8t = sb("e8t", [E, E * 128], BF16)
    ident = sb("identt", [128, 128], F32)
    selt = sb("selt", [128, 8], F32)
    hmaskt = sb("hmaskt", [128, 1], F32)
    gkf = sb("gkf", [128, NCH * E], F32)
    gksb = sb("gksb", [128, E], F32)
    invcol = sb("invcol", [128, 8], F32)
    nm2col = sb("nm2col", [128, 8], F32)
    cwT = sb("cwT", [E, T], BF16)
    invb = scr[:, 0:T]
    nm2b = scr[:, T:2 * T]
    cwb = scr[:, 2 * T:3 * T]
    mean = scr[0:1, 3 * T:4 * T]
    ex2 = scr[0:1, 4 * T:5 * T]
    var = ex2
    inv = ex2
    nm2 = mean
    halo_o = sb("halo_o", [128, NCH], F32)
    halo_i = sb("halo_i", [128, NCH], F32)
    z0 = sb("z0", [128, NCH], F32)
    eu = sb("eu", [128, NCH], F32)
    omk = sb("omk", [128, NCH], F32)
    omv = sb("omv", [128, NCH], F32)
    omr = sb("omr", [128, NCH], F32)
    dec = sb("dec", [128, NCH], F32)
    d1024 = sb("d1024", [128, NCH], F32)
    w_ = sb("w_", [128, NCH], F32)
    dsc = sb("dscr", [128, 96], F32)
    sumt = sb("sumt", [128, 8], F32)

    PA = ps("PA", [128, T])
    PB = ps("PB", [128, T])
    PC = ps("PC", [128, T])
    PM = ps("PM", [128, T])
    PST = [PA, PB, PC]
    SL = lambda s: PST[s // 2][:, (s % 2) * 512:(s % 2 + 1) * 512]
    slu = [0] * 6

    nst = {"pm_w": 0, "pm_r": 0}

    def pm_write_group(mm_emits):
        em.wait("t", "evpm", nst["pm_r"])
        for i, f in enumerate(mm_emits):
            em.emit("t", f, inc=("mmpm" if i == len(mm_emits) - 1 else None))
        nst["pm_w"] += 1

    def pm_read_chain(eng, fns):
        em.wait(eng, "mmpm", nst["pm_w"])
        for i, f in enumerate(fns):
            em.emit(eng, f, inc=("evpm" if i == len(fns) - 1 else None))
        nst["pm_r"] += 1
        return nst["pm_r"]

    for name in ("td", "tf", "tmk", "tmv", "tmr"):
        em.dma("y", vecs[name][:, :], di[name][:, :], "sx")
    em.dma("g", e8t[:, :], di["e8"][:, :], "wge")
    em.dma("y", ident[:, :], di["ident"][:, :], "sx")
    em.dma("y", selt[:, :], di["sel"][:, :], "sx")
    em.dma("y", hmaskt[:, :], di["hmask"][:, :], "sx")
    em.dma("y", gkf.rearrange("p (j e) -> p j e", j=NCH),
           di["gk"].rearrange("(j p) e -> p j e", p=128), "wg")
    em.dma("y", x_res.rearrange("p (j t) -> p j t", j=NCH),
           di["x"].rearrange("(j p) t -> p j t", p=128), "sx")
    for e_ in ("v", "s", "t"):
        em.wait(e_, "sx")
    em.wait("t", "wg")
    em.wait("t", "wge")
    em.wait("v", "wg")
    em.emit("v", lambda v: v.memset(onev[:, :], 1.0))
    em.emit("v", lambda v: v.memset(onevb[:, :], 1.0))
    em.emit("v", lambda v: v.memset(ones1[:, :], 1.0), inc="vc")
    em.wait("t", "vc")
    em.emit("s", lambda s: s.activation(w_[:, :], vecs["td"][:, :], AF.Exp), inc="sc")
    em.wait("v", "sc")
    em.emit("v", lambda v: v.tensor_scalar(w_[:, :], w_[:, :], -1.0, None, ALU.mult),
            inc="vc")
    em.wait("s", "vc")
    em.emit("s", lambda s: s.activation(dec[:, :], w_[:, :], AF.Exp))
    em.emit("s", lambda s: s.activation(d1024[:, :], w_[:, :], AF.Exp, scale=1024.0))
    em.emit("s", lambda s: s.activation(eu[:, :], vecs["tf"][:, :], AF.Exp), inc="sc")
    for nm, om in (("tmk", omk), ("tmv", omv), ("tmr", omr)):
        em.emit("v", lambda v, nm=nm, om=om: v.tensor_scalar(
            om[:, :], vecs[nm][:, :], -1.0, 1.0, ALU.mult, ALU.add), inc="vc")

    mms = [lambda t, j=j: t.matmul(PM[0:1, 0:E], onev[:, :],
                                   gkf[:, j * E:(j + 1) * E],
                                   start=(j == 0), stop=(j == NCH - 1))
           for j in range(NCH)]
    pm_write_group(mms)
    pm_read_chain("s", [lambda s: s.activation(tmp[0:1, 0:E], PM[0:1, 0:E],
                                               AF.Copy)])
    em.wait("t", "evpm", nst["pm_r"])
    pm_write_group([lambda t: t.matmul(PM[:, 0:E], ones1[:, :], tmp[0:1, 0:E],
                                       start=True, stop=True)])
    pm_read_chain("s", [lambda s: s.activation(gksb[:, :], PM[:, 0:E], AF.Copy)])

    def ln_stats(src_ap, cols, lnw=None, lnsem="evoa"):
        em.wait("t", "evpm", nst["pm_r"])
        tbm = tmp.bitcast(BF16)
        nsq = 0
        for half in range(2):
            for j in range(NCH):
                sl = slice((nsq % 4) * 512, (nsq % 4) * 512 + 512)
                if nsq >= 4:
                    em.wait("s", "mmsq", em.val("mmsq") - 3)
                if lnw is not None and half == 0:
                    em.wait("s", lnsem, lnw(j))
                em.emit("s", lambda s, j=j, half=half, sl=sl: s.activation(
                    tbm[:, sl], src_ap[:, cols(j)][:, half * 512:(half + 1) * 512],
                    AF.Copy), inc="sq")
                em.wait("t", "sq")
                em.emit("t", lambda t, j=j, half=half, sl=sl: t.matmul(
                    PM[0:1, half * 512:(half + 1) * 512],
                    onevb[:, :], tbm[:, sl],
                    start=(j == 0), stop=(j == NCH - 1)),
                    inc="mmsq")
                if half == 1 and j == NCH - 1:
                    em.emit("t", lambda t: t.matmul(
                        PC[0:1, 256:512], onevb[:, :],
                        tbm[:, 0:256], start=True, stop=True), inc="mmpm")
                nsq += 1
        nst["pm_w"] += 1
        pm_read_chain("s", [
            lambda s: s.activation(mean[0:1, 0:512], PM[0:1, 0:512],
                                   AF.Copy, scale=1.0 / D),
            lambda s: s.activation(mean[0:1, 512:1024], PM[0:1, 512:1024],
                                   AF.Copy, scale=1.0 / D)])
        em.wait("t", "evpm", nst["pm_r"])
        tb = tmp.bitcast(BF16)
        nsq = 0
        for half in range(2):
            for j in range(NCH):
                sl = slice((nsq % 4) * 512, (nsq % 4) * 512 + 512)
                if nsq >= 4:
                    em.wait("s", "mmsq", em.val("mmsq") - 3)
                if lnw is not None and half == 0:
                    em.wait("s", lnsem, lnw(j))
                em.emit("s", lambda s, j=j, half=half, sl=sl: s.activation(
                    tb[:, sl], src_ap[:, cols(j)][:, half * 512:(half + 1) * 512],
                    AF.Square), inc="sq")
                em.wait("t", "sq")
                fin = (half == 1 and j == NCH - 1)
                em.emit("t", lambda t, j=j, half=half, sl=sl: t.matmul(
                    PM[0:1, half * 512:(half + 1) * 512],
                    onevb[:, :], tb[:, sl],
                    start=(j == 0), stop=(j == NCH - 1)), inc="mmsq")
                if fin:
                    em.emit("t", lambda t: t.matmul(
                        PC[0:1, 0:256], onevb[:, :],
                        tb[:, 0:256], start=True, stop=True),
                        inc="mmpm")
                nsq += 1
        nst["pm_w"] += 1
        pm_read_chain("v", [
            lambda v: v.tensor_scalar(ex2[0:1, 0:512], PM[0:1, 0:512],
                                      1.0 / D, None, ALU.mult),
            lambda v: v.tensor_scalar(ex2[0:1, 512:1024], PM[0:1, 512:1024],
                                      1.0 / D, None, ALU.mult)])
        em.gemit("v", lambda v: v.tensor_tensor(tmp[0:1, 0:T], mean, mean, ALU.mult))
        em.gemit("v", lambda v: v.tensor_tensor(var, ex2, tmp[0:1, 0:T], ALU.subtract))
        em.emit("v", lambda v: v.tensor_scalar(var, var, EPS, None, ALU.add), inc="vc")
        em.wait("s", "vc")
        em.gemit("s", lambda s: s.activation(tmp[0:1, 0:T], var, AF.Ln))
        em.emit("s", lambda s: s.activation(inv, tmp[0:1, 0:T], AF.Exp,
                                            scale=-0.5), inc="sc")
        em.wait("v", "sc")
        em.gemit("v", lambda v: v.tensor_tensor(nm2, mean, inv, ALU.mult))
        em.emit("v", lambda v: v.tensor_scalar(nm2, nm2, -1.0, None, ALU.mult),
                inc="vc")
        em.wait("t", "vc")
        for vsrc, vdst in ((inv, invb), (nm2, nm2b)):
            mms = []
            for half in range(2):
                mms.append(lambda t, half=half, vsrc=vsrc: t.matmul(
                    PM[:, half * 512:(half + 1) * 512],
                    ones1[:, :],
                    vsrc[0:1, half * 512:(half + 1) * 512],
                    start=True, stop=True))
            pm_write_group(mms)
            pm_read_chain("s", [
                lambda s, vdst=vdst: s.activation(vdst[:, 0:512], PM[:, 0:512], AF.Copy),
                lambda s, vdst=vdst: s.activation(vdst[:, 512:1024], PM[:, 512:1024],
                                                  AF.Copy)])
        em.wait("v", "evpm", nst["pm_r"])

    def ln_apply(dst_cols, src_ap, cols, save_cols=False):
        for j in range(NCH):
            em.emit("v", lambda v, j=j: v.tensor_tensor(
                tmp[:, :], src_ap[:, cols(j)], invb[:, :], ALU.mult))
            em.emit("v", lambda v, j=j: v.tensor_tensor(
                dst_cols(j), tmp[:, :], nm2b[:, :], ALU.add), inc="lnap")
            if save_cols:
                em.emit("v", lambda v, j=j: v.tensor_tensor(
                    z0[:, j:j + 1], tmp[:, 0:1], nm2b[:, 0:1], ALU.add))
                em.emit("v", lambda v, j=j: v.tensor_tensor(
                    halo_o[:, j:j + 1], tmp[:, T - 1:T], nm2b[:, T - 1:T],
                    ALU.add), inc="lnap")

    def gating(src_ap, cols, mid_emit=None):
        X = tmp
        mms = []
        for tc in range(8):
            for j in range(NCH):
                mms.append(lambda t, tc=tc, j=j: t.matmul(
                    PM[:, tc * 8:(tc + 1) * 8],
                    src_ap[:, cols(j)][:, tc * 128:(tc + 1) * 128],
                    gkf[:, j * E:(j + 1) * E],
                    start=(j == 0), stop=(j == NCH - 1)))
        for tc in range(8):
            mms.append(lambda t, tc=tc: t.matmul(
                PM[:, 512 + tc:513 + tc], inv[0:1, tc * 128:(tc + 1) * 128],
                ones1[0:1, 0:1], start=True, stop=True))
            mms.append(lambda t, tc=tc: t.matmul(
                PM[:, 520 + tc:521 + tc], nm2[0:1, tc * 128:(tc + 1) * 128],
                ones1[0:1, 0:1], start=True, stop=True))
        pm_write_group(mms)
        if mid_emit is not None:
            mid_emit()
        em.wait("s", "mmpm", nst["pm_w"])
        em.gemit("s", lambda s: s.activation(invcol[:, 0:8], PM[:, 512:520],
                                             AF.Copy))
        em.emit("s", lambda s: s.activation(nm2col[:, 0:8], PM[:, 520:528],
                                            AF.Copy), inc="scst")
        r3 = lambda ap: ap.rearrange("p (tc e) -> p tc e", e=8)
        bc = lambda ap: ap.rearrange("p (tc one) -> p tc one", one=1).broadcast_to(
            [128, 8, 8])
        l_all = X[:, 0:64]; t2 = X[:, 64:128]; m1 = X[:, 128:136]
        m2 = X[:, 136:144]; w1 = X[:, 144:152]; w2 = X[:, 152:160]
        mk1 = X[:, 160:224]; l2 = X[:, 224:288]; mk2 = X[:, 288:352]
        u1 = X[:, 352:416]; cw_all = X[:, 416:480]
        gksb3 = gksb.rearrange("p (one e) -> p one e", one=1).broadcast_to(
            [128, 8, 8])
        em.wait("v", "mmpm", nst["pm_w"])
        em.wait("v", "scst")
        em.gemit("v", lambda v: v.tensor_tensor(r3(t2), gksb3, bc(nm2col[:, 0:8]),
                                                ALU.mult))
        em.gemit("v", lambda v: v.tensor_tensor(r3(l_all), r3(PM[:, 0:64]),
                                                bc(invcol[:, 0:8]), ALU.mult))
        em.emit("v", lambda v: v.tensor_tensor(l_all, l_all, t2, ALU.add),
                inc="evpm")
        em.gemit("v", lambda v: v.engine_nop())
        nst["pm_r"] += 1
        em.gemit("v", lambda v: v.tensor_reduce(
            m1.rearrange("p (tc one) -> p tc one", one=1), r3(l_all),
            axis=AXX, op=ALU.max))
        em.gemit("v", lambda v: v.tensor_tensor(r3(mk1), r3(l_all), bc(m1),
                                                ALU.is_equal))
        em.gemit("v", lambda v: v.scalar_tensor_tensor(l2, mk1, -1e30, l_all,
                                                       ALU.mult, ALU.add))
        em.gemit("v", lambda v: v.tensor_reduce(
            m2.rearrange("p (tc one) -> p tc one", one=1), r3(l2),
            axis=AXX, op=ALU.max))
        em.gemit("v", lambda v: v.tensor_tensor(r3(mk2), r3(l2), bc(m2),
                                                ALU.is_equal))
        em.emit("v", lambda v: v.tensor_tensor(w1, m1, m2, ALU.subtract),
                inc="vc")
        em.wait("s", "vc")
        em.emit("s", lambda s: s.activation(w1, w1, AF.Sigmoid), inc="scst")
        em.wait("v", "scst")
        em.gemit("v", lambda v: v.tensor_scalar(w2, w1, -1.0, 1.0, ALU.mult,
                                                ALU.add))
        em.gemit("v", lambda v: v.tensor_tensor(r3(u1), r3(mk1), bc(w1),
                                                ALU.mult))
        em.gemit("v", lambda v: v.tensor_tensor(r3(cw_all), r3(mk2), bc(w2),
                                                ALU.mult))
        em.emit("v", lambda v: v.tensor_tensor(cw_all, cw_all, u1, ALU.add),
                inc="vc")
        em.wait("t", "vc")
        for grp in range(2):
            mms = []
            for q in range(4):
                tc = grp * 4 + q
                mms.append(lambda t, tc=tc, q=q, grp=grp: t.transpose(
                    PM[0:8, grp * 512 + q * 128:grp * 512 + (q + 1) * 128],
                    cw_all[:, tc * 8:(tc + 1) * 8], ident[:, :]))
            pm_write_group(mms)
            pm_read_chain("s", [lambda s, grp=grp: s.activation(
                cwT[:, grp * 512:(grp + 1) * 512],
                PM[0:8, grp * 512:(grp + 1) * 512], AF.Copy)])

    def cw_broadcast(e, scalar_wait):
        mms = []
        for half in range(2):
            mms.append(lambda t, e=e, half=half: t.matmul(
                PM[:, half * 512:(half + 1) * 512],
                e8t[:, e * 128:(e + 1) * 128],
                cwT[:, half * 512:(half + 1) * 512],
                start=True, stop=True))
        pm_write_group(mms)
        if scalar_wait is not None:
            em.wait("s", scalar_wait[0], scalar_wait[1])
        return pm_read_chain("s", [
            lambda s: s.activation(cwb[:, 0:512], PM[:, 0:512], AF.Copy),
            lambda s: s.activation(cwb[:, 512:1024], PM[:, 512:1024], AF.Copy)])

    def ffn_pass(xn_cols, tag):
        wsem, wosem = "wf" + tag, "wq" + tag
        mmh, mmo = "mmh" + tag, "mmo" + tag
        evg, evo = "evg" + tag, "evo" + tag
        wi_cols = lambda b, m_: slice((b * 2 + m_) * (NCH * MBLK),
                                      (b * 2 + m_ + 1) * (NCH * MBLK))
        wo_cols = lambda b: slice(b * (NCH * MBLK), (b + 1) * (NCH * MBLK))

        def load_wi(nuse):
            e, blk = divmod(nuse, NBLK)
            b = nuse % 2
            if nuse == 0 and tag == "b":
                em.wait("g", "mmr", 16)
            if nuse >= 2:
                em.wait("g", mmh, 4 * (nuse - 1))
            for m_, w in ((0, di["wi0"]), (1, di["wi1"])):
                src_ = w[e, :, blk * MBLK:(blk + 1) * MBLK].rearrange(
                    "(j p) m -> p j m", p=128)
                dst = wib[:, wi_cols(b, m_)].rearrange("p (j m) -> p j m", j=NCH)
                em.dma("g", dst, src_, wsem)

        def load_wo(nuse):
            e, blk = divmod(nuse, NBLK)
            b = nuse % 2
            if nuse == 0 and tag == "b":
                em.wait("g", "mmwo", 16)
            if nuse >= 2:
                em.wait("g", mmo, 4 * (nuse - 1))
            src_ = di["wo2"][e, :, blk * MBLK:(blk + 1) * MBLK].rearrange(
                "(j p) m -> p j m", p=128)
            dst = wob[:, wo_cols(b)].rearrange("p (j m) -> p j m", j=NCH)
            em.dma("g", dst, src_, wosem)

        load_wi(0)
        load_wo(0)
        em.wait("t", "lnap")
        sil_gate = em.val("vc")
        n_h = 0
        n_o = 0
        n_g = 0
        for e in range(E):
            cwb_rdy = cw_broadcast(e, ((evg, 32 * e) if e > 0 else None))
            for blk in range(NBLK):
                nuse = e * NBLK + blk
                b = nuse % 2
                if nuse + 1 < E * NBLK:
                    load_wi(nuse + 1)
                em.wait("t", wsem, 16 * 2 * (nuse + 1))
                for mc in range(2):
                    mi = blk * 2 + mc
                    for half in range(2):
                        hs = slice(half * 512, (half + 1) * 512)
                        s0 = (2 * n_h) % 6
                        s1 = (2 * n_h + 1) % 6
                        em.wait("t", f"evsl{s0}", slu[s0])
                        em.wait("t", f"evsl{s1}", slu[s1])
                        slu[s0] += 1; slu[s1] += 1
                        for j in range(NCH):
                            em.emit("t", lambda t, b=b, mc=mc, j=j, hs=hs, s0=s0:
                                    t.matmul(SL(s0),
                                             wib[:, wi_cols(b, 0)][:, j * MBLK + mc * 128:
                                                                   j * MBLK + (mc + 1) * 128],
                                             xn_cols(j)[:, hs],
                                             start=(j == 0), stop=(j == NCH - 1)))
                        for j in range(NCH):
                            em.emit("t", lambda t, b=b, mc=mc, j=j, hs=hs, s1=s1:
                                    t.matmul(SL(s1),
                                             wib[:, wi_cols(b, 1)][:, j * MBLK + mc * 128:
                                                                   j * MBLK + (mc + 1) * 128],
                                             xn_cols(j)[:, hs],
                                             start=(j == 0), stop=(j == NCH - 1)),
                                    inc=(mmh if j == NCH - 1 else None))
                        n_h += 1
                        th = slice((n_h % 2) * 512, (n_h % 2) * 512 + 512)
                        em.wait("s", mmh, n_h)
                        if n_h == 1:
                            em.wait("s", "vc", sil_gate)
                        if n_g >= 2:
                            em.wait("s", evg, n_g - 1)
                        em.emit("s", lambda s, th=th, s0=s0: s.activation(
                            tmp[:, th], SL(s0), AF.Silu), inc=f"evsl{s0}")
                        em.wait("v", f"evsl{s0}", slu[s0])
                        if n_h == 1:
                            em.wait("v", "evpm", cwb_rdy)
                        em.emit("v", lambda v, th=th, s1=s1: v.tensor_tensor(
                            tmp[:, th], tmp[:, th], SL(s1), ALU.mult),
                            inc=f"evsl{s1}")
                        em.emit("v", lambda v, mi=mi, hs=hs, th=th: v.tensor_tensor(
                            g_[:, mi * T:(mi + 1) * T][:, hs], tmp[:, th],
                            cwb[:, hs], ALU.mult), inc=evg)
                        n_g += 1
            last_w = [32 * e + 2]
            def gwait(j, e=e, last_w=last_w):
                tgt = 32 * e + 2 * (j + 1)
                if tgt > last_w[0]:
                    em.wait("t", evg, tgt)
                    last_w[0] = tgt
            for blk in range(NBLK):
                nuse = e * NBLK + blk
                b = nuse % 2
                if nuse + 1 < E * NBLK:
                    load_wo(nuse + 1)
                em.wait("t", wosem, 16 * (nuse + 1))
                for dc in range(2):
                    dii = blk * 2 + dc
                    for half in range(2):
                        hs = slice(half * 512, (half + 1) * 512)
                        s0 = n_o % 6
                        em.wait("t", f"evsl{s0}", slu[s0])
                        slu[s0] += 1
                        for j in range(NCH):
                            gwait(j)
                            em.emit("t", lambda t, b=b, dc=dc, j=j, hs=hs, s0=s0:
                                    t.matmul(SL(s0),
                                             wob[:, wo_cols(b)][:, j * MBLK + dc * 128:
                                                                j * MBLK + (dc + 1) * 128],
                                             g_[:, j * T:(j + 1) * T][:, hs],
                                             start=(j == 0), stop=(j == NCH - 1)),
                                    inc=(mmo if j == NCH - 1 else None))
                        n_o += 1
                        em.wait("v", mmo, n_o)
                        em.emit("v", lambda v, dii=dii, hs=hs, s0=s0:
                                v.scalar_tensor_tensor(
                                    x_res[:, dii * T:(dii + 1) * T][:, hs], SL(s0),
                                    0.5, x_res[:, dii * T:(dii + 1) * T][:, hs],
                                    ALU.mult, ALU.add), inc=f"evsl{s0}")
                        em.emit("v", lambda v: v.engine_nop(), inc=evo)
        em.wait("y", evo, n_o)
        em.wait("g", evo, n_o)
        em.wait("v", evo, n_o)
        if tag != "a":
            em.wait("t", evo, n_o)
            em.wait("s", evo, n_o)

    def attn_pass():
        xk_cols = lambda j: xkT[:, j * T:(j + 1) * T]
        xv_cols = lambda j: xvT[:, j * T:(j + 1) * T]
        xr_cols = lambda j: xrT[:, j * T:(j + 1) * T]
        y_cols = lambda j: y_[:, j * T:(j + 1) * T]
        projk = x_res.bitcast(BF16)
        SE = [scr[:, 0:T], scr[:, 2 * T:3 * T]]
        SB = [scr[:, T:2 * T], scr[:, 3 * T:4 * T]]
        SC4 = scr[:, 4 * T:5 * T]

        em.dma("y", spill.rearrange("(j p) t -> p j t", p=128),
               x_res.rearrange("p (j t) -> p j t", j=NCH), "spl")
        lnw = lambda j: 224 + (j // 2) * 4 + (j % 2) * 2 + 2
        ln_stats(x_res, x_cols, lnw=lnw)
        x3 = x_res.rearrange("p (j t) -> p j t", j=NCH)
        ivl = invb[:, T - 1:T].broadcast_to([128, NCH])
        nml = nm2b[:, T - 1:T].broadcast_to([128, NCH])
        iv0 = invb[:, 0:1].broadcast_to([128, NCH])
        nm0 = nm2b[:, 0:1].broadcast_to([128, NCH])
        em.gemit("v", lambda v: v.tensor_tensor(
            halo_o[:, :], x3[:, :, T - 1], ivl, ALU.mult))
        em.emit("v", lambda v: v.tensor_tensor(
            halo_o[:, :], halo_o[:, :], nml, ALU.add))
        em.gemit("v", lambda v: v.tensor_tensor(
            z0[:, :], x3[:, :, 0], iv0, ALU.mult))
        em.emit("v", lambda v: v.tensor_tensor(
            z0[:, :], z0[:, :], nm0, ALU.add), inc="mixz")
        em.wait("y", "mixz")
        em.dma("y", cc_h_in[:, :], halo_o[:, :], "chi")
        em.wait("g", "chi")
        em.emit_cc(lambda g: g.collective_compute(
            "AllGather", ALU.bypass, replica_groups=RG,
            ins=[cc_h_in.opt()], outs=[cc_h_out.opt()]), "agh")
        em.wait("y", "agh", 1)
        em.dma("y", scr[:, 2 * T:2 * T + 8 * NCH].rearrange(
                   "p (r c) -> p r c", r=8),
               cc_h_out.rearrange("(r p) c -> p r c", p=128), "cho")
        em.wait("v", "spl")
        em.wait("g", "spl")
        tb = tmp.bitcast(BF16)
        srb = scr.bitcast(BF16)
        zbp = [tb[:, 0:T], srb[:, 6 * T:7 * T]]
        dbp = [tb[:, T:2 * T], srb[:, 7 * T:8 * T]]
        mix3 = (("tmk", xk_cols), ("tmv", xv_cols), ("tmr", xr_cols))

        def mix_adds(j):
            zb = zbp[j % 2]
            for tm, dst in mix3:
                em.emit("v", lambda v, j=j, dst=dst, zb=zb: v.tensor_tensor(
                    dst(j)[:, 1:T], dst(j)[:, 1:T], zb[0:128, 0:T - 1],
                    ALU.add), inc=("mixv" if tm == "tmr" else None))
        for j in range(NCH):
            zb, db = zbp[j % 2], dbp[j % 2]
            em.emit("v", lambda v, j=j: v.tensor_tensor(
                SC4[:, :], x_res[:, x_cols(j)], invb[:, :], ALU.mult))
            em.emit("v", lambda v, zb=zb: v.tensor_tensor(
                zb, SC4[:, :], nm2b[:, :], ALU.add))
            em.emit("v", lambda v, zb=zb, db=db: v.tensor_tensor(
                db[0:128, 0:T - 1], zb[0:128, 1:T], zb[0:128, 0:T - 1],
                ALU.subtract), inc="mixd")
            em.wait("s", "mixd", j + 1)
            for tm, dst in mix3:
                em.emit("s", lambda s, j=j, tm=tm, dst=dst, db=db: s.activation(
                    dst(j)[:, 1:T], db[0:128, 0:T - 1], AF.Copy,
                    scale=vecs[tm][:, j:j + 1]),
                    inc=("sxm" if tm == "tmr" else None))
            if j >= 1:
                em.wait("v", "sxm", j)
                mix_adds(j - 1)
        em.wait("v", "sxm", NCH)
        mix_adds(NCH - 1)
        em.wait("v", "cho")
        em.gemit("v", lambda v: v.memset(halo_i[:, :], 0.0))
        for r in range(8):
            em.gemit("v", lambda v, r=r: v.scalar_tensor_tensor(
                halo_i[:, :], scr[:, 2 * T + r * NCH:2 * T + (r + 1) * NCH],
                selt[:, r:r + 1], halo_i[:, :], ALU.mult, ALU.add))
        for tm, om, dst in (("tmk", omk, xk_cols), ("tmv", omv, xv_cols),
                            ("tmr", omr, xr_cols)):
            em.emit("v", lambda v, om=om: v.tensor_tensor(
                tmp[:, 128:128 + NCH], halo_i[:, :], om[:, :], ALU.mult))
            em.gemit("v", lambda v, tm=tm: v.tensor_tensor(
                tmp[:, 160:160 + NCH], z0[:, :], vecs[tm][:, :], ALU.mult))
            em.gemit("v", lambda v: v.tensor_tensor(
                tmp[:, 192:192 + NCH], tmp[:, 128:128 + NCH],
                tmp[:, 160:160 + NCH], ALU.add))
            for j in range(NCH):
                em.emit("v", lambda v, j=j, dst=dst: v.tensor_copy(
                    dst(j)[:, 0:1], tmp[:, 192 + j:193 + j]),
                    inc=("mixf" if j == NCH - 1 else None))
        em.wait("v", "mmpm", nst["pm_w"])
        em.wait("v", "evpm", nst["pm_r"])

        pk_cols = lambda b: slice(16384 + b * (NCH * MBLK),
                                  16384 + (b + 1) * (NCH * MBLK))
        pv_cols = lambda b: slice(16384 + (2 + b) * (NCH * MBLK),
                                  16384 + (3 + b) * (NCH * MBLK))
        pr_cols = lambda b: slice(b * (NCH * MBLK), (b + 1) * (NCH * MBLK))

        def load_proj(blk):
            b = blk % 2
            if blk >= 2:
                em.wait("g", "mmr", 2 * (blk - 1))
            if blk == 0:
                em.wait("g", "spl")
            for w, cols, buf, mw in ((di["wkT"], pk_cols, projk, 10 + 2 * b),
                                     (di["wvT"], pv_cols, projk, 14 + 2 * b),
                                     (di["wrT"], pr_cols, wob, None)):
                if blk < 2 and mw is not None:
                    em.wait("g", "mixv", mw)
                src_ = w[:, blk * MBLK:(blk + 1) * MBLK].rearrange(
                    "(j p) m -> p j m", p=128)
                em.dma("g", buf[:, cols(b)].rearrange("p (j m) -> p j m", j=NCH),
                       src_, "wpj")
        load_proj(0)
        load_proj(1)

        def emit_tensor(dj):
            blk, dc = divmod(dj, 2)
            b = blk % 2
            if dj % 2 == 0 and 2 <= dj and blk + 1 < 8:
                load_proj(blk + 1)
            em.wait("t", "wpj", 16 * 3 * (blk + 1))
            for s in range(6):
                em.wait("t", f"evsl{s}", slu[s])
                slu[s] += 1
            for buf, cols_, xc, mmn, P_ in ((projk, pk_cols, xk_cols, "mmk", PA),
                                            (projk, pv_cols, xv_cols, "mmv", PB),
                                            (wob, pr_cols, xr_cols, "mmr", PC)):
                for half in range(2):
                    hs = slice(half * 512, (half + 1) * 512)
                    for j in range(NCH):
                        em.emit("t", lambda t, b=b, dc=dc, j=j, hs=hs, buf=buf,
                                cols_=cols_, xc=xc, P_=P_: t.matmul(
                            P_[:, hs],
                            buf[:, cols_(b)][:, j * MBLK + dc * 128:
                                             j * MBLK + (dc + 1) * 128],
                            xc(j)[:, hs],
                            start=(j == 0), stop=(j == NCH - 1)),
                            inc=(mmn if (j == NCH - 1 and half == 1) else None))

        def emit_pre(dj):
            p = dj % 2
            e_t, b_t = SE[p], SB[p]
            so = 4 * p
            dbc = dec[:, dj:dj + 1].broadcast_to([128, T])
            em.wait("s", "mmk", dj + 1)
            if dj >= 2:
                em.wait("s", "sfree", dj - 1)
            em.emit("s", lambda s, e_t=e_t: s.activation(
                e_t[:, 0:512], PA[:, 0:512], AF.Exp), inc="evsl0")
            em.emit("s", lambda s, e_t=e_t: s.activation(
                e_t[:, 512:1024], PA[:, 512:1024], AF.Exp), inc="evsl1")
            em.wait("s", "mmr", dj + 1)
            em.emit("s", lambda s: s.activation(
                PC[:, 0:512], PC[:, 0:512], AF.Exp, scale=-1.0))
            em.emit("s", lambda s: s.activation(
                PC[:, 512:1024], PC[:, 512:1024], AF.Exp, scale=-1.0))
            em.emit("s", lambda s: s.activation(
                PC[:, 0:512], PC[:, 0:512], AF.Ln, bias=1.0))
            em.emit("s", lambda s: s.activation(
                PC[:, 512:1024], PC[:, 512:1024], AF.Ln, bias=1.0))
            em.emit("s", lambda s, dj=dj: s.activation(
                y_cols(dj)[:, 0:512], PC[:, 0:512], AF.Exp, scale=-1.0),
                inc="evsl4")
            em.emit("s", lambda s, dj=dj: s.activation(
                y_cols(dj)[:, 512:1024], PC[:, 512:1024], AF.Exp, scale=-1.0),
                inc="evsl5")
            em.wait("v", "evsl1", slu[1])
            em.wait("v", "mmv", dj + 1)
            em.emit("v", lambda v, e_t=e_t, b_t=b_t: v.tensor_tensor(
                b_t[:, 0:512], e_t[:, 0:512], PB[:, 0:512], ALU.mult),
                inc="evsl2")
            em.emit("v", lambda v, e_t=e_t, b_t=b_t: v.tensor_tensor(
                b_t[:, 512:1024], e_t[:, 512:1024], PB[:, 512:1024], ALU.mult),
                inc="evsl3")
            if dj >= 2:
                em.wait("v", "cin", 16 * (dj - 1))
            em.gemit("v", lambda v, dbc=dbc, b_t=b_t: v.tensor_tensor_scan(
                PM[:, :], dbc, b_t[:, :], 0.0, ALU.mult, ALU.add))
            em.gemit("v", lambda v, so=so: v.tensor_copy(
                sumt[:, so:so + 1], PM[:, T - 1:T]))
            em.gemit("v", lambda v, dbc=dbc, e_t=e_t: v.tensor_tensor_scan(
                PM[:, :], dbc, e_t[:, :], 0.0, ALU.mult, ALU.add))
            em.emit("v", lambda v, so=so: v.tensor_copy(
                sumt[:, so + 1:so + 2], PM[:, T - 1:T]), inc="sumv")
            em.wait("y", "sumv", dj + 1)
            em.dma("y", cc_s_in[dj][:, :], sumt[:, so:so + 2], "cin")
            em.wait("g", "cin", 16 * (dj + 1))
            em.emit_cc(lambda g, dj=dj: g.collective_compute(
                "AllGather", ALU.bypass, replica_groups=RG,
                ins=[cc_s_in[dj].opt()], outs=[cc_s_out[dj].opt()]), f"ag{dj}")
            em.wait("y", f"ag{dj}", 1)
            em.wait("y", "rsdone", max(0, dj - 1))
            dbase = 48 * (dj % 2)
            em.dma("y", dsc[:, dbase:dbase + 16].rearrange("p (r c) -> p r c", c=2),
                   cc_s_out[dj].rearrange("(r p) c -> p r c", p=128), "bdma")

        def emit_post(dj):
            p = dj % 2
            e_t, b_t = SE[p], SB[p]
            so = 4 * p
            dbc = dec[:, dj:dj + 1].broadcast_to([128, T])
            d1b = d1024[:, dj:dj + 1].broadcast_to([128, 8])
            dbase = 48 * (dj % 2)
            bnum_r = dsc[:, dbase:dbase + 16].rearrange(
                "p (r c) -> p c r", c=2)[:, 0, :]
            bden_r = dsc[:, dbase:dbase + 16].rearrange(
                "p (r c) -> p c r", c=2)[:, 1, :]
            em.wait("v", "bdma", 16 * (dj + 1))
            em.gemit("v", lambda v, d1b=d1b, bnum_r=bnum_r, dbase=dbase:
                     v.tensor_tensor_scan(
                dsc[:, dbase + 24:dbase + 32], d1b, bnum_r, 0.0,
                ALU.mult, ALU.add))
            em.gemit("v", lambda v, dbase=dbase: v.tensor_tensor(
                dsc[:, dbase + 32:dbase + 40], dsc[:, dbase + 24:dbase + 32],
                selt[:, :], ALU.mult))
            em.gemit("v", lambda v, so=so, dbase=dbase: v.reduce_sum(
                sumt[:, so + 2:so + 3], dsc[:, dbase + 32:dbase + 40], axis=AXX))
            em.gemit("v", lambda v, d1b=d1b, bden_r=bden_r, dbase=dbase:
                     v.tensor_tensor_scan(
                dsc[:, dbase + 24:dbase + 32], d1b, bden_r, 0.0,
                ALU.mult, ALU.add))
            em.gemit("v", lambda v, dbase=dbase: v.tensor_tensor(
                dsc[:, dbase + 32:dbase + 40], dsc[:, dbase + 24:dbase + 32],
                selt[:, :], ALU.mult))
            em.gemit("v", lambda v, so=so, dbase=dbase: v.reduce_sum(
                sumt[:, so + 3:so + 4], dsc[:, dbase + 32:dbase + 40], axis=AXX))
            em.emit("v", lambda v: v.engine_nop(), inc="sden")
            em.emit("v", lambda v: v.engine_nop(), inc="rsdone")
            eub = eu[:, dj:dj + 1].broadcast_to([128, T])
            em.emit("v", lambda v, b_t=b_t, eub=eub: v.tensor_tensor(
                tmp[:, :], b_t[:, :], eub, ALU.mult))
            em.emit("v", lambda v, e_t=e_t, eub=eub: v.tensor_tensor(
                SC4[:, :], e_t[:, :], eub, ALU.mult))
            em.gemit("v", lambda v, dbc=dbc, b_t=b_t, so=so: v.tensor_tensor_scan(
                b_t[:, :], dbc, b_t[:, :], sumt[:, so + 2:so + 3],
                ALU.mult, ALU.add))
            em.gemit("v", lambda v, dbc=dbc, e_t=e_t, so=so: v.tensor_tensor_scan(
                e_t[:, :], dbc, e_t[:, :], sumt[:, so + 3:so + 4],
                ALU.mult, ALU.add))
            em.emit("v", lambda v, b_t=b_t: v.tensor_tensor(
                tmp[:, 1:T], tmp[:, 1:T], b_t[:, 0:T - 1], ALU.add))
            em.emit("v", lambda v, e_t=e_t: v.tensor_tensor(
                SC4[:, 1:T], SC4[:, 1:T], e_t[:, 0:T - 1], ALU.add))
            em.gemit("v", lambda v, so=so: v.tensor_tensor(
                tmp[:, 0:1], tmp[:, 0:1], sumt[:, so + 2:so + 3], ALU.add))
            em.gemit("v", lambda v, so=so: v.tensor_tensor(
                SC4[:, 0:1], SC4[:, 0:1], sumt[:, so + 3:so + 4], ALU.add))
            em.emit("v", lambda v: v.engine_nop(), inc="dden")
            em.wait("s", "dden", dj + 1)
            em.emit("s", lambda s: s.activation(SC4[:, :], SC4[:, :], AF.Ln))
            em.emit("s", lambda s, e_t=e_t: s.activation(
                e_t[:, :], SC4[:, :], AF.Exp, scale=-1.0), inc="srec")
            em.wait("v", "srec", dj + 1)
            em.emit("v", lambda v, e_t=e_t: v.tensor_tensor(
                tmp[:, :], tmp[:, :], e_t[:, :], ALU.mult))
            em.wait("v", "evsl5", slu[5])
            em.emit("v", lambda v, dj=dj: v.tensor_tensor(
                y_cols(dj), tmp[:, :], y_cols(dj), ALU.mult), inc="sfree")

        em.wait("t", "mixf")
        for dj in range(NCH):
            emit_tensor(dj)
            emit_pre(dj)
            if dj >= 1:
                emit_post(dj - 1)
        emit_post(NCH - 1)
        nst["pm_r"] += 1
        em.emit("v", lambda v: v.engine_nop(), inc="evpm")

        em.wait("y", "mmr", 16)
        em.dma("y", x_res.rearrange("p (j t) -> p j t", j=NCH),
               spill.rearrange("(j p) t -> p j t", p=128), "rld")

        def load_wo2(blk):
            b = blk % 2
            if blk >= 2:
                em.wait("g", "mmwo", 2 * (blk - 1))
            else:
                em.wait("g", "mmr", 16)
            src_ = di["woT"][:, blk * MBLK:(blk + 1) * MBLK].rearrange(
                "(j p) m -> p j m", p=128)
            em.dma("g", wob[:, pr_cols(b)].rearrange("p (j m) -> p j m", j=NCH),
                   src_, "wwo")
        load_wo2(0)
        n_wo = 0
        for blk in range(8):
            b = blk % 2
            if blk + 1 < 8:
                load_wo2(blk + 1)
            em.wait("t", "wwo", 16 * (blk + 1))
            for cc_ in range(2):
                ci = blk * 2 + cc_
                em.wait("t", "evsl0", slu[0])
                em.wait("t", "evsl1", slu[1])
                slu[0] += 1; slu[1] += 1
                for half in range(2):
                    hs = slice(half * 512, (half + 1) * 512)
                    for j in range(NCH):
                        if n_wo == 0 and half == 0:
                            em.wait("t", "sfree", j + 1)
                        em.emit("t", lambda t, b=b, cc_=cc_, j=j, hs=hs: t.matmul(
                            PA[:, hs],
                            wob[:, pr_cols(b)][:, j * MBLK + cc_ * 128:
                                               j * MBLK + (cc_ + 1) * 128],
                            y_cols(j)[:, hs],
                            start=(j == 0), stop=(j == NCH - 1)),
                            inc=("mmwo" if (j == NCH - 1 and half == 1) else None))
                n_wo += 1
                em.wait("v", "mmwo", n_wo)
                if ci == 0:
                    em.wait("v", "rld")
                em.emit("v", lambda v, ci=ci: v.tensor_tensor(
                    x_res[:, ci * T:(ci + 1) * T][:, 0:512], PA[:, 0:512],
                    x_res[:, ci * T:(ci + 1) * T][:, 0:512], ALU.add),
                    inc="evsl0")
                em.emit("v", lambda v, ci=ci: v.tensor_tensor(
                    x_res[:, ci * T:(ci + 1) * T][:, 512:1024], PA[:, 512:1024],
                    x_res[:, ci * T:(ci + 1) * T][:, 512:1024], ALU.add),
                    inc="evsl1")
                em.emit("v", lambda v: v.engine_nop(), inc="evwo")
        em.wait("v", "evwo", n_wo)
        em.wait("y", "evwo", n_wo)
        em.wait("v", "mmwo", n_wo)

    xn_cols = lambda j: xn[:, j * T:(j + 1) * T]
    x_cols = lambda j: slice(j * T, (j + 1) * T)

    ln_stats(x_res, x_cols)
    if upto in ("ln1",):
        ln_apply(xn_cols, x_res, x_cols)
    if dbg == "xn1":
        em.wait("g", "lnap")
        for j in range(NCH):
            em.dma("g", ddbg[j * 128:(j + 1) * 128, :], xn_cols(j), "dbo")
        em.wait("y", "dbo")
    if upto not in ("ln1",):
        gating(x_res, x_cols,
               mid_emit=lambda: ln_apply(xn_cols, x_res, x_cols))
        if dbg == "cw":
            em.wait("g", "evpm", nst["pm_r"])
            em.dma("g", ddbg[0:8, :], cwT[:, :], "dbo")
            em.wait("y", "dbo")
        ffn_pass(xn_cols, "a")
        if dbg == "ffn1":
            em.dma("y", ddbg.rearrange("(j p) t -> p j t", p=128),
                   x_res.rearrange("p (j t) -> p j t", j=NCH), "dbo")
            em.wait("y", "dbo")
    if upto in ("attn", "full"):
        attn_pass()
        if dbg == "attn":
            em.dma("y", ddbg.rearrange("(j p) t -> p j t", p=128),
                   x_res.rearrange("p (j t) -> p j t", j=NCH), "dbo")
            em.wait("y", "dbo")
    if upto == "full":
        em.wait("v", "mmr", 16)
        ln_stats(x_res, x_cols, lnw=lambda j: j + 1, lnsem="evwo")
        gating(x_res, x_cols,
               mid_emit=lambda: ln_apply(xn_cols, x_res, x_cols))
        ffn_pass(xn_cols, "b")

    if upto == "full":
        for j in range(NCH):
            em.wait("y", "evob", 224 + (j // 2) * 4 + (j % 2) * 2 + 2)
            em.dma("y", dout[j * 128:(j + 1) * 128, :], x_res[:, j * T:(j + 1) * T],
                   "so")
    else:
        em.dma("y", dout.rearrange("(j p) t -> p j t", p=128),
               x_res.rearrange("p (j t) -> p j t", j=NCH), "so")
    em.wait("y", "so")

    with nc.Block() as block:
        block.tensor(lambda t: [f(t) for f in em.q["t"]])
        block.vector(lambda v: [f(v) for f in em.q["v"]])
        block.scalar(lambda s: [f(s) for f in em.q["s"]])
        block.gpsimd(lambda gq: [f(gq) for f in em.q["g"]])
        block.sync(lambda sy: [f(sy) for f in em.q["y"]])
    stack.close()
    return nc


def make_inputs(ins, core):
    x = np.asarray(ins["L0_attn_x"])[0]
    xT = np.ascontiguousarray(x.T)
    cvec = lambda v: np.ascontiguousarray(
        np.asarray(v, np.float32).reshape(NCH, 128).T)
    e8 = np.zeros((E, E * 128), np.float32)
    for e in range(E):
        e8[e, e * 128:(e + 1) * 128] = 1.0
    sel = np.zeros((128, 8), np.float32)
    if core > 0:
        sel[:, core - 1] = 1.0
    hmask = np.full((128, 1), 0.0 if core == 0 else 1.0, np.float32)
    return {
        "x": np.ascontiguousarray(xT[:, core * T:(core + 1) * T]),
        "wkT": np.ascontiguousarray(np.asarray(ins["L0_attn_Wk"]).T),
        "wvT": np.ascontiguousarray(np.asarray(ins["L0_attn_Wv"]).T),
        "wrT": np.ascontiguousarray(np.asarray(ins["L0_attn_Wr"]).T),
        "woT": np.ascontiguousarray(np.asarray(ins["L0_attn_Wo"]).T),
        "wi0": np.ascontiguousarray(np.asarray(ins["L0_ffn_wi_0"])),
        "wi1": np.ascontiguousarray(np.asarray(ins["L0_ffn_wi_1"])),
        "wo2": np.ascontiguousarray(np.asarray(ins["L0_ffn_wo"])),
        "gk": np.ascontiguousarray(np.asarray(ins["L0_ffn_gate_kernel"])),
        "td": cvec(ins["L0_attn_time_decay"]),
        "tf": cvec(ins["L0_attn_time_first"]),
        "tmk": cvec(ins["L0_attn_time_mix_k"]),
        "tmv": cvec(ins["L0_attn_time_mix_v"]),
        "tmr": cvec(ins["L0_attn_time_mix_r"]),
        "e8": e8, "ident": np.eye(128, dtype=np.float32),
        "sel": sel, "hmask": hmask,
    }


from concourse.bass_utils import run_bass_kernel_spmd

_NC_CACHE = {}

def kernel(**inputs):
    if "nc" not in _NC_CACHE:
        _NC_CACHE["nc"] = build(upto="full", dbg=None)
    nc = _NC_CACHE["nc"]
    in_maps = [make_inputs(inputs, c) for c in range(8)]
    res = run_bass_kernel_spmd(nc, in_maps, core_ids=list(range(8)))
    out = np.concatenate([res.results[c]["out"].T for c in range(8)], axis=0)
    return np.ascontiguousarray(out, dtype=np.float32).reshape(1, 8192, 2048)
